# version 05
# baseline (speedup 1.0000x reference)
import numpy as np
import ml_dtypes
from contextlib import ExitStack
import concourse.bass as bass
import concourse.mybir as mybir
from concourse.bass_utils import run_bass_kernel_spmd

F32 = mybir.dt.float32
BF16 = mybir.dt.bfloat16
U8 = mybir.dt.uint8
I32 = mybir.dt.int32
FP8 = mybir.dt.float8e4
AF = mybir.ActivationFunctionType
ALU = mybir.AluOpType

NSEQ = 8192
DM = 2048
CH = 128
NCH = NSEQ // CH
TT = 512
NTT = NSEQ // TT
MLP = 16384
KCH = DM // 128
TQ = 2048
RSQ = float(128) ** -0.5
C1 = 0.7978845608028654
C2 = 0.7978845608028654 * 0.044715
DQW = 256
N_DQ = (2 * MLP // DQW) + (8 * DM // DQW)

TWO_LAUNCH = False


class Ctx:
    def __init__(self, nc):
        self.nc = nc
        self.ops = {k: [] for k in ("sync", "gpsimd", "vector", "scalar", "tensor")}
        self.C = {k: 0 for k in ('dli', 'dgp', 'sv', 'ss', 'st', 'sp', 'cc', 'so', 'dwq', 'dsp', 'dqi', 'dsc', 'dqo', 'dqg', 'dsw', 'dqr', 'dql0', 'dql1', 'dqc', 'drc', 'dxa0', 'dxa1', 'dcs', 'dnt', 'dnl', 'dwd0', 'dwd1', 'dwd2', 'dwd3', 'dh1', 'dpc', 'dhl', 'dwo')}
        self.SEM = {}

    def add(self, eng, fn):
        self.ops[eng].append(fn)

    def _mk(self, eng, build, waits, sem, inc):
        waits = [(s, v) for (s, v) in waits if v > 0]
        self.C[sem] += inc
        SEM = self.SEM

        def f(e, _b=build, _w=tuple(waits), _s=sem, _i=inc):
            for s, v in _w:
                e.wait_ge(SEM[s], v)
            ins = _b(e)
            if _i:
                ins.then_inc(SEM[_s], _i)
        self.add(eng, f)
        return self.C[sem]

    def V(self, build, waits=(), serial=True):
        w = list(waits)
        if serial:
            w.append(("sv", self.C["sv"]))
        return self._mk("vector", build, w, "sv", 1)

    def S(self, build, waits=(), serial=True):
        w = list(waits)
        if serial:
            w.append(("ss", self.C["ss"]))
        return self._mk("scalar", build, w, "ss", 1)

    def G(self, build, waits=(), serial=True):
        w = list(waits)
        if serial:
            w.append(("sp", self.C["sp"]))
        return self._mk("gpsimd", build, w, "sp", 1)

    def T(self, build, waits=(), inc=0):
        return self._mk("tensor", build, waits, "st", inc)

    def Dg(self, build, waits=(), sem="dgp"):
        return self._mk("gpsimd", build, waits, sem, 16)

    def Ds(self, build, waits=(), sem="dli"):
        return self._mk("sync", build, waits, sem, 16)

    def Do(self, build, waits=()):
        return self._mk("gpsimd", build, waits, "so", 16)

    def CC(self, build, waits=()):
        return self._mk("gpsimd", build, waits, "cc", 1)

    def W(self, eng, waits):
        waits = [(s, v) for (s, v) in waits if v > 0]
        SEM = self.SEM

        def f(e, _w=tuple(waits)):
            for s, v in _w:
                e.wait_ge(SEM[s], v)
        self.add(eng, f)


def build_program(part="all", dbg=False):
    nc = bass.Bass("TRN2", target_bir_lowering=False)
    ct = Ctx(nc)
    need_attn = part in ("all", "attn")
    need_ffn = part in ("all", "ffn")

    din = lambda n, s, d: nc.dram_tensor(n, s, d, kind="ExternalInput").ap()
    dint = lambda n, s, d, sp="Local": nc.dram_tensor(n, s, d, addr_space=sp).ap()

    if need_attn:
        xT = din("xT", [DM, NSEQ], F32)
        wq = din("wq", [DM, 512], F32)
        wk = din("wk", [DM, 128], F32)
        wv = din("wv", [DM, 128], F32)
        wo = din("wo", [512, DM], F32)
        cosT = din("cosT", [128, NSEQ], F32)
        sinT = din("sinT", [128, NSEQ], F32)
    if need_ffn:
        x0q = din("x0q", [TQ, DM], F32)
        gbytes = din("gbytes", [128, MLP], U8)
        ubytes = din("ubytes", [128, MLP], U8)
        dbytes = din("dbytes", [1024, DM], U8)
        gscale = din("gscale", [128, MLP], F32)
        uscale = din("uscale", [128, MLP], F32)
        dscale = din("dscale", [1024, DM], F32)
    idm = din("idm", [128, 128], F32)
    masku = din("masku", [128, 128], F32)
    negu16 = din("negu16", [128, 128], F32)

    if need_attn:
        qraw_d = dint("qraw_d", [128, 4 * NSEQ], BF16)
        qrope_d = dint("qrope_d", [128, 4 * NSEQ], BF16)
        rstd_d = dint("rstd_d", [1, TT], F32)
        rsin = [dint(f"rsin{i}", [TQ, DM], F32) for i in range(4)]
        rsout = [dint(f"rsout{i}", [512, DM], F32) for i in range(4)]
    if part == "attn":
        h1src = [nc.dram_tensor(f"ho{i}", [512, DM], F32, kind="ExternalOutput").ap()
                 for i in range(4)]
        if dbg:
            dbg_t = {}
            for nm, shp, dt_ in (("d_qraw", [128, 4 * NSEQ], BF16), ("d_qrope", [128, 4 * NSEQ], BF16),
                                 ("d_kraw", [128, NSEQ], BF16), ("d_krope", [128, NSEQ], BF16),
                                 ("d_ktl", [128, NSEQ], BF16), ("d_k2", [128, NSEQ], BF16),
                                 ("d_eA", [128, NSEQ], BF16), ("d_vtok", [128, NCH * 129], BF16),
                                 ("d_rkc", [128, NCH], F32), ("d_eacs", [128, NCH], F32),
                                 ("d_rsin", [TQ, DM], F32), ("d_ofeat", [128, 4 * TQ], BF16),
                                 ("d_rcol", [128, 4], F32)):
                dbg_t[nm] = nc.dram_tensor(nm, shp, dt_, kind="ExternalOutput").ap()
    if need_ffn:
        out_d = nc.dram_tensor("out", [TQ, DM], F32, kind="ExternalOutput").ap()
        agg_in = dint("agg_in", [256, MLP], BF16)
        agu_in = dint("agu_in", [256, MLP], BF16)
        agd_in = dint("agd_in", [2048, DM], BF16)
        gate_wt = dint("gate_wt", [DM, MLP], BF16, "Shared")
        up_wt = dint("up_wt", [DM, MLP], BF16, "Shared")
        down_wt = dint("down_wt", [MLP, DM], BF16, "Shared")
        inter_d = dint("inter_d", [MLP, TQ], BF16)
        h1_d = dint("h1_d", [TQ, DM], F32)
    if part == "ffn":
        rsout = [din(f"rsi{i}", [512, DM], F32) for i in range(4)]
        if dbg:
            dbg_f = {}
            for nm, shp, dt_ in (("f_h1", [TQ, DM], F32), ("f_inter", [MLP, TQ], BF16),
                                 ("f_gw", [DM, MLP], BF16), ("f_dw", [MLP, DM], BF16),
                                 ("f_down", [TQ, DM], F32)):
                dbg_f[nm] = nc.dram_tensor(nm, shp, dt_, kind="ExternalOutput").ap()

    es = ExitStack()
    with es:
        sb = lambda n, s, d: es.enter_context(nc.sbuf_tensor(n, s, d))
        for s in ('dli', 'dgp', 'sv', 'ss', 'st', 'sp', 'cc', 'so', 'dwq', 'dsp', 'dqi', 'dsc', 'dqo', 'dqg', 'dsw', 'dqr', 'dql0', 'dql1', 'dqc', 'drc', 'dxa0', 'dxa1', 'dcs', 'dnt', 'dnl', 'dwd0', 'dwd1', 'dwd2', 'dwd3', 'dh1', 'dpc', 'dhl', 'dwo'):
            ct.SEM[s] = es.enter_context(nc.semaphore(s))

        idf_sb = sb("idf", [128, 128], F32)
        idb_sb = sb("idb", [128, 128], BF16)
        mask_sb = sb("mask", [128, 128], F32)
        nu16_sb = sb("nu16", [128, 128], F32)
        ones_bf = sb("ones_bf", [128, 1], BF16)
        ones_f = sb("ones_f", [128, 1], F32)
        onesk1 = sb("onesk1", [1, 128], F32)
        epsb = sb("epsb", [128, 1], F32)
        oneb = sb("oneb", [128, 1], F32)

        ct.Ds(lambda e: e.dma_start(out=idf_sb[:], in_=idm))
        d_mk = ct.Ds(lambda e: e.dma_start(out=mask_sb[:], in_=masku))
        d_nu = ct.Ds(lambda e: e.dma_start(out=nu16_sb[:], in_=negu16))
        d_idb = ct.Dg(lambda e: e.dma_start(out=idb_sb[:], in_=idm))
        ct.V(lambda v: v.memset(ones_bf[:], 1.0))
        ct.V(lambda v: v.memset(ones_f[:], 1.0))
        ct.V(lambda v: v.memset(onesk1[:], 1.0))
        ct.V(lambda v: v.memset(epsb[:], 1e-5))
        m_const = ct.V(lambda v: v.memset(oneb[:], 1.0))

        if need_ffn:
            dqv = sb("dqv", [128, DQW], I32)
            dqs = sb("dqs", [128, DQW], F32)
            dqt1 = sb("dqt1", [128, DQW], I32)
            dqt2 = sb("dqt2", [128, DQW], I32)
            dq8 = sb("dq8", [128, DQW], U8)
            dqbf = sb("dqbf", [128, DQW], BF16)
            dqlo = sb("dqlo", [128, DQW], BF16)
            dqhi = sb("dqhi", [128, DQW], BF16)
            dq_jobs = []
            nw = MLP // DQW
            for w in range(nw):
                dq_jobs.append((gbytes[:, DQW * w:DQW * (w + 1)], gscale[:, DQW * w:DQW * (w + 1)],
                                agg_in[0:128, DQW * w:DQW * (w + 1)], agg_in[128:256, DQW * w:DQW * (w + 1)]))
            for w in range(nw):
                dq_jobs.append((ubytes[:, DQW * w:DQW * (w + 1)], uscale[:, DQW * w:DQW * (w + 1)],
                                agu_in[0:128, DQW * w:DQW * (w + 1)], agu_in[128:256, DQW * w:DQW * (w + 1)]))
            nwd = DM // DQW
            for jc in range(8):
                for w in range(nwd):
                    dq_jobs.append((dbytes[128 * jc:128 * (jc + 1), DQW * w:DQW * (w + 1)],
                                    dscale[128 * jc:128 * (jc + 1), DQW * w:DQW * (w + 1)],
                                    agd_in[128 * jc:128 * (jc + 1), DQW * w:DQW * (w + 1)],
                                    agd_in[1024 + 128 * jc:1152 + 128 * jc, DQW * w:DQW * (w + 1)]))
            dqst = {"i": 0, "last_out": 0, "cc": 0}

            def emit_dq_tile():
                i = dqst["i"]
                if i >= len(dq_jobs):
                    return
                dqst["i"] += 1
                src, scl, dlo, dhi = dq_jobs[i]
                d_in = ct.Dg(lambda e, s=src: e.dma_start(out=dqv[:], in_=s),
                             waits=[("sv", dqst.get("mcons", 0)), ("dqo", dqst.get("mout", 0))],
                             sem="dqi")
                d_sc = ct.Ds(lambda e, s=scl: e.dma_start(out=dqs[:], in_=s),
                             waits=[("sv", dqst.get("mcons", 0))], sem="dsc")
                wd = [("dqi", d_in), ("dsc", d_sc)]
                ct.V(lambda v: v.tensor_scalar(out=dqt1[:], in0=dqv[:], scalar1=8, scalar2=4,
                                               op0=ALU.bitwise_and, op1=ALU.logical_shift_left), waits=wd)
                ct.V(lambda v: v.tensor_scalar(out=dqt2[:], in0=dqv[:], scalar1=7, scalar2=2,
                                               op0=ALU.bitwise_and, op1=ALU.logical_shift_left))
                ct.V(lambda v: v.tensor_tensor(out=dq8[:], in0=dqt1[:], in1=dqt2[:], op=ALU.add))
                ct.V(lambda v: v.tensor_copy(out=dqbf[:], in_=dq8[:].bitcast(FP8)))
                ct.V(lambda v: v.scalar_tensor_tensor(out=dqlo[:], in0=dqbf[:], scalar=64.0,
                                                      in1=dqs[:], op0=ALU.mult, op1=ALU.mult))
                ct.V(lambda v: v.tensor_scalar(out=dqt1[:], in0=dqv[:], scalar1=0x70, scalar2=2,
                                               op0=ALU.bitwise_and, op1=ALU.logical_shift_right))
                ct.V(lambda v: v.tensor_single_scalar(out=dqt2[:], in_=dqv[:], scalar=0x80, op=ALU.bitwise_and))
                ct.V(lambda v: v.tensor_tensor(out=dq8[:], in0=dqt1[:], in1=dqt2[:], op=ALU.add))
                ct.V(lambda v: v.tensor_copy(out=dqbf[:], in_=dq8[:].bitcast(FP8)))
                mcons = ct.V(lambda v: v.scalar_tensor_tensor(out=dqhi[:], in0=dqbf[:], scalar=64.0,
                                                              in1=dqs[:], op0=ALU.mult, op1=ALU.mult))
                dqst["mcons"] = mcons
                ct.Dg(lambda e, d=dlo: e.dma_start(out=d, in_=dqlo[:]), waits=[("sv", mcons)], sem="dqo")
                m2 = ct.Dg(lambda e, d=dhi: e.dma_start(out=d, in_=dqhi[:]), sem="dqo")
                dqst["mout"] = m2
                dqst["last_out"] = m2

            def dq_finish_and_ag():
                while dqst["i"] < len(dq_jobs):
                    emit_dq_tile()
                ct.CC(lambda e: e.collective_compute("AllGather", ALU.bypass,
                                                     replica_groups=[list(range(8))],
                                                     ins=[agg_in[:]], outs=[gate_wt[:]]),
                      waits=[("dqo", dqst["last_out"])])
                ct.CC(lambda e: e.collective_compute("AllGather", ALU.bypass,
                                                     replica_groups=[list(range(8))],
                                                     ins=[agu_in[:]], outs=[up_wt[:]]))
                dqst["cc"] = ct.CC(lambda e: e.collective_compute("AllGather", ALU.bypass,
                                                                  replica_groups=[list(range(8))],
                                                                  ins=[agd_in[:]], outs=[down_wt[:]]))
        else:
            def emit_dq_tile():
                pass

            def dq_finish_and_ag():
                pass

        cc_rs_done = 0
        if need_attn:
            esA = ExitStack()
            with esA:
                sbA = lambda n, s, d: esA.enter_context(nc.sbuf_tensor(n, s, d))
                v_tok = sbA("v_tok", [128, NCH * 129], BF16)
                k_rope = sbA("k_rope", [128, NSEQ], BF16)
                ktl = sbA("ktl", [128, NSEQ], BF16)
                k2 = sbA("k2", [128, NSEQ], BF16)
                eA = sbA("eA", [128, NSEQ], BF16)
                rkc = sbA("rkc", [128, NCH], F32)
                eacs = sbA("eacs", [128, NCH], F32)
                S_sb = sbA("S_sb", [128, 128], F32)
                Sbf = sbA("Sbf", [128, 128], BF16)
                wsum_sb = sbA("wsum_sb", [1, 640], F32)
                wsumv_sb = sbA("wsumv_sb", [1, 128], F32)
                k_raw = sbA("k_raw", [128, NSEQ], BF16)

                ct.V(lambda v: v.memset(S_sb[:], 0.0))
                ct.V(lambda v: v.memset(Sbf[:], 0.0))
                m_vones = ct.V(lambda v: v.memset(
                    v_tok[:].rearrange("p (c w) -> p c w", w=129)[:, :, 128:129], 1.0))

                esA1 = ExitStack()
                with esA1:
                    sb1 = lambda n, s, d: esA1.enter_context(nc.sbuf_tensor(n, s, d))
                    ps1 = lambda n, s, d: esA1.enter_context(nc.psum_tensor(n, s, d))
                    wqkv_sb = sb1("wqkv", [128, KCH * 640], BF16)
                    wv_sb = sb1("wv_sb", [128, KCH * 128], BF16)
                    xa = [sb1(f"xa{i}", [128, TT], BF16) for i in range(2)]
                    xsq = [sb1(f"xsq{i}", [128, TT], BF16) for i in range(2)]
                    qstage = sb1("qstage", [128, 4 * TT], BF16)
                    murow = sb1("murow", [1, TT], F32)
                    nmurow = sb1("nmurow", [1, TT], F32)
                    m2row = sb1("m2row", [1, TT], F32)
                    varrow = sb1("varrow", [1, TT], F32)
                    rstdrow = sb1("rstdrow", [1, TT], F32)
                    rbc_sb = sb1("rbc_sb", [128, TT], F32)
                    rcol_sb = sb1("rcol_sb", [128, 4], F32)
                    esPA = ExitStack()
                    psPA = lambda n, s, d: esPA.enter_context(nc.psum_tensor(n, s, d))
                    proj_ps = [psPA(f"proj{i}", [128, TT], F32) for i in range(5)]
                    v_ps = psPA("v_ps", [128, TT], F32)
                    misc_ps = psPA("misc_ps", [128, TT], F32)
                    stats2_ps = psPA("stats2_ps", [128, TT], F32)

                    for kc in range(KCH):
                        ct.Dg(lambda e, kc=kc: e.dma_start(
                            out=wqkv_sb[:, 640 * kc:640 * kc + 512], in_=wq[128 * kc:128 * (kc + 1), :]), sem="dwq")
                        ct.Dg(lambda e, kc=kc: e.dma_start(
                            out=wqkv_sb[:, 640 * kc + 512:640 * (kc + 1)], in_=wk[128 * kc:128 * (kc + 1), :]), sem="dwq")
                        ct.Dg(lambda e, kc=kc: e.dma_start(
                            out=wv_sb[:, 128 * kc:128 * (kc + 1)], in_=wv[128 * kc:128 * (kc + 1), :]), sem="dwq")
                    d_w = ct.C["dwq"]
                    for kc in range(KCH):
                        ct.T(lambda t, kc=kc: t.matmul(out=misc_ps[0:1, 0:512], lhsT=ones_bf[:],
                                                       rhs=wqkv_sb[:, 640 * kc:640 * kc + 512],
                                                       start=(kc == 0), stop=(kc == KCH - 1)),
                             waits=([("dwq", d_w), ("sv", m_const)] if kc == 0 else []))
                        ct.T(lambda t, kc=kc: t.matmul(out=stats2_ps[0:1, 0:128], lhsT=ones_bf[:],
                                                       rhs=wqkv_sb[:, 640 * kc + 512:640 * (kc + 1)],
                                                       start=(kc == 0), stop=(kc == KCH - 1)))
                        ct.T(lambda t, kc=kc: t.matmul(out=v_ps[0:1, 0:128], lhsT=ones_bf[:],
                                                       rhs=wv_sb[:, 128 * kc:128 * (kc + 1)],
                                                       start=(kc == 0), stop=(kc == KCH - 1)),
                             inc=(1 if kc == KCH - 1 else 0))
                    m_ws = ct.C["st"]
                    ct.S(lambda s: s.copy(out=wsum_sb[0:1, 0:512], in_=misc_ps[0:1, 0:512]), waits=[("st", m_ws)])
                    ct.S(lambda s: s.copy(out=wsum_sb[0:1, 512:640], in_=stats2_ps[0:1, 0:128]))
                    m_wsv = ct.S(lambda s: s.copy(out=wsumv_sb[:], in_=v_ps[0:1, 0:128]))

                    ch_end_st = {}
                    ch_end_ss = {}
                    prev_ev = {"sv": 0, "ss": 0}
                    prev_spill = 0
                    for i in range(NTT):
                        t0 = TT * i
                        for kc in range(KCH):
                            p = kc % 2
                            wxa = [("st", ch_end_st.get((i, kc - 2), ch_end_st.get((i - 1, KCH - 2 + kc), 0))),
                                   ("ss", ch_end_ss.get((i, kc - 2), ch_end_ss.get((i - 1, KCH - 2 + kc), 0)))]
                            d_xa = ct.Dg(lambda e, p=p, kc=kc, t0=t0: e.dma_start(
                                out=xa[p][:], in_=xT[128 * kc:128 * (kc + 1), t0:t0 + TT]), waits=wxa,
                                sem=f"dxa{p}")
                            wfirst = [(f"dxa{p}", d_xa)] + ([("ss", m_wsv)] if (i == 0 and kc == 0) else [])
                            if kc == 0:
                                wfirst += [("sv", prev_ev["sv"]), ("ss", prev_ev["ss"])]
                            ct.T(lambda t, p=p, kc=kc: t.matmul(
                                out=misc_ps[0:1, :], lhsT=ones_bf[:], rhs=xa[p][:],
                                start=(kc == 0), stop=(kc == KCH - 1)), waits=wfirst)
                            msq = ct.S(lambda s, p=p: s.activation(out=xsq[p][:], in_=xa[p][:], func=AF.Square),
                                       waits=[(f"dxa{p}", d_xa)])
                            ct.T(lambda t, p=p, kc=kc: t.matmul(
                                out=stats2_ps[0:1, :], lhsT=ones_bf[:], rhs=xsq[p][:],
                                start=(kc == 0), stop=(kc == KCH - 1)), waits=[("ss", msq)])
                            for mc in range(5):
                                ct.T(lambda t, mc=mc, kc=kc, p=p: t.matmul(
                                    out=proj_ps[mc][:],
                                    lhsT=wqkv_sb[:, 640 * kc + 128 * mc:640 * kc + 128 * (mc + 1)],
                                    rhs=xa[p][:], start=(kc == 0), stop=False),
                                    waits=([("sv", prev_ev["sv"]), ("dwq", d_w)] if kc == 0 else []))
                            for ts4 in range(4):
                                ct.T(lambda t, ts4=ts4, kc=kc, p=p: t.matmul(
                                    out=v_ps[:, 128 * ts4:128 * (ts4 + 1)],
                                    lhsT=xa[p][:, 128 * ts4:128 * (ts4 + 1)],
                                    rhs=wv_sb[:, 128 * kc:128 * (kc + 1)],
                                    start=(kc == 0 and ts4 == 0), stop=False),
                                    waits=([("sv", prev_ev["sv"])] if (kc == 0 and ts4 == 0) else []),
                                    inc=(1 if ts4 == 3 else 0))
                            ch_end_st[(i, kc)] = ct.C["st"]
                            ch_end_ss[(i, kc)] = ct.C["ss"]
                        m_stats = ch_end_st[(i, KCH - 1)]
                        ct.V(lambda v: v.tensor_scalar(out=murow[:], in0=misc_ps[0:1, :],
                                                       scalar1=1.0 / DM, scalar2=None, op0=ALU.mult),
                             waits=[("st", m_stats)])
                        m_negmu = ct.V(lambda v: v.tensor_scalar(out=nmurow[:], in0=murow[:],
                                                                 scalar1=-1.0, scalar2=None, op0=ALU.mult))
                        ct.V(lambda v: v.tensor_scalar(out=m2row[:], in0=stats2_ps[0:1, :],
                                                       scalar1=1.0 / DM, scalar2=None, op0=ALU.mult))
                        ct.V(lambda v: v.tensor_tensor(out=varrow[:], in0=murow[:], in1=murow[:], op=ALU.mult))
                        m_var = ct.V(lambda v: v.tensor_tensor(out=varrow[:], in0=m2row[:], in1=varrow[:],
                                                               op=ALU.subtract))
                        ct.S(lambda s: s.activation(out=varrow[:], in_=varrow[:], func=AF.Ln,
                                                    bias=epsb[0:1, 0:1]), waits=[("sv", m_var)])
                        m_rstd = ct.S(lambda s: s.activation(out=rstdrow[:], in_=varrow[:], func=AF.Exp,
                                                             scale=-0.5))
                        for mc in range(5):
                            ct.T(lambda t, mc=mc: t.matmul(
                                out=proj_ps[mc][:], lhsT=wsum_sb[0:1, 128 * mc:128 * (mc + 1)],
                                rhs=nmurow[:], start=False, stop=True),
                                waits=([("sv", m_negmu)] if mc == 0 else []), inc=(1 if mc == 4 else 0))
                        m_proj = ct.C["st"]
                        for ts4 in range(4):
                            ct.T(lambda t, ts4=ts4: t.matmul(
                                out=v_ps[:, 128 * ts4:128 * (ts4 + 1)],
                                lhsT=nmurow[0:1, 128 * ts4:128 * (ts4 + 1)], rhs=wsumv_sb[:],
                                start=False, stop=True), inc=(1 if ts4 == 3 else 0))
                        m_vproj = ct.C["st"]
                        m_rbc = ct.T(lambda t: t.matmul(out=misc_ps[:], lhsT=onesk1[:], rhs=rstdrow[:],
                                                        start=True, stop=True),
                                     waits=[("ss", m_rstd), ("sv", m_var)], inc=1)
                        m_rd1 = ct.Dg(lambda e: e.dma_start(out=rstd_d, in_=rstdrow[:]),
                              waits=[("ss", m_rstd), ("sv", prev_ev["sv"])], sem="drc")
                        m_rcold = ct.Dg(lambda e: e.dma_start(
                            out=rcol_sb[:], in_=rstd_d.rearrange("o (j p) -> p (o j)", p=128),
                            allow_slow_non_contiguous=True), waits=[("drc", m_rd1)], sem="drc")
                        m_rbcsb = ct.S(lambda s: s.copy(out=rbc_sb[:], in_=misc_ps[:]), waits=[("st", m_rbc)])
                        for mc in range(5):
                            dst = (qstage[:, TT * mc:TT * (mc + 1)] if mc < 4
                                   else k_raw[:, t0:t0 + TT])
                            ct.V(lambda v, dst=dst, mc=mc: v.tensor_tensor(
                                out=dst, in0=proj_ps[mc][:], in1=rbc_sb[:], op=ALU.mult),
                                waits=[("st", m_proj), ("ss", m_rbcsb)] +
                                      ([("dsp", prev_spill)] if mc == 0 else []))
                        m_qk = ct.C["sv"]
                        for ts4 in range(4):
                            c0 = 129 * (4 * i + ts4)
                            ct.V(lambda v, ts4=ts4, c0=c0: v.tensor_scalar(
                                out=v_tok[:, c0:c0 + 128], in0=v_ps[:, 128 * ts4:128 * (ts4 + 1)],
                                scalar1=rcol_sb[:, ts4:ts4 + 1], scalar2=None, op0=ALU.mult),
                                waits=[("st", m_vproj), ("drc", m_rcold), ("sv", m_vones)])
                        m_spill = ct.Dg(lambda e, t0=t0: e.dma_start(
                            out=qraw_d[:].rearrange("p (h n) -> p h n", h=4)[:, :, t0:t0 + TT],
                            in_=qstage[:].rearrange("p (h n) -> p h n", h=4)),
                            waits=[("sv", m_qk)], sem="dsp")
                        ch_end_st[(i, KCH - 1)] = ct.C["st"]
                        ch_end_ss[(i, KCH - 1)] = ct.C["ss"]
                        prev_ev = {"sv": ct.C["sv"], "ss": ct.C["ss"]}
                        prev_spill = m_spill
                        for _ in range(4):
                            emit_dq_tile()
                    m_pa_v = ct.C["sv"]
                    m_pa_dgp = ct.C["dsp"]

                    esPA.close()
                    cosg = [sb1(f"cosg{i}", [128, TT], F32) for i in range(2)]
                    sing = [sb1(f"sing{i}", [128, TT], F32) for i in range(2)]
                    qg = [sb1(f"qg{i}", [128, 4 * TT], BF16) for i in range(2)]
                    qsw = sb1("qsw", [128, 4 * TT], BF16)
                    ksw = sb1("ksw", [128, TT], BF16)
                    tS1 = [sb1(f"tS1_{i}", [128, 128], F32) for i in range(2)]
                    tS2 = [sb1(f"tS2_{i}", [128, 128], F32) for i in range(2)]
                    tS4 = [sb1(f"tS4_{i}", [128, 128], F32) for i in range(2)]
                    tV1 = [sb1(f"tV1_{i}", [128, 128], F32) for i in range(2)]
                    l1T_sb = [sb1(f"l1T_{i}", [128, 128], F32) for i in range(2)]
                    tp = [sb1(f"tp{i}", [128, TT], F32) for i in range(2)]
                    l1T_ps = [ps1(f"l1T_ps{i}", [128, 128], F32) for i in range(2)]
                    A_ps = [ps1(f"A_ps{i}", [128, 512], F32) for i in range(2)]

                    grp_pool = {}
                    preb_k2 = {}
                    preb_pe = {}
                    preb_ss = {}
                    for gq in range(NTT):
                        pg = gq % 2
                        g0 = TT * gq
                        dcs0 = ct.Ds(lambda e, pg=pg, g0=g0: e.dma_start(out=cosg[pg][:], in_=cosT[:, g0:g0 + TT]),
                                     waits=[("sp", grp_pool.get(gq - 2, 0))], sem="dcs")
                        dcs = ct.Ds(lambda e, pg=pg, g0=g0: e.dma_start(out=sing[pg][:], in_=sinT[:, g0:g0 + TT]),
                                    sem="dcs")
                        d_qg = ct.Dg(lambda e, pg=pg, g0=g0: e.dma_start(
                            out=qg[pg][:].rearrange("p (h n) -> p h n", h=4),
                            in_=qraw_d[:].rearrange("p (h n) -> p h n", h=4)[:, :, g0:g0 + TT]),
                            waits=[("sp", grp_pool.get(gq - 2, 0)),
                                   ("dqr", grp_pool.get(("d", gq - 2), 0)),
                                   ("dsp", m_pa_dgp)], sem="dqg")
                        d_sw1 = ct.Dg(lambda e, pg=pg: e.dma_start(out=qsw[0:64, :], in_=qg[pg][64:128, :]),
                                      waits=[("dqg", d_qg), ("sp", grp_pool.get(gq - 1, 0))], sem="dsw")
                        d_sw2 = ct.Dg(lambda e, pg=pg: e.dma_start(out=qsw[64:128, :], in_=qg[pg][0:64, :]), sem="dsw")
                        d_sw3 = ct.Dg(lambda e, g0=g0: e.dma_start(out=ksw[0:64, :], in_=k_raw[64:128, g0:g0 + TT]), sem="dsw")
                        d_sw4 = ct.Dg(lambda e, g0=g0: e.dma_start(out=ksw[64:128, :], in_=k_raw[0:64, g0:g0 + TT]), sem="dsw")
                        wro = [("dcs", dcs), ("dsw", d_sw4), ("dqg", d_qg), ("sv", m_pa_v)]
                        for h in range(4):
                            c0 = TT * h
                            for lo, hi in ((0, 64), (64, 128)):
                                sgn = ALU.subtract if lo == 0 else ALU.add
                                ct.G(lambda g, c0=c0, pg=pg, lo=lo, hi=hi: g.tensor_tensor(
                                    out=tp[0][lo:hi, :], in0=qg[pg][lo:hi, c0:c0 + TT],
                                    in1=cosg[pg][lo:hi, :], op=ALU.mult), waits=(wro if (h == 0 and lo == 0) else []))
                                ct.G(lambda g, c0=c0, pg=pg, lo=lo, hi=hi: g.tensor_tensor(
                                    out=tp[1][lo:hi, :], in0=qsw[lo:hi, c0:c0 + TT],
                                    in1=sing[pg][lo:hi, :], op=ALU.mult))
                                ct.G(lambda g, c0=c0, pg=pg, lo=lo, hi=hi, sgn=sgn: g.tensor_tensor(
                                    out=qg[pg][lo:hi, c0:c0 + TT], in0=tp[0][lo:hi, :],
                                    in1=tp[1][lo:hi, :], op=sgn))
                        m_qrope = ct.C["sp"]
                        ct.Dg(lambda e, pg=pg, g0=g0: e.dma_start(
                            out=qrope_d[:].rearrange("p (h n) -> p h n", h=4)[:, :, g0:g0 + TT],
                            in_=qg[pg][:].rearrange("p (h n) -> p h n", h=4)),
                            waits=[("sp", m_qrope)], sem="dqr")
                        for lo, hi in ((0, 64), (64, 128)):
                            sgn = ALU.subtract if lo == 0 else ALU.add
                            ct.G(lambda g, g0=g0, pg=pg, lo=lo, hi=hi: g.tensor_tensor(
                                out=tp[0][lo:hi, :], in0=k_raw[lo:hi, g0:g0 + TT],
                                in1=cosg[pg][lo:hi, :], op=ALU.mult))
                            ct.G(lambda g, g0=g0, pg=pg, lo=lo, hi=hi: g.tensor_tensor(
                                out=tp[1][lo:hi, :], in0=ksw[lo:hi, :],
                                in1=sing[pg][lo:hi, :], op=ALU.mult))
                            ct.G(lambda g, g0=g0, pg=pg, lo=lo, hi=hi, sgn=sgn: g.tensor_tensor(
                                out=k_rope[lo:hi, g0:g0 + TT], in0=tp[0][lo:hi, :],
                                in1=tp[1][lo:hi, :], op=sgn))
                        m_kr = ct.C["sp"]
                        grp_pool[gq] = m_kr
                        grp_pool[("d", gq)] = ct.C["dqr"]

                        for cc4 in range(4):
                            c = 4 * gq + cc4
                            c0 = CH * c
                            pc = c % 2
                            m_s1 = ct.S(lambda s, c0=c0, pc=pc: s.activation(
                                out=tS1[pc][:], in_=k_raw[:, c0:c0 + CH], func=AF.Exp, scale=-1.0),
                                waits=[("sv", preb_k2.get(c - 2, 0)), ("st", preb_pe.get(c - 2, 0))])
                            m_l1 = ct.S(lambda s, pc=pc: s.activation(out=tS2[pc][:], in_=tS1[pc][:],
                                                                      func=AF.Ln, bias=oneb[:, 0:1]))
                            m_tr = ct.T(lambda t, pc=pc: t.transpose(out=l1T_ps[pc][:], in_=tS2[pc][:],
                                                                     identity=idf_sb[:]),
                                        waits=[("ss", m_l1)], inc=1)
                            m_l1T = ct.S(lambda s, pc=pc: s.copy(out=l1T_sb[pc][:], in_=l1T_ps[pc][:]),
                                         waits=[("st", m_tr)])
                            m_s4 = ct.S(lambda s, c0=c0, pc=pc: s.activation(out=tS4[pc][:],
                                                                             in_=k_raw[:, c0:c0 + CH], func=AF.Exp))
                            m_A = ct.T(lambda t, pc=pc: t.matmul(out=A_ps[pc][:, 0:128], lhsT=l1T_sb[pc][:],
                                                                 rhs=nu16_sb[:], start=True, stop=True),
                                       waits=[("ss", m_l1T), ("dli", d_nu)])
                            m_smk = ct.T(lambda t, pc=pc: t.matmul(out=A_ps[pc][:, 480:481], lhsT=tS4[pc][:],
                                                                   rhs=ones_f[:], start=False, stop=True),
                                         waits=[("ss", m_s4)], inc=1)
                            ct.S(lambda s, c0=c0, pc=pc: s.activation(out=eA[:, c0:c0 + CH],
                                                                      in_=A_ps[pc][:, 0:128], func=AF.Exp),
                                 waits=[("st", m_smk)])
                            ct.S(lambda s, pc=pc: s.activation(out=tS1[pc][:], in_=A_ps[pc][:, 0:128],
                                                               func=AF.Exp, scale=-1.0))
                            m_eac = ct.S(lambda s, c=c, pc=pc: s.activation(out=eacs[:, c:c + 1],
                                                                            in_=A_ps[pc][:, 127:128], func=AF.Exp))
                            ct.V(lambda v, c=c, pc=pc: v.reciprocal(out=rkc[:, c:c + 1],
                                                                    in_=A_ps[pc][:, 480:481]),
                                 waits=[("st", m_smk)])
                            ct.V(lambda v, pc=pc: v.tensor_tensor(out=tV1[pc][:], in0=tS4[pc][:],
                                                                  in1=tS1[pc][:], op=ALU.mult),
                                 waits=[("ss", m_eac)])
                            ct.V(lambda v, c0=c0, pc=pc: v.tensor_copy(out=ktl[:, c0:c0 + CH], in_=tV1[pc][:]))
                            m_k2c = ct.V(lambda v, c0=c0, c=c, pc=pc: v.tensor_scalar(
                                out=k2[:, c0:c0 + CH], in0=tV1[pc][:], scalar1=eacs[:, c:c + 1],
                                scalar2=None, op0=ALU.mult))
                            preb_k2[c] = m_k2c
                            preb_pe[c] = ct.C["st"]
                            preb_ss[c] = ct.C["ss"]
                        emit_dq_tile()
                        emit_dq_tile()
                    m_preb_v = ct.C["sv"]
                    m_preb_p = ct.C["sp"]
                    m_preb_s = ct.C["ss"]
                    if part == "attn" and dbg:
                        wdbg = [("sv", ct.C["sv"]), ("ss", ct.C["ss"]), ("sp", ct.C["sp"]),
                                ("st", ct.C["st"]), ("dgp", ct.C["dgp"])]
                        ct.Do(lambda e: e.dma_start(out=dbg_t["d_kraw"], in_=k_raw[:]), waits=wdbg)
                        ct.Do(lambda e: e.dma_start(out=dbg_t["d_krope"], in_=k_rope[:]))
                        ct.Do(lambda e: e.dma_start(out=dbg_t["d_ktl"], in_=ktl[:]))
                        ct.Do(lambda e: e.dma_start(out=dbg_t["d_k2"], in_=k2[:]))
                        ct.Do(lambda e: e.dma_start(out=dbg_t["d_eA"], in_=eA[:]))
                        ct.Do(lambda e: e.dma_start(out=dbg_t["d_vtok"], in_=v_tok[:]))
                        ct.Do(lambda e: e.dma_start(out=dbg_t["d_rkc"], in_=rkc[:]))
                        ct.Do(lambda e: e.dma_start(out=dbg_t["d_eacs"], in_=eacs[:]))
                        ct.Do(lambda e: e.dma_start(out=dbg_t["d_rcol"], in_=rcol_sb[:]))
                        ct.Do(lambda e: e.dma_start(out=dbg_t["d_qraw"], in_=qraw_d[:]))
                        ct.Do(lambda e: e.dma_start(out=dbg_t["d_qrope"], in_=qrope_d[:]))

                esB = ExitStack()
                with esB:
                    sbB = lambda n, s, d: esB.enter_context(nc.sbuf_tensor(n, s, d))
                    psB = lambda n, s, d: esB.enter_context(nc.psum_tensor(n, s, d))
                    P_sb = [sbB(f"P{i}", [128, 512], BF16) for i in range(2)]
                    qro = [sbB(f"qro{i}", [128, 512], BF16) for i in range(2)]
                    Pg_sb = sbB("Pg", [128, 512], BF16)
                    qrawc = sbB("qrawc", [128, 512], BF16)
                    qe_f = sbB("qe_f", [128, 512], F32)
                    qhat = sbB("qhat", [128, 512], BF16)
                    k2t_sb = sbB("k2t", [128, 128], BF16)
                    v2t = sbB("v2t", [128, 128], BF16)
                    rq_sb = sbB("rq_sb", [128, 4], F32)
                    rd_sb = sbB("rd_sb", [128, 4], F32)
                    gla_f = sbB("gla_f", [128, 512], F32)
                    o_tok = sbB("o_tok", [128, 512], BF16)
                    ofeat_q = sbB("ofeat_q", [128, 4 * TQ], BF16)
                    wo_sb = sbB("wo_sb", [128, 4 * DM], BF16)
                    wop_sb = [sbB(f"wop{i}", [128, DM], F32) for i in range(2)]

                    ya = [[psB(f"ya{c}_{b}", [128, 512], F32) for b in range(2)] for c in range(2)]
                    gl_ps = [psB(f"gl{c}", [128, 512], F32) for c in range(2)]
                    sc_ps = [psB(f"sc{i}", [128, 512], F32) for i in range(2)]

                    for oc in range(4):
                        ct.Dg(lambda e, oc=oc: e.dma_start(
                            out=wo_sb[:, DM * oc:DM * (oc + 1)], in_=wo[128 * oc:128 * (oc + 1), :]),
                            sem="dwo")
                    d_wo = ct.C["dwo"]

                    rot = sc_ps
                    YOF = [0, 132]
                    uidx = 0
                    exp_marks = {}
                    bank_wait = {0: [], 1: []}
                    m_fin_s = m_preb_s
                    m_finv = {0: m_preb_v, 1: m_preb_v}
                    m_sbf = 0
                    m_sq_prev = 0
                    m_qS_prev = 0
                    m_kv_prev = 0
                    m_ot_prev = 0
                    m_oftr_prev = 0
                    m_qh_prev = 0
                    m_rsd = 0
                    tile_q_last = {}
                    pend = {}

                    def diag_block(tau):
                        nonlocal_vars = None
                        return

                    def emit_diag(tau, dmarks):
                        ctx = tau % 2
                        nonlocal_ = dmarks
                        m_y = dmarks["m_y"]
                        d_q = ct.Dg(lambda e, tau=tau: e.dma_start(
                            out=qrawc[:].rearrange("p (h n) -> p h n", h=4),
                            in_=qraw_d[:].rearrange("p (h n) -> p h n", h=4)[:, :, CH * tau:CH * (tau + 1)]),
                            waits=[("ss", dmarks["fin_s_prev"])], sem="dqc")
                        m_qe = ct.S(lambda s: s.activation(out=qe_f[:], in_=qrawc[:], func=AF.Exp),
                                    waits=[("dqc", d_q), ("sv", dmarks["qh_prev"]), ("st", dmarks["sq_prev"])])
                        for h in range(4):
                            ct.V(lambda v, h=h, tau=tau: v.scalar_tensor_tensor(
                                out=qhat[:, 128 * h:128 * (h + 1)], in0=qe_f[:, 128 * h:128 * (h + 1)],
                                scalar=0.5, in1=eA[:, CH * tau:CH * (tau + 1)],
                                op0=ALU.mult, op1=ALU.mult),
                                waits=([("ss", m_qe), ("st", dmarks["qS_prev"]),
                                        ("ss", preb_ss[tau]), ("sv", preb_k2[tau])] if h == 0 else []))
                        m_qh = ct.C["sv"]
                        m_gs = ct.T(lambda t, tau=tau, ctx=ctx: t.matmul(
                            out=gl_ps[ctx][:], lhsT=ktl[:, CH * tau:CH * (tau + 1)], rhs=qhat[:],
                            start=True, stop=True),
                            waits=[("sv", m_qh), ("st", m_y), ("ss", dmarks["fin_s_prev"])], inc=1)
                        for h in range(4):
                            ct.V(lambda v, h=h, tau=tau, ctx=ctx: v.scalar_tensor_tensor(
                                out=Pg_sb[:, 128 * h:128 * (h + 1)], in0=gl_ps[ctx][:, 128 * h:128 * (h + 1)],
                                scalar=rkc[:, tau:tau + 1], in1=mask_sb[:],
                                op0=ALU.mult, op1=ALU.mult),
                                waits=([("st", m_gs)] if h == 0 else []))
                        m_pg = ct.C["sv"]
                        for h in range(4):
                            ct.T(lambda t, h=h, tau=tau, ctx=ctx: t.matmul(
                                out=gl_ps[ctx][:, 128 * h:128 * (h + 1)], lhsT=Pg_sb[:, 128 * h:128 * (h + 1)],
                                rhs=v_tok[:, 129 * tau:129 * tau + 128],
                                start=(h == 0), stop=False), waits=([("sv", m_pg)] if h == 0 else []))
                        for h in range(4):
                            ct.T(lambda t, h=h, ctx=ctx: t.matmul(
                                out=gl_ps[ctx][:, 128 * h:128 * (h + 1)], lhsT=qhat[:, 128 * h:128 * (h + 1)],
                                rhs=Sbf[:], start=False, stop=(h == 3)),
                                waits=([("ss", dmarks["sbf_prev"])] if h == 0 else []), inc=(1 if h == 3 else 0))
                        m_qS = ct.C["st"]
                        for h in range(4):
                            ct.T(lambda t, h=h, ctx=ctx: t.matmul(
                                out=ya[ctx][0][:, 400 + h:401 + h], lhsT=qe_f[:, 128 * h:128 * (h + 1)],
                                rhs=ones_f[:], start=False, stop=(h == 3)),
                                waits=([("st", m_y)] if h == 0 else []), inc=(1 if h == 3 else 0))
                        m_sq = ct.C["st"]
                        m_k2t = ct.T(lambda t, tau=tau, ctx=ctx: t.transpose(
                            out=ya[ctx][0][:, 300:364].bitcast(BF16), in_=k2[:, CH * tau:CH * (tau + 1)],
                            identity=idb_sb[:]),
                            waits=[("dgp", d_idb), ("sv", preb_k2[tau])], inc=1)
                        m_k2s = ct.S(lambda s, ctx=ctx: s.copy(out=k2t_sb[:], in_=ya[ctx][0][:, 300:364].bitcast(BF16)),
                                     waits=[("st", m_k2t)])
                        m_v2 = ct.V(lambda v, tau=tau: v.tensor_scalar(
                            out=v2t[:], in0=v_tok[:, 129 * tau:129 * tau + 128],
                            scalar1=rkc[:, tau:tau + 1], scalar2=None, op0=ALU.mult),
                            waits=[("st", dmarks["kv_prev"])])
                        m_kv = ct.T(lambda t, ctx=ctx: t.matmul(
                            out=ya[ctx][1][:, 300:428], lhsT=k2t_sb[:], rhs=v2t[:],
                            start=False, stop=True), waits=[("ss", m_k2s), ("sv", m_v2)], inc=1)
                        m_su = ct.V(lambda v, tau=tau, ctx=ctx: v.scalar_tensor_tensor(
                            out=S_sb[:], in0=S_sb[:], scalar=eacs[:, tau:tau + 1], in1=ya[ctx][1][:, 300:428],
                            op0=ALU.mult, op1=ALU.add), waits=[("st", max(m_kv, m_qS))])
                        m_sbf_n = ct.S(lambda s: s.copy(out=Sbf[:], in_=S_sb[:]), waits=[("sv", m_su)])
                        m_rq = ct.V(lambda v, ctx=ctx: v.reciprocal(out=rq_sb[:], in_=ya[ctx][0][:, 400:404]),
                                    waits=[("st", m_sq)])
                        for h in range(4):
                            bnk, off = h // 2, YOF[h % 2]
                            ct.V(lambda v, h=h, ctx=ctx, bnk=bnk, off=off: v.reciprocal(
                                out=rd_sb[:, h:h + 1], in_=ya[ctx][bnk][:, off + 128:off + 129]),
                                waits=([("st", m_qS)] if h == 0 else []))
                        ct.V(lambda v: v.tensor_scalar(out=rd_sb[:], in0=rd_sb[:], scalar1=0.5,
                                                       scalar2=None, op0=ALU.mult))
                        m_glaev = ct.S(lambda s, ctx=ctx: s.copy(out=gla_f[:], in_=gl_ps[ctx][:]),
                                       waits=[("st", m_qS), ("sv", dmarks["ot_prev"])])
                        for h in range(4):
                            bnk, off = h // 2, YOF[h % 2]
                            ct.V(lambda v, h=h: v.tensor_scalar(
                                out=gla_f[:, 128 * h:128 * (h + 1)], in0=gla_f[:, 128 * h:128 * (h + 1)],
                                scalar1=rq_sb[:, h:h + 1], scalar2=None, op0=ALU.mult),
                                waits=([("ss", m_glaev), ("sv", m_rq)] if h == 0 else []))
                            ct.V(lambda v, h=h, ctx=ctx, bnk=bnk, off=off: v.scalar_tensor_tensor(
                                out=o_tok[:, 128 * h:128 * (h + 1)], in0=ya[ctx][bnk][:, off:off + 128],
                                scalar=rd_sb[:, h:h + 1], in1=gla_f[:, 128 * h:128 * (h + 1)],
                                op0=ALU.mult, op1=ALU.add),
                                waits=([("st", dmarks["oftr_prev"])] if h == 0 else []))
                        m_ot = ct.C["sv"]
                        tl = 128 * (tau % 16)
                        trdst = [(0, 436), (1, 436), (0, 300), (1, 300)]
                        for h in range(4):
                            bnk, off = trdst[h]
                            mtr = ct.T(lambda t, h=h, ctx=ctx, bnk=bnk, off=off: t.transpose(
                                out=ya[ctx][bnk][:, off:off + 64].bitcast(BF16),
                                in_=o_tok[:, 128 * h:128 * (h + 1)], identity=idb_sb[:]),
                                waits=([("sv", m_ot)] if h == 0 else []), inc=1)
                            ct.S(lambda s, h=h, tl=tl, ctx=ctx, bnk=bnk, off=off: s.copy(
                                out=ofeat_q[:, TQ * h + tl:TQ * h + tl + 128],
                                in_=ya[ctx][bnk][:, off:off + 64].bitcast(BF16)), waits=[("st", mtr)])
                        return {"m_sq": m_sq, "m_qS": m_qS, "m_kv": m_kv, "m_qh": m_qh,
                                "m_sbf": m_sbf_n, "m_ot": m_ot,
                                "m_oftr": ct.C["st"], "fin_s": ct.C["ss"], "fin_v": ct.C["sv"]}

                    def emit_wo(qtr):
                        nonlocal m_rsd
                        ctx = (16 * qtr + 15) % 2
                        wbanks = [ya[ctx][0], ya[ctx][1], gl_ps[ctx]]
                        em = {}
                        rsm = {}
                        widx = 0
                        for ts in range(16):
                            for kq in range(4):
                                bk = wbanks[widx % 3]
                                for oc in range(4):
                                    w0 = []
                                    if oc == 0:
                                        w0 = [("sv", em.get(widx - 3, 0))]
                                        if widx < 3:
                                            w0 += [("sv", m_finv[ctx]), ("ss", ct.C["ss"]), ("dwo", d_wo)]
                                    ct.T(lambda t, ts=ts, kq=kq, oc=oc, bk=bk: t.matmul(
                                        out=bk[:, 0:512],
                                        lhsT=ofeat_q[:, TQ * oc + 128 * ts:TQ * oc + 128 * (ts + 1)],
                                        rhs=wo_sb[:, DM * oc + 512 * kq:DM * oc + 512 * (kq + 1)],
                                        start=(oc == 0), stop=(oc == 3)),
                                        waits=w0, inc=(1 if oc == 3 else 0))
                                m_wp = ct.C["st"]
                                pwo = widx % 2
                                ct.V(lambda v, bk=bk, kq=kq, pwo=pwo: v.tensor_copy(
                                    out=wop_sb[pwo][:, 512 * kq:512 * (kq + 1)], in_=bk[:, 0:512]),
                                    waits=[("st", m_wp), ("dh1", rsm.get(widx - 2, 0))])
                                em[widx] = ct.C["sv"]
                                m_rsd = ct.Dg(lambda e, qtr=qtr, ts=ts, kq=kq, pwo=pwo: e.dma_start(
                                    out=rsin[qtr][128 * ts:128 * (ts + 1), 512 * kq:512 * (kq + 1)],
                                    in_=wop_sb[pwo][:, 512 * kq:512 * (kq + 1)]),
                                    waits=[("sv", em[widx])], sem="dh1")
                                rsm[widx] = m_rsd
                                widx += 1
                        m_finv[ctx] = ct.C["sv"]
                        ct.CC(lambda e, qtr=qtr: e.collective_compute(
                            "ReduceScatter", ALU.add, replica_groups=[[0, 1, 2, 3], [4, 5, 6, 7]],
                            ins=[rsin[qtr][:]], outs=[rsout[qtr][:]]),
                            waits=[("dh1", m_rsd)])

                    pending_diag = None
                    for tau in range(NCH):
                        pt = tau % 2
                        ctx = tau % 2
                        d_qt = ct.Dg(lambda e, tau=tau, pt=pt: e.dma_start(
                            out=qro[pt][:].rearrange("p (h n) -> p h n", h=4),
                            in_=qrope_d[:].rearrange("p (h n) -> p h n", h=4)[:, :, CH * tau:CH * (tau + 1)]),
                            waits=[("st", tile_q_last.get(tau - 2, 0)),
                                   ("dqr", grp_pool[("d", tau // 4)])], sem=f"dql{pt}")
                        q3 = qro[pt][:].rearrange("p (h n) -> p h n", h=4)
                        for c in range(tau + 1):
                            r = uidx % 2
                            diag = (c == tau)
                            wsc = [("ss", exp_marks.get(uidx - 2, 0)), (f"dql{pt}", d_qt),
                                   ("sp", grp_pool[c // 4])] + bank_wait[r]
                            bank_wait[r] = []
                            m_sc = ct.T(lambda t, c=c, r=r, q3=q3: t.matmul(
                                out=rot[r][:], lhsT=k_rope[:, CH * c:CH * (c + 1)], rhs=q3,
                                start=True, stop=True), waits=wsc, inc=1)
                            m_exp = ct.S(lambda s, r=r: s.activation(
                                out=P_sb[r][:], in_=rot[r][:], func=AF.Exp, scale=RSQ),
                                waits=[("st", m_sc)], serial=False)
                            exp_marks[uidx] = m_exp
                            ywait = [("ss", m_exp)]
                            if diag:
                                for h in range(4):
                                    ct.V(lambda v, h=h, r=r: v.tensor_tensor(
                                        out=P_sb[r][:, 128 * h:128 * (h + 1)],
                                        in0=P_sb[r][:, 128 * h:128 * (h + 1)],
                                        in1=mask_sb[:], op=ALU.mult),
                                        waits=([("ss", m_exp), ("dli", d_mk)] if h == 0 else []))
                                ywait = [("sv", ct.C["sv"])]
                            if c == 0:
                                ywait.append(("sv", m_finv[ctx]))
                                ywait.append(("ss", m_fin_s))
                            for h in range(4):
                                bnk, off = h // 2, YOF[h % 2]
                                ct.T(lambda t, h=h, r=r, c=c, diag=diag, ctx=ctx, bnk=bnk, off=off: t.matmul(
                                    out=ya[ctx][bnk][:, off:off + 129], lhsT=P_sb[r][:, 128 * h:128 * (h + 1)],
                                    rhs=v_tok[:, 129 * c:129 * c + 129],
                                    start=(c == 0 and h % 2 == 0), stop=diag),
                                    waits=(ywait if h == 0 else []), inc=(1 if h == 3 else 0))
                            m_y = ct.C["st"]
                            uidx += 1
                            if c == 2 and pending_diag is not None:
                                ptau, pm = pending_diag
                                res = emit_diag(ptau, pm)
                                m_sq_prev, m_qS_prev, m_kv_prev = res["m_sq"], res["m_qS"], res["m_kv"]
                                m_qh_prev, m_sbf, m_ot_prev = res["m_qh"], res["m_sbf"], res["m_ot"]
                                m_oftr_prev, m_fin_s = res["m_oftr"], res["fin_s"]
                                m_finv[ptau % 2] = res["fin_v"]
                                pending_diag = None
                                if ptau % 16 == 15:
                                    if ptau == 15 and need_ffn:
                                        dq_finish_and_ag()
                                    emit_wo(ptau // 16)
                        tile_q_last[tau] = ct.C["st"]
                        if pending_diag is not None:
                            ptau, pm = pending_diag
                            res = emit_diag(ptau, pm)
                            m_sq_prev, m_qS_prev, m_kv_prev = res["m_sq"], res["m_qS"], res["m_kv"]
                            m_qh_prev, m_sbf, m_ot_prev = res["m_qh"], res["m_sbf"], res["m_ot"]
                            m_oftr_prev, m_fin_s = res["m_oftr"], res["fin_s"]
                            m_finv[ptau % 2] = res["fin_v"]
                            if ptau % 16 == 15:
                                if ptau == 15 and need_ffn:
                                    dq_finish_and_ag()
                                emit_wo(ptau // 16)
                        pending_diag = (tau, {"m_y": m_y, "fin_s_prev": m_fin_s,
                                              "qh_prev": m_qh_prev, "sq_prev": m_sq_prev,
                                              "qS_prev": m_qS_prev, "kv_prev": m_kv_prev,
                                              "sbf_prev": m_sbf, "ot_prev": m_ot_prev,
                                              "oftr_prev": m_oftr_prev})
                        if tau < 12 and need_ffn:
                            for _ in range(6):
                                emit_dq_tile()
                    ptau, pm = pending_diag
                    res = emit_diag(ptau, pm)
                    m_fin_s = res["fin_s"]
                    m_finv[ptau % 2] = res["fin_v"]
                    emit_wo(3)
                    cc_rs_done = ct.C["cc"]

            if part == "attn":
                for i in range(4):
                    ct.Do(lambda e, i=i: e.dma_start(out=h1src[i], in_=rsout[i][:]),
                          waits=[("cc", cc_rs_done)])
                if dbg:
                    ct.Do(lambda e: e.dma_start(out=dbg_t["d_rsin"], in_=rsin[0][:]))
                ct.W("gpsimd", [("so", ct.C["so"])])

        if need_ffn:
            esC = ExitStack()
            with esC:
                sbC = lambda n, s, d: esC.enter_context(nc.sbuf_tensor(n, s, d))
                psC = lambda n, s, d: esC.enter_context(nc.psum_tensor(n, s, d))
                xT_f = sbC("xT_f", [128, KCH * TQ], BF16)
                esC1 = ExitStack()
                with esC1:
                    sbC1 = lambda n, s, d: esC1.enter_context(nc.sbuf_tensor(n, s, d))
                    att_f = sbC1("att_f", [128, DM], F32)
                    atsq = sbC1("atsq", [128, DM], F32)
                    x0t = sbC1("x0t", [128, DM], F32)
                    h1t = sbC1("h1t", [128, DM], F32)
                    fi_t = sbC1("fi_t", [128, DM], BF16)
                    mu_c = sbC1("mu_c", [128, 1], F32)
                    m2_c = sbC1("m2_c", [128, 1], F32)
                    r_c = sbC1("r_c", [128, 1], F32)
                    psC1 = lambda n, s, d: esC1.enter_context(nc.psum_tensor(n, s, d))
                    tr2_ps = [psC1(f"tr2_{i}", [128, 128], BF16) for i in range(2)]

                    wcc = [("cc", cc_rs_done)] if need_attn else []
                    for ts in range(16):
                        qtr, j = ts // 4, ts % 4
                        d_rs = ct.Ds(lambda e, qtr=qtr, j=j: e.dma_start(
                            out=att_f[:], in_=rsout[qtr][128 * j:128 * (j + 1), :]),
                            waits=wcc + [("st", ct.C["st"]), ("ss", ct.C["ss"]), ("sv", ct.C["sv"])],
                            sem="dpc")
                        d_x0 = ct.Ds(lambda e, ts=ts: e.dma_start(
                            out=x0t[:], in_=x0q[128 * ts:128 * (ts + 1), :]), sem="dpc")
                        ct.V(lambda v: v.tensor_reduce(out=mu_c[:], in_=att_f[:], axis=mybir.AxisListType.X,
                                                       op=ALU.add), waits=[("dpc", d_x0)])
                        ct.V(lambda v: v.tensor_scalar(out=mu_c[:], in0=mu_c[:], scalar1=1.0 / DM,
                                                       scalar2=None, op0=ALU.mult))
                        m_sq1 = ct.S(lambda s: s.activation(out=atsq[:], in_=att_f[:], func=AF.Square),
                                     waits=[("dpc", d_x0)])
                        ct.V(lambda v: v.tensor_reduce(out=m2_c[:], in_=atsq[:], axis=mybir.AxisListType.X,
                                                       op=ALU.add), waits=[("ss", m_sq1)])
                        ct.V(lambda v: v.tensor_scalar(out=m2_c[:], in0=m2_c[:], scalar1=1.0 / DM,
                                                       scalar2=None, op0=ALU.mult))
                        ct.V(lambda v: v.tensor_tensor(out=r_c[:], in0=mu_c[:], in1=mu_c[:], op=ALU.mult))
                        m_v1 = ct.V(lambda v: v.tensor_tensor(out=r_c[:], in0=m2_c[:], in1=r_c[:], op=ALU.subtract))
                        ct.S(lambda s: s.activation(out=r_c[:], in_=r_c[:], func=AF.Ln, bias=epsb[:, 0:1]),
                             waits=[("sv", m_v1)])
                        m_r1 = ct.S(lambda s: s.activation(out=r_c[:], in_=r_c[:], func=AF.Exp, scale=-0.5))
                        ct.V(lambda v: v.tensor_scalar(out=att_f[:], in0=att_f[:], scalar1=mu_c[:, 0:1],
                                                       scalar2=r_c[:, 0:1], op0=ALU.subtract, op1=ALU.mult),
                             waits=[("ss", m_r1)])
                        m_h1 = ct.V(lambda v: v.tensor_tensor(out=h1t[:], in0=att_f[:], in1=x0t[:], op=ALU.add),
                                    waits=[("dh1", ct.C["dh1"])])
                        ct.Dg(lambda e, ts=ts: e.dma_start(out=h1_d[128 * ts:128 * (ts + 1), :], in_=h1t[:]),
                              waits=[("sv", m_h1)], sem="dh1")
                        ct.V(lambda v: v.tensor_reduce(out=mu_c[:], in_=h1t[:], axis=mybir.AxisListType.X,
                                                       op=ALU.add))
                        ct.V(lambda v: v.tensor_scalar(out=mu_c[:], in0=mu_c[:], scalar1=1.0 / DM,
                                                       scalar2=None, op0=ALU.mult))
                        m_sq2 = ct.S(lambda s: s.activation(out=atsq[:], in_=h1t[:], func=AF.Square),
                                     waits=[("sv", m_h1)])
                        ct.V(lambda v: v.tensor_reduce(out=m2_c[:], in_=atsq[:], axis=mybir.AxisListType.X,
                                                       op=ALU.add), waits=[("ss", m_sq2)])
                        ct.V(lambda v: v.tensor_scalar(out=m2_c[:], in0=m2_c[:], scalar1=1.0 / DM,
                                                       scalar2=None, op0=ALU.mult))
                        ct.V(lambda v: v.tensor_tensor(out=r_c[:], in0=mu_c[:], in1=mu_c[:], op=ALU.mult))
                        m_v2 = ct.V(lambda v: v.tensor_tensor(out=r_c[:], in0=m2_c[:], in1=r_c[:], op=ALU.subtract))
                        ct.S(lambda s: s.activation(out=r_c[:], in_=r_c[:], func=AF.Ln, bias=epsb[:, 0:1]),
                             waits=[("sv", m_v2)])
                        m_r2 = ct.S(lambda s: s.activation(out=r_c[:], in_=r_c[:], func=AF.Exp, scale=-0.5))
                        m_fi = ct.V(lambda v: v.tensor_scalar(out=fi_t[:], in0=h1t[:], scalar1=mu_c[:, 0:1],
                                                              scalar2=r_c[:, 0:1], op0=ALU.subtract, op1=ALU.mult),
                                    waits=[("ss", m_r2)])
                        for kc in range(KCH):
                            pk = kc % 2
                            mtr = ct.T(lambda t, kc=kc, pk=pk: t.transpose(
                                out=tr2_ps[pk][:], in_=fi_t[:, 128 * kc:128 * (kc + 1)], identity=idb_sb[:]),
                                waits=([("sv", m_fi), ("ss", ct.C["ss"])] if kc < 2 else [("ss", ct.C["ss"])]),
                                inc=1)
                            ct.S(lambda s, kc=kc, ts=ts, pk=pk: s.copy(
                                out=xT_f[:, TQ * kc + 128 * ts:TQ * kc + 128 * (ts + 1)], in_=tr2_ps[pk][:]),
                                waits=[("st", mtr)])
                    m_xtf = ct.C["ss"]

                if not need_attn:
                    dq_finish_and_ag()
                esF1 = ExitStack()
                with esF1:
                    sbF = lambda n, s, d: esF1.enter_context(nc.sbuf_tensor(n, s, d))
                    psF = lambda n, s, d: esF1.enter_context(nc.psum_tensor(n, s, d))
                    wgbuf = [sbF(f"wg{i}", [128, KCH * 256], BF16) for i in range(2)]
                    wubuf = [sbF(f"wu{i}", [128, KCH * 256], BF16) for i in range(2)]
                    gt1 = sbF("gt1", [128, 1024], F32)
                    gt2 = sbF("gt2", [128, 1024], F32)
                    inter_sb = [sbF(f"inter{i}", [128, 1024], BF16) for i in range(2)]
                    gps = [psF(f"gps{i}", [128, 1024], F32) for i in range(2)]
                    ups = [psF(f"ups{i}", [128, 1024], F32) for i in range(2)]

                    def _sd(slot):
                        cc_, m = slot // 16, slot % 16
                        return 8 * cc_ + m if m < 8 else 64 + 8 * cc_ + (m - 8)

                    wbuf_mark = {}
                    ss_idx = 0
                    inter_dma = {}
                    for w in range(64):
                        pw = w % 2
                        s0 = 2 * w
                        np_off = 128 * _sd(s0)
                        wld = [("st", wbuf_mark.get(w - 2, 0)),
                               ("cc", dqst["cc"] if w == 0 else 0)]
                        for piece in range(4):
                            ct.Dg(lambda e, pw=pw, np_off=np_off, piece=piece: e.dma_start(
                                out=wgbuf[pw][:, 1024 * piece:1024 * (piece + 1)],
                                in_=gate_wt.rearrange("(c p) n -> p c n", p=128)[:, 4 * piece:4 * (piece + 1), np_off:np_off + 256]),
                                waits=(wld if piece == 0 else []), sem=f"dwd{pw}")
                            ct.Dg(lambda e, pw=pw, np_off=np_off, piece=piece: e.dma_start(
                                out=wubuf[pw][:, 1024 * piece:1024 * (piece + 1)],
                                in_=up_wt.rearrange("(c p) n -> p c n", p=128)[:, 4 * piece:4 * (piece + 1), np_off:np_off + 256]),
                                sem=f"dwd{pw}")
                        d_wg = ct.C[f"dwd{pw}"]
                        for sl in range(2):
                            slot = s0 + sl
                            for th in range(2):
                                pss = ss_idx % 2
                                ss_idx += 1
                                wmm = [(f"dwd{pw}", d_wg), ("ss", m_xtf),
                                       ("sv", inter_dma.get(("cons", ss_idx - 2 - 1), 0))]
                                for kc in range(KCH):
                                    for tq_ in range(2):
                                        kp = (kc // 2) if kc % 2 == 0 else 8 + kc // 2
                                        ct.T(lambda t, kc=kc, kp=kp, tq_=tq_, th=th, sl=sl, pw=pw, pss=pss: t.matmul(
                                            out=gps[pss][:, 512 * tq_:512 * (tq_ + 1)],
                                            lhsT=wgbuf[pw][:, 256 * kc + 128 * sl:256 * kc + 128 * (sl + 1)],
                                            rhs=xT_f[:, TQ * kp + 1024 * th + 512 * tq_:TQ * kp + 1024 * th + 512 * (tq_ + 1)],
                                            start=(kc == 0), stop=(kc == KCH - 1)),
                                            waits=(wmm if (kc == 0 and tq_ == 0) else []))
                                for kc in range(KCH):
                                    for tq_ in range(2):
                                        kp = (kc // 2) if kc % 2 == 0 else 8 + kc // 2
                                        ct.T(lambda t, kc=kc, kp=kp, tq_=tq_, th=th, sl=sl, pw=pw, pss=pss: t.matmul(
                                            out=ups[pss][:, 512 * tq_:512 * (tq_ + 1)],
                                            lhsT=wubuf[pw][:, 256 * kc + 128 * sl:256 * kc + 128 * (sl + 1)],
                                            rhs=xT_f[:, TQ * kp + 1024 * th + 512 * tq_:TQ * kp + 1024 * th + 512 * (tq_ + 1)],
                                            start=(kc == 0), stop=(kc == KCH - 1)),
                                            inc=(1 if (kc == KCH - 1 and tq_ == 1) else 0))
                                m_gu = ct.C["st"]
                                wbuf_mark[w] = m_gu
                                m_g2 = ct.S(lambda s, pss=pss: s.activation(out=gt1[:], in_=gps[pss][:],
                                                                            func=AF.Square),
                                            waits=[("st", m_gu), ("sv", ct.C["sv"])])
                                ct.V(lambda v: v.tensor_scalar(out=gt1[:], in0=gt1[:], scalar1=C2, scalar2=C1,
                                                               op0=ALU.mult, op1=ALU.add), waits=[("ss", m_g2)])
                                m_t3 = ct.V(lambda v, pss=pss: v.tensor_tensor(out=gt2[:], in0=gt1[:],
                                                                               in1=gps[pss][:], op=ALU.mult))
                                m_t4 = ct.S(lambda s: s.activation(out=gt2[:], in_=gt2[:], func=AF.Tanh),
                                            waits=[("sv", m_t3)])
                                ct.V(lambda v, pss=pss: v.scalar_tensor_tensor(
                                    out=gt1[:], in0=gt2[:], scalar=1.0, in1=gps[pss][:],
                                    op0=ALU.add, op1=ALU.mult), waits=[("ss", m_t4)])
                                pi = ss_idx % 2
                                m_int = ct.V(lambda v, pss=pss, pi=pi: v.scalar_tensor_tensor(
                                    out=inter_sb[pi][:], in0=gt1[:], scalar=0.5, in1=ups[pss][:],
                                    op0=ALU.mult, op1=ALU.mult),
                                    waits=[("dnt", inter_dma.get(("dma", ss_idx - 3), 0))])
                                inter_dma[("cons", ss_idx - 1)] = m_int
                                md = ct.Dg(lambda e, slot=slot, th=th, pi=pi: e.dma_start(
                                    out=inter_d[128 * slot:128 * (slot + 1), 1024 * th:1024 * (th + 1)],
                                    in_=inter_sb[pi][:]), waits=[("sv", m_int)], sem="dnt")
                                inter_dma[("dma", ss_idx - 1)] = md
                    m_f1_dma = ct.C["dnt"]

                esF2 = ExitStack()
                with esF2:
                    sbF2 = lambda n, s, d: esF2.enter_context(nc.sbuf_tensor(n, s, d))
                    psF2 = lambda n, s, d: esF2.enter_context(nc.psum_tensor(n, s, d))
                    ints_sb = sbF2("ints", [128, 128 * 256], BF16)
                    wdbuf = [sbF2(f"wd{i}", [128, DM], BF16) for i in range(4)]
                    dn_f = sbF2("dn_f", [128, DM], F32)
                    dsq = sbF2("dsq", [128, DM], F32)
                    h1b = sbF2("h1b", [128, DM], F32)
                    mu3 = sbF2("mu3", [128, 1], F32)
                    m23 = sbF2("m23", [128, 1], F32)
                    r3 = sbF2("r3", [128, 1], F32)
                    d_ps = [psF2(f"dps{i}", [128, DM], F32) for i in range(2)]

                    wd_mark = {}
                    m_evac = {}
                    for pp in range(8):
                        tc0 = 256 * pp
                        wint = [("dnt", m_f1_dma), ("st", m_evac.get(pp - 1, 0)),
                                ("sv", m_evac.get(("v", pp - 1), 0))]
                        dints = 0
                        for grp in range(8):
                            dints = ct.Dg(lambda e, grp=grp, tc0=tc0: e.dma_start(
                                out=ints_sb[:, 256 * 16 * grp:256 * 16 * (grp + 1)],
                                in_=inter_d.rearrange("(sl p) t -> p sl t", p=128)[:, 16 * grp:16 * (grp + 1), tc0:tc0 + 256]),
                                waits=(wint if grp == 0 else []), sem="dnl")
                        for slot in range(128):
                            pws = slot % 4
                            d_wd = ct.Dg(lambda e, slot=slot, pws=pws: e.dma_start(
                                out=wdbuf[pws][:], in_=down_wt[128 * slot:128 * (slot + 1), :]),
                                waits=[("st", wd_mark.get((pp, slot - 4), wd_mark.get((pp - 1, 124 + slot), 0)))],
                                sem=f"dwd{pws}")
                            for tsb in range(2):
                                for hq in range(4):
                                    ct.T(lambda t, slot=slot, tsb=tsb, hq=hq, pws=pws: t.matmul(
                                        out=d_ps[tsb][:, 512 * hq:512 * (hq + 1)],
                                        lhsT=ints_sb[:, 256 * slot + 128 * tsb:256 * slot + 128 * (tsb + 1)],
                                        rhs=wdbuf[pws][:, 512 * hq:512 * (hq + 1)],
                                        start=(slot == 0), stop=(slot == 127)),
                                        waits=([(f"dwd{pws}", d_wd), ("dnl", dints)] if (tsb == 0 and hq == 0) else []),
                                        inc=(1 if (tsb == 1 and hq == 3) else 0))
                            wd_mark[(pp, slot)] = ct.C["st"]
                        m_dn = ct.C["st"]
                        for tsb in range(2):
                            ts_g = 2 * pp + tsb
                            d_h1 = ct.Ds(lambda e, ts_g=ts_g: e.dma_start(
                                out=h1b[:], in_=h1_d[128 * ts_g:128 * (ts_g + 1), :]),
                                waits=[("so", ct.C["so"]), ("dh1", ct.C["dh1"])], sem="dhl")
                            m_cp = ct.V(lambda v, tsb=tsb: v.tensor_copy(out=dn_f[:], in_=d_ps[tsb][:]),
                                        waits=[("st", m_dn), ("so", ct.C["so"])])
                            if part == "ffn" and dbg:
                                m_dbgd = ct.Dg(lambda e, ts_g=ts_g: e.dma_start(
                                    out=dbg_f["f_down"][128 * ts_g:128 * (ts_g + 1), :], in_=dn_f[:]),
                                    waits=[("sv", m_cp)])
                                ct.W("vector", [("dgp", m_dbgd)])
                            ct.V(lambda v: v.tensor_reduce(out=mu3[:], in_=dn_f[:], axis=mybir.AxisListType.X,
                                                           op=ALU.add))
                            ct.V(lambda v: v.tensor_scalar(out=mu3[:], in0=mu3[:], scalar1=1.0 / DM,
                                                           scalar2=None, op0=ALU.mult))
                            m_sq3 = ct.S(lambda s: s.activation(out=dsq[:], in_=dn_f[:], func=AF.Square),
                                         waits=[("sv", m_cp)])
                            ct.V(lambda v: v.tensor_reduce(out=m23[:], in_=dsq[:], axis=mybir.AxisListType.X,
                                                           op=ALU.add), waits=[("ss", m_sq3)])
                            ct.V(lambda v: v.tensor_scalar(out=m23[:], in0=m23[:], scalar1=1.0 / DM,
                                                           scalar2=None, op0=ALU.mult))
                            ct.V(lambda v: v.tensor_tensor(out=r3[:], in0=mu3[:], in1=mu3[:], op=ALU.mult))
                            m_v3 = ct.V(lambda v: v.tensor_tensor(out=r3[:], in0=m23[:], in1=r3[:],
                                                                  op=ALU.subtract))
                            ct.S(lambda s: s.activation(out=r3[:], in_=r3[:], func=AF.Ln, bias=epsb[:, 0:1]),
                                 waits=[("sv", m_v3)])
                            m_r3 = ct.S(lambda s: s.activation(out=r3[:], in_=r3[:], func=AF.Exp, scale=-0.5))
                            ct.V(lambda v: v.tensor_scalar(out=dn_f[:], in0=dn_f[:], scalar1=mu3[:, 0:1],
                                                           scalar2=r3[:, 0:1], op0=ALU.subtract, op1=ALU.mult),
                                 waits=[("ss", m_r3)])
                            m_out = ct.V(lambda v: v.tensor_tensor(out=dn_f[:], in0=dn_f[:], in1=h1b[:],
                                                                   op=ALU.add), waits=[("dhl", d_h1)])
                            ct.Do(lambda e, ts_g=ts_g: e.dma_start(
                                out=out_d[128 * ts_g:128 * (ts_g + 1), :], in_=dn_f[:]),
                                waits=[("sv", m_out)])
                        m_evac[pp] = ct.C["st"]
                        m_evac[("v", pp)] = ct.C["sv"]
                    if part == "ffn" and dbg:
                        ct.Do(lambda e: e.dma_start(out=dbg_f["f_h1"], in_=h1_d[:]))
                        ct.Do(lambda e: e.dma_start(out=dbg_f["f_inter"], in_=inter_d[:]))
                        ct.Do(lambda e: e.dma_start(out=dbg_f["f_gw"], in_=gate_wt[:]))
                        ct.Do(lambda e: e.dma_start(out=dbg_f["f_dw"], in_=down_wt[:]))
                    ct.W("gpsimd", [("so", ct.C["so"])])

        with nc.Block() as block:
            for eng in ("sync", "gpsimd", "vector", "scalar", "tensor"):
                ops = ct.ops[eng]
                if ops:
                    getattr(block, eng)(lambda e, ops=ops: [f(e) for f in ops])
    return nc


FEAT = np.concatenate([np.arange(0, 2048, 2), np.arange(1, 2048, 2)])
NORD = np.concatenate([np.arange(0, 16384, 2), np.arange(1, 16384, 2)])
INVFEAT = np.argsort(FEAT)


def _tok_slice(g):
    return np.concatenate([np.arange(2048 * q + 512 * g, 2048 * q + 512 * g + 512)
                           for q in range(4)])


def _host_inputs(inputs):
    x = inputs["L0_attn_hidden_states"]
    Wq = inputs["L0_attn_Wq"]; Wk = inputs["L0_attn_Wk"]
    Wv = inputs["L0_attn_Wv"]; Wo = inputs["L0_attn_Wo"]

    inv_freq = 1.0 / (10000.0 ** (np.arange(0, 128, 2, dtype=np.float64) / 128.0))
    freqs = np.arange(NSEQ, dtype=np.float64)[:, None] * inv_freq[None, :]
    cosT = np.ascontiguousarray(np.tile(np.cos(freqs).T.astype(np.float32), (2, 1)))
    sinT = np.ascontiguousarray(np.tile(np.sin(freqs).T.astype(np.float32), (2, 1)))
    idm = np.eye(128, dtype=np.float32)
    sgrid = np.arange(128)[:, None]
    tgrid = np.arange(128)[None, :]
    masku = (sgrid <= tgrid).astype(np.float32)
    negu16 = masku * np.float32(-1.0 / 16.0)

    def packed_T(W_packed):
        b = W_packed.view('<u1')[:, ::4]
        return np.ascontiguousarray(b.T)
    gb = packed_T(inputs["L0_ffn_gate_W_packed"])[:, NORD]
    ub = packed_T(inputs["L0_ffn_up_W_packed"])[:, NORD]
    db = packed_T(inputs["L0_ffn_down_W_packed"])[:, FEAT]
    gsT = np.repeat(inputs["L0_ffn_gate_scales"].T, 16, axis=0)[:, NORD]
    usT = np.repeat(inputs["L0_ffn_up_scales"].T, 16, axis=0)[:, NORD]
    dsT = np.repeat(inputs["L0_ffn_down_scales"].T, 16, axis=0)[:, FEAT]

    maps = []
    for c in range(8):
        beta, g = c // 4, c % 4
        toks = _tok_slice(g)
        xb = x[beta]
        m = {
            "xT": np.ascontiguousarray(xb.T[FEAT]),
            "x0q": np.ascontiguousarray(xb[toks][:, FEAT]),
            "wq": np.ascontiguousarray(Wq[FEAT, 512 * g:512 * (g + 1)]),
            "wk": np.ascontiguousarray(Wk[FEAT, 128 * g:128 * (g + 1)]),
            "wv": np.ascontiguousarray(Wv[FEAT, 128 * g:128 * (g + 1)]),
            "wo": np.ascontiguousarray(Wo[512 * g:512 * (g + 1)][:, FEAT]),
            "cosT": cosT, "sinT": sinT,
            "idm": idm, "masku": masku, "negu16": negu16,
            "gbytes": np.ascontiguousarray(gb[128 * c:128 * (c + 1)]),
            "ubytes": np.ascontiguousarray(ub[128 * c:128 * (c + 1)]),
            "dbytes": np.ascontiguousarray(db[1024 * c:1024 * (c + 1)]),
            "gscale": np.ascontiguousarray(gsT[128 * c:128 * (c + 1)]),
            "uscale": np.ascontiguousarray(usT[128 * c:128 * (c + 1)]),
            "dscale": np.ascontiguousarray(dsT[1024 * c:1024 * (c + 1)]),
        }
        maps.append(m)
    return maps


def kernel(**inputs):
    maps = _host_inputs(inputs)
    nc = build_program("all")
    res = run_bass_kernel_spmd(nc, maps, core_ids=list(range(8))).results
    out = np.zeros((2, NSEQ, DM), dtype=np.float32)
    for c in range(8):
        beta, g = c // 4, c % 4
        toks = _tok_slice(g)
        out[beta, toks] = res[c]["out"][:, INVFEAT]
    return out


# version 06
# speedup vs baseline: 1.0265x; 1.0265x over previous
import numpy as np
import ml_dtypes
from contextlib import ExitStack
import concourse.bass as bass
import concourse.mybir as mybir
from concourse.bass_utils import run_bass_kernel_spmd

F32 = mybir.dt.float32
BF16 = mybir.dt.bfloat16
U8 = mybir.dt.uint8
I32 = mybir.dt.int32
FP8 = mybir.dt.float8e4
AF = mybir.ActivationFunctionType
ALU = mybir.AluOpType

NSEQ = 8192
DM = 2048
CH = 128
NCH = NSEQ // CH
TT = 512
NTT = NSEQ // TT
MLP = 16384
KCH = DM // 128
TQ = 2048
RSQ = float(128) ** -0.5
C1 = 0.7978845608028654
C2 = 0.7978845608028654 * 0.044715
DQW = 256
N_DQ = (2 * MLP // DQW) + (8 * DM // DQW)

TWO_LAUNCH = False


class Ctx:
    def __init__(self, nc):
        self.nc = nc
        self.ops = {k: [] for k in ("sync", "gpsimd", "vector", "scalar", "tensor")}
        self.C = {k: 0 for k in ('dli', 'dgp', 'sv', 'ss', 'st', 'sp', 'cc', 'so', 'dwq', 'dsp', 'dqi', 'dsc', 'dqo', 'dqg', 'dsw', 'dqr', 'dql0', 'dql1', 'dqc', 'drc', 'dxa0', 'dxa1', 'dcs', 'dnt', 'dnl', 'dwd0', 'dwd1', 'dwd2', 'dwd3', 'dh1', 'dpc', 'dhl', 'dwo')}
        self.SEM = {}

    def add(self, eng, fn):
        self.ops[eng].append(fn)

    def _mk(self, eng, build, waits, sem, inc):
        waits = [(s, v) for (s, v) in waits if v > 0]
        self.C[sem] += inc
        SEM = self.SEM

        def f(e, _b=build, _w=tuple(waits), _s=sem, _i=inc):
            for s, v in _w:
                e.wait_ge(SEM[s], v)
            ins = _b(e)
            if _i:
                ins.then_inc(SEM[_s], _i)
        self.add(eng, f)
        return self.C[sem]

    def V(self, build, waits=(), serial=True):
        w = list(waits)
        if serial:
            w.append(("sv", self.C["sv"]))
        return self._mk("vector", build, w, "sv", 1)

    def S(self, build, waits=(), serial=True):
        w = list(waits)
        if serial:
            w.append(("ss", self.C["ss"]))
        return self._mk("scalar", build, w, "ss", 1)

    def G(self, build, waits=(), serial=True):
        w = list(waits)
        if serial:
            w.append(("sp", self.C["sp"]))
        return self._mk("gpsimd", build, w, "sp", 1)

    def T(self, build, waits=(), inc=0):
        return self._mk("tensor", build, waits, "st", inc)

    def Dg(self, build, waits=(), sem="dgp"):
        return self._mk("gpsimd", build, waits, sem, 16)

    def Ds(self, build, waits=(), sem="dli"):
        return self._mk("sync", build, waits, sem, 16)

    def Do(self, build, waits=()):
        return self._mk("gpsimd", build, waits, "so", 16)

    def CC(self, build, waits=()):
        return self._mk("gpsimd", build, waits, "cc", 1)

    def W(self, eng, waits):
        waits = [(s, v) for (s, v) in waits if v > 0]
        SEM = self.SEM

        def f(e, _w=tuple(waits)):
            for s, v in _w:
                e.wait_ge(SEM[s], v)
        self.add(eng, f)


def build_program(part="all", dbg=False):
    nc = bass.Bass("TRN2", target_bir_lowering=False)
    ct = Ctx(nc)
    need_attn = part in ("all", "attn")
    need_ffn = part in ("all", "ffn")

    din = lambda n, s, d: nc.dram_tensor(n, s, d, kind="ExternalInput").ap()
    dint = lambda n, s, d, sp="Local": nc.dram_tensor(n, s, d, addr_space=sp).ap()

    if need_attn:
        xT = din("xT", [DM, NSEQ], F32)
        wq = din("wq", [DM, 512], F32)
        wk = din("wk", [DM, 128], F32)
        wv = din("wv", [DM, 128], F32)
        wo = din("wo", [512, DM], F32)
        cosT = din("cosT", [128, NSEQ], F32)
        sinT = din("sinT", [128, NSEQ], F32)
    if need_ffn:
        x0q = din("x0q", [TQ, DM], F32)
        gbytes = din("gbytes", [128, MLP], U8)
        ubytes = din("ubytes", [128, MLP], U8)
        dbytes = din("dbytes", [1024, DM], U8)
        gscale = din("gscale", [128, MLP], F32)
        uscale = din("uscale", [128, MLP], F32)
        dscale = din("dscale", [1024, DM], F32)
    idm = din("idm", [128, 128], F32)
    masku = din("masku", [128, 128], F32)
    negu16 = din("negu16", [128, 128], F32)

    if need_attn:
        qraw_d = dint("qraw_d", [128, 4 * NSEQ], BF16)
        qrope_d = dint("qrope_d", [128, 4 * NSEQ], BF16)
        rstd_d = dint("rstd_d", [1, TT], F32)
        rsin = [dint(f"rsin{i}", [TQ, DM], F32) for i in range(4)]
        rsout = [dint(f"rsout{i}", [512, DM], F32) for i in range(4)]
    if part == "attn":
        h1src = [nc.dram_tensor(f"ho{i}", [512, DM], F32, kind="ExternalOutput").ap()
                 for i in range(4)]
        if dbg:
            dbg_t = {}
            for nm, shp, dt_ in (("d_qraw", [128, 4 * NSEQ], BF16), ("d_qrope", [128, 4 * NSEQ], BF16),
                                 ("d_kraw", [128, NSEQ], BF16), ("d_krope", [128, NSEQ], BF16),
                                 ("d_ktl", [128, NSEQ], BF16), ("d_k2", [128, NSEQ], BF16),
                                 ("d_eA", [128, NSEQ], BF16), ("d_vtok", [128, NCH * 129], BF16),
                                 ("d_rkc", [128, NCH], F32), ("d_eacs", [128, NCH], F32),
                                 ("d_rsin", [TQ, DM], F32), ("d_ofeat", [128, 4 * TQ], BF16),
                                 ("d_rcol", [128, 4], F32)):
                dbg_t[nm] = nc.dram_tensor(nm, shp, dt_, kind="ExternalOutput").ap()
    if need_ffn:
        out_d = nc.dram_tensor("out", [TQ, DM], F32, kind="ExternalOutput").ap()
        agg_in = dint("agg_in", [256, MLP], BF16)
        agu_in = dint("agu_in", [256, MLP], BF16)
        agd_in = dint("agd_in", [2048, DM], BF16)
        gate_wt = dint("gate_wt", [DM, MLP], BF16, "Shared")
        up_wt = dint("up_wt", [DM, MLP], BF16, "Shared")
        down_wt = dint("down_wt", [MLP, DM], BF16, "Shared")
        inter_d = dint("inter_d", [MLP, TQ], BF16)
        h1_d = dint("h1_d", [TQ, DM], F32)
    if part == "ffn":
        rsout = [din(f"rsi{i}", [512, DM], F32) for i in range(4)]
        if dbg:
            dbg_f = {}
            for nm, shp, dt_ in (("f_h1", [TQ, DM], F32), ("f_inter", [MLP, TQ], BF16),
                                 ("f_gw", [DM, MLP], BF16), ("f_dw", [MLP, DM], BF16),
                                 ("f_down", [TQ, DM], F32)):
                dbg_f[nm] = nc.dram_tensor(nm, shp, dt_, kind="ExternalOutput").ap()

    es = ExitStack()
    with es:
        sb = lambda n, s, d: es.enter_context(nc.sbuf_tensor(n, s, d))
        for s in ('dli', 'dgp', 'sv', 'ss', 'st', 'sp', 'cc', 'so', 'dwq', 'dsp', 'dqi', 'dsc', 'dqo', 'dqg', 'dsw', 'dqr', 'dql0', 'dql1', 'dqc', 'drc', 'dxa0', 'dxa1', 'dcs', 'dnt', 'dnl', 'dwd0', 'dwd1', 'dwd2', 'dwd3', 'dh1', 'dpc', 'dhl', 'dwo'):
            ct.SEM[s] = es.enter_context(nc.semaphore(s))

        idf_sb = sb("idf", [128, 128], F32)
        idb_sb = sb("idb", [128, 128], BF16)
        mask_sb = sb("mask", [128, 128], F32)
        nu16_sb = sb("nu16", [128, 128], F32)
        ones_bf = sb("ones_bf", [128, 1], BF16)
        ones_f = sb("ones_f", [128, 1], F32)
        onesk1 = sb("onesk1", [1, 128], F32)
        epsb = sb("epsb", [128, 1], F32)
        oneb = sb("oneb", [128, 1], F32)

        ct.Ds(lambda e: e.dma_start(out=idf_sb[:], in_=idm))
        d_mk = ct.Ds(lambda e: e.dma_start(out=mask_sb[:], in_=masku))
        d_nu = ct.Ds(lambda e: e.dma_start(out=nu16_sb[:], in_=negu16))
        d_idb = ct.Dg(lambda e: e.dma_start(out=idb_sb[:], in_=idm))
        ct.V(lambda v: v.memset(ones_bf[:], 1.0))
        ct.V(lambda v: v.memset(ones_f[:], 1.0))
        ct.V(lambda v: v.memset(onesk1[:], 1.0))
        ct.V(lambda v: v.memset(epsb[:], 1e-5))
        m_const = ct.V(lambda v: v.memset(oneb[:], 1.0))

        if need_ffn:
            dqv = sb("dqv", [128, DQW], I32)
            dqs = sb("dqs", [128, DQW], F32)
            dqt1 = sb("dqt1", [128, DQW], I32)
            dqt2 = sb("dqt2", [128, DQW], I32)
            dq8 = sb("dq8", [128, DQW], U8)
            dqbf = sb("dqbf", [128, DQW], BF16)
            dqlo = sb("dqlo", [128, DQW], BF16)
            dqhi = sb("dqhi", [128, DQW], BF16)
            dq_jobs = []
            nw = MLP // DQW
            for w in range(nw):
                dq_jobs.append((gbytes[:, DQW * w:DQW * (w + 1)], gscale[:, DQW * w:DQW * (w + 1)],
                                agg_in[0:128, DQW * w:DQW * (w + 1)], agg_in[128:256, DQW * w:DQW * (w + 1)]))
            for w in range(nw):
                dq_jobs.append((ubytes[:, DQW * w:DQW * (w + 1)], uscale[:, DQW * w:DQW * (w + 1)],
                                agu_in[0:128, DQW * w:DQW * (w + 1)], agu_in[128:256, DQW * w:DQW * (w + 1)]))
            nwd = DM // DQW
            for jc in range(8):
                for w in range(nwd):
                    dq_jobs.append((dbytes[128 * jc:128 * (jc + 1), DQW * w:DQW * (w + 1)],
                                    dscale[128 * jc:128 * (jc + 1), DQW * w:DQW * (w + 1)],
                                    agd_in[128 * jc:128 * (jc + 1), DQW * w:DQW * (w + 1)],
                                    agd_in[1024 + 128 * jc:1152 + 128 * jc, DQW * w:DQW * (w + 1)]))
            dqst = {"i": 0, "last_out": 0, "cc": 0}

            def emit_dq_tile():
                i = dqst["i"]
                if i >= len(dq_jobs):
                    return
                dqst["i"] += 1
                src, scl, dlo, dhi = dq_jobs[i]
                d_in = ct.Dg(lambda e, s=src: e.dma_start(out=dqv[:], in_=s),
                             waits=[("sv", dqst.get("mcons", 0)), ("dqo", dqst.get("mout", 0))],
                             sem="dqi")
                d_sc = ct.Ds(lambda e, s=scl: e.dma_start(out=dqs[:], in_=s),
                             waits=[("sv", dqst.get("mcons", 0))], sem="dsc")
                wd = [("dqi", d_in), ("dsc", d_sc)]
                ct.V(lambda v: v.tensor_scalar(out=dqt1[:], in0=dqv[:], scalar1=8, scalar2=4,
                                               op0=ALU.bitwise_and, op1=ALU.logical_shift_left), waits=wd)
                ct.V(lambda v: v.tensor_scalar(out=dqt2[:], in0=dqv[:], scalar1=7, scalar2=2,
                                               op0=ALU.bitwise_and, op1=ALU.logical_shift_left))
                ct.V(lambda v: v.tensor_tensor(out=dq8[:], in0=dqt1[:], in1=dqt2[:], op=ALU.add))
                ct.V(lambda v: v.tensor_copy(out=dqbf[:], in_=dq8[:].bitcast(FP8)))
                ct.V(lambda v: v.scalar_tensor_tensor(out=dqlo[:], in0=dqbf[:], scalar=64.0,
                                                      in1=dqs[:], op0=ALU.mult, op1=ALU.mult))
                ct.V(lambda v: v.tensor_scalar(out=dqt1[:], in0=dqv[:], scalar1=0x70, scalar2=2,
                                               op0=ALU.bitwise_and, op1=ALU.logical_shift_right))
                ct.V(lambda v: v.tensor_single_scalar(out=dqt2[:], in_=dqv[:], scalar=0x80, op=ALU.bitwise_and))
                ct.V(lambda v: v.tensor_tensor(out=dq8[:], in0=dqt1[:], in1=dqt2[:], op=ALU.add))
                ct.V(lambda v: v.tensor_copy(out=dqbf[:], in_=dq8[:].bitcast(FP8)))
                mcons = ct.V(lambda v: v.scalar_tensor_tensor(out=dqhi[:], in0=dqbf[:], scalar=64.0,
                                                              in1=dqs[:], op0=ALU.mult, op1=ALU.mult))
                dqst["mcons"] = mcons
                ct.Dg(lambda e, d=dlo: e.dma_start(out=d, in_=dqlo[:]), waits=[("sv", mcons)], sem="dqo")
                m2 = ct.Dg(lambda e, d=dhi: e.dma_start(out=d, in_=dqhi[:]), sem="dqo")
                dqst["mout"] = m2
                dqst["last_out"] = m2

            def dq_finish_and_ag():
                while dqst["i"] < len(dq_jobs):
                    emit_dq_tile()
                ct.CC(lambda e: e.collective_compute("AllGather", ALU.bypass,
                                                     replica_groups=[list(range(8))],
                                                     ins=[agg_in[:]], outs=[gate_wt[:]]),
                      waits=[("dqo", dqst["last_out"])])
                ct.CC(lambda e: e.collective_compute("AllGather", ALU.bypass,
                                                     replica_groups=[list(range(8))],
                                                     ins=[agu_in[:]], outs=[up_wt[:]]))
                dqst["cc"] = ct.CC(lambda e: e.collective_compute("AllGather", ALU.bypass,
                                                                  replica_groups=[list(range(8))],
                                                                  ins=[agd_in[:]], outs=[down_wt[:]]))
        else:
            def emit_dq_tile():
                pass

            def dq_finish_and_ag():
                pass

        cc_rs_done = 0
        if need_attn:
            esA = ExitStack()
            with esA:
                sbA = lambda n, s, d: esA.enter_context(nc.sbuf_tensor(n, s, d))
                v_tok = sbA("v_tok", [128, NCH * 129], BF16)
                k_rope = sbA("k_rope", [128, NSEQ], BF16)
                ktl = sbA("ktl", [128, NSEQ], BF16)
                k2 = sbA("k2", [128, NSEQ], BF16)
                eA = sbA("eA", [128, NSEQ], BF16)
                rkc = sbA("rkc", [128, NCH], F32)
                eacs = sbA("eacs", [128, NCH], F32)
                S_sb = sbA("S_sb", [128, 128], F32)
                Sbf = sbA("Sbf", [128, 128], BF16)
                wsum_sb = sbA("wsum_sb", [1, 640], F32)
                wsumv_sb = sbA("wsumv_sb", [1, 128], F32)
                k_raw = sbA("k_raw", [128, NSEQ], BF16)

                ct.V(lambda v: v.memset(S_sb[:], 0.0))
                ct.V(lambda v: v.memset(Sbf[:], 0.0))
                m_vones = ct.V(lambda v: v.memset(
                    v_tok[:].rearrange("p (c w) -> p c w", w=129)[:, :, 128:129], 1.0))

                esA1 = ExitStack()
                with esA1:
                    sb1 = lambda n, s, d: esA1.enter_context(nc.sbuf_tensor(n, s, d))
                    ps1 = lambda n, s, d: esA1.enter_context(nc.psum_tensor(n, s, d))
                    wqkv_sb = sb1("wqkv", [128, KCH * 640], BF16)
                    wv_sb = sb1("wv_sb", [128, KCH * 128], BF16)
                    xa = [sb1(f"xa{i}", [128, TT], BF16) for i in range(2)]
                    xsq = [sb1(f"xsq{i}", [128, TT], BF16) for i in range(2)]
                    qstage = sb1("qstage", [128, 4 * TT], BF16)
                    murow = sb1("murow", [1, TT], F32)
                    nmurow = sb1("nmurow", [1, TT], F32)
                    m2row = sb1("m2row", [1, TT], F32)
                    varrow = sb1("varrow", [1, TT], F32)
                    rstdrow = sb1("rstdrow", [1, TT], F32)
                    rbc_sb = sb1("rbc_sb", [128, TT], F32)
                    rcol_sb = sb1("rcol_sb", [128, 4], F32)
                    esPA = ExitStack()
                    psPA = lambda n, s, d: esPA.enter_context(nc.psum_tensor(n, s, d))
                    proj_ps = [psPA(f"proj{i}", [128, TT], F32) for i in range(5)]
                    v_ps = psPA("v_ps", [128, TT], F32)
                    misc_ps = psPA("misc_ps", [128, TT], F32)
                    stats2_ps = psPA("stats2_ps", [128, TT], F32)

                    for kc in range(KCH):
                        ct.Dg(lambda e, kc=kc: e.dma_start(
                            out=wqkv_sb[:, 640 * kc:640 * kc + 512], in_=wq[128 * kc:128 * (kc + 1), :]), sem="dwq")
                        ct.Dg(lambda e, kc=kc: e.dma_start(
                            out=wqkv_sb[:, 640 * kc + 512:640 * (kc + 1)], in_=wk[128 * kc:128 * (kc + 1), :]), sem="dwq")
                        ct.Dg(lambda e, kc=kc: e.dma_start(
                            out=wv_sb[:, 128 * kc:128 * (kc + 1)], in_=wv[128 * kc:128 * (kc + 1), :]), sem="dwq")
                    d_w = ct.C["dwq"]
                    for kc in range(KCH):
                        ct.T(lambda t, kc=kc: t.matmul(out=misc_ps[0:1, 0:512], lhsT=ones_bf[:],
                                                       rhs=wqkv_sb[:, 640 * kc:640 * kc + 512],
                                                       start=(kc == 0), stop=(kc == KCH - 1)),
                             waits=([("dwq", d_w), ("sv", m_const)] if kc == 0 else []))
                        ct.T(lambda t, kc=kc: t.matmul(out=stats2_ps[0:1, 0:128], lhsT=ones_bf[:],
                                                       rhs=wqkv_sb[:, 640 * kc + 512:640 * (kc + 1)],
                                                       start=(kc == 0), stop=(kc == KCH - 1)))
                        ct.T(lambda t, kc=kc: t.matmul(out=v_ps[0:1, 0:128], lhsT=ones_bf[:],
                                                       rhs=wv_sb[:, 128 * kc:128 * (kc + 1)],
                                                       start=(kc == 0), stop=(kc == KCH - 1)),
                             inc=(1 if kc == KCH - 1 else 0))
                    m_ws = ct.C["st"]
                    ct.S(lambda s: s.copy(out=wsum_sb[0:1, 0:512], in_=misc_ps[0:1, 0:512]), waits=[("st", m_ws)])
                    ct.S(lambda s: s.copy(out=wsum_sb[0:1, 512:640], in_=stats2_ps[0:1, 0:128]))
                    m_wsv = ct.S(lambda s: s.copy(out=wsumv_sb[:], in_=v_ps[0:1, 0:128]))

                    ch_end_st = {}
                    ch_end_ss = {}
                    prev_ev = {"sv": 0, "ss": 0}
                    prev_spill = 0
                    for i in range(NTT):
                        t0 = TT * i
                        for kc in range(KCH):
                            p = kc % 2
                            wxa = [("st", ch_end_st.get((i, kc - 2), ch_end_st.get((i - 1, KCH - 2 + kc), 0))),
                                   ("ss", ch_end_ss.get((i, kc - 2), ch_end_ss.get((i - 1, KCH - 2 + kc), 0)))]
                            d_xa = ct.Dg(lambda e, p=p, kc=kc, t0=t0: e.dma_start(
                                out=xa[p][:], in_=xT[128 * kc:128 * (kc + 1), t0:t0 + TT]), waits=wxa,
                                sem=f"dxa{p}")
                            wfirst = [(f"dxa{p}", d_xa)] + ([("ss", m_wsv)] if (i == 0 and kc == 0) else [])
                            if kc == 0:
                                wfirst += [("sv", prev_ev["sv"]), ("ss", prev_ev["ss"])]
                            ct.T(lambda t, p=p, kc=kc: t.matmul(
                                out=misc_ps[0:1, :], lhsT=ones_bf[:], rhs=xa[p][:],
                                start=(kc == 0), stop=(kc == KCH - 1)), waits=wfirst)
                            msq = ct.S(lambda s, p=p: s.activation(out=xsq[p][:], in_=xa[p][:], func=AF.Square),
                                       waits=[(f"dxa{p}", d_xa)])
                            ct.T(lambda t, p=p, kc=kc: t.matmul(
                                out=stats2_ps[0:1, :], lhsT=ones_bf[:], rhs=xsq[p][:],
                                start=(kc == 0), stop=(kc == KCH - 1)), waits=[("ss", msq)])
                            for mc in range(5):
                                ct.T(lambda t, mc=mc, kc=kc, p=p: t.matmul(
                                    out=proj_ps[mc][:],
                                    lhsT=wqkv_sb[:, 640 * kc + 128 * mc:640 * kc + 128 * (mc + 1)],
                                    rhs=xa[p][:], start=(kc == 0), stop=False),
                                    waits=([("sv", prev_ev["sv"]), ("dwq", d_w)] if kc == 0 else []))
                            for ts4 in range(4):
                                ct.T(lambda t, ts4=ts4, kc=kc, p=p: t.matmul(
                                    out=v_ps[:, 128 * ts4:128 * (ts4 + 1)],
                                    lhsT=xa[p][:, 128 * ts4:128 * (ts4 + 1)],
                                    rhs=wv_sb[:, 128 * kc:128 * (kc + 1)],
                                    start=(kc == 0 and ts4 == 0), stop=False),
                                    waits=([("sv", prev_ev["sv"])] if (kc == 0 and ts4 == 0) else []),
                                    inc=(1 if ts4 == 3 else 0))
                            ch_end_st[(i, kc)] = ct.C["st"]
                            ch_end_ss[(i, kc)] = ct.C["ss"]
                        m_stats = ch_end_st[(i, KCH - 1)]
                        ct.V(lambda v: v.tensor_scalar(out=murow[:], in0=misc_ps[0:1, :],
                                                       scalar1=1.0 / DM, scalar2=None, op0=ALU.mult),
                             waits=[("st", m_stats)])
                        m_negmu = ct.V(lambda v: v.tensor_scalar(out=nmurow[:], in0=murow[:],
                                                                 scalar1=-1.0, scalar2=None, op0=ALU.mult))
                        ct.V(lambda v: v.tensor_scalar(out=m2row[:], in0=stats2_ps[0:1, :],
                                                       scalar1=1.0 / DM, scalar2=None, op0=ALU.mult))
                        ct.V(lambda v: v.tensor_tensor(out=varrow[:], in0=murow[:], in1=murow[:], op=ALU.mult))
                        m_var = ct.V(lambda v: v.tensor_tensor(out=varrow[:], in0=m2row[:], in1=varrow[:],
                                                               op=ALU.subtract))
                        ct.S(lambda s: s.activation(out=varrow[:], in_=varrow[:], func=AF.Ln,
                                                    bias=epsb[0:1, 0:1]), waits=[("sv", m_var)])
                        m_rstd = ct.S(lambda s: s.activation(out=rstdrow[:], in_=varrow[:], func=AF.Exp,
                                                             scale=-0.5))
                        for mc in range(5):
                            ct.T(lambda t, mc=mc: t.matmul(
                                out=proj_ps[mc][:], lhsT=wsum_sb[0:1, 128 * mc:128 * (mc + 1)],
                                rhs=nmurow[:], start=False, stop=True),
                                waits=([("sv", m_negmu)] if mc == 0 else []), inc=(1 if mc == 4 else 0))
                        m_proj = ct.C["st"]
                        for ts4 in range(4):
                            ct.T(lambda t, ts4=ts4: t.matmul(
                                out=v_ps[:, 128 * ts4:128 * (ts4 + 1)],
                                lhsT=nmurow[0:1, 128 * ts4:128 * (ts4 + 1)], rhs=wsumv_sb[:],
                                start=False, stop=True), inc=(1 if ts4 == 3 else 0))
                        m_vproj = ct.C["st"]
                        m_rbc = ct.T(lambda t: t.matmul(out=misc_ps[:], lhsT=onesk1[:], rhs=rstdrow[:],
                                                        start=True, stop=True),
                                     waits=[("ss", m_rstd), ("sv", m_var)], inc=1)
                        m_rd1 = ct.Dg(lambda e: e.dma_start(out=rstd_d, in_=rstdrow[:]),
                              waits=[("ss", m_rstd), ("sv", prev_ev["sv"])], sem="drc")
                        m_rcold = ct.Dg(lambda e: e.dma_start(
                            out=rcol_sb[:], in_=rstd_d.rearrange("o (j p) -> p (o j)", p=128),
                            allow_slow_non_contiguous=True), waits=[("drc", m_rd1)], sem="drc")
                        m_rbcsb = ct.S(lambda s: s.copy(out=rbc_sb[:], in_=misc_ps[:]), waits=[("st", m_rbc)])
                        for mc in range(5):
                            dst = (qstage[:, TT * mc:TT * (mc + 1)] if mc < 4
                                   else k_raw[:, t0:t0 + TT])
                            ct.V(lambda v, dst=dst, mc=mc: v.tensor_tensor(
                                out=dst, in0=proj_ps[mc][:], in1=rbc_sb[:], op=ALU.mult),
                                waits=[("st", m_proj), ("ss", m_rbcsb)] +
                                      ([("dsp", prev_spill)] if mc == 0 else []))
                        m_qk = ct.C["sv"]
                        for ts4 in range(4):
                            c0 = 129 * (4 * i + ts4)
                            ct.V(lambda v, ts4=ts4, c0=c0: v.tensor_scalar(
                                out=v_tok[:, c0:c0 + 128], in0=v_ps[:, 128 * ts4:128 * (ts4 + 1)],
                                scalar1=rcol_sb[:, ts4:ts4 + 1], scalar2=None, op0=ALU.mult),
                                waits=[("st", m_vproj), ("drc", m_rcold), ("sv", m_vones)])
                        m_spill = ct.Dg(lambda e, t0=t0: e.dma_start(
                            out=qraw_d[:].rearrange("p (h n) -> p h n", h=4)[:, :, t0:t0 + TT],
                            in_=qstage[:].rearrange("p (h n) -> p h n", h=4)),
                            waits=[("sv", m_qk)], sem="dsp")
                        ch_end_st[(i, KCH - 1)] = ct.C["st"]
                        ch_end_ss[(i, KCH - 1)] = ct.C["ss"]
                        prev_ev = {"sv": ct.C["sv"], "ss": ct.C["ss"]}
                        prev_spill = m_spill
                        for _ in range(4):
                            emit_dq_tile()
                    m_pa_v = ct.C["sv"]
                    m_pa_dgp = ct.C["dsp"]

                    esPA.close()
                    cosg = [sb1(f"cosg{i}", [128, TT], F32) for i in range(2)]
                    sing = [sb1(f"sing{i}", [128, TT], F32) for i in range(2)]
                    qg = [sb1(f"qg{i}", [128, 4 * TT], BF16) for i in range(2)]
                    qsw = sb1("qsw", [128, 4 * TT], BF16)
                    ksw = sb1("ksw", [128, TT], BF16)
                    tS1 = [sb1(f"tS1_{i}", [128, 128], F32) for i in range(2)]
                    tS2 = [sb1(f"tS2_{i}", [128, 128], F32) for i in range(2)]
                    tS4 = [sb1(f"tS4_{i}", [128, 128], F32) for i in range(2)]
                    tV1 = [sb1(f"tV1_{i}", [128, 128], F32) for i in range(2)]
                    l1T_sb = [sb1(f"l1T_{i}", [128, 128], F32) for i in range(2)]
                    tp = [sb1(f"tp{i}", [128, TT], F32) for i in range(2)]
                    l1T_ps = [ps1(f"l1T_ps{i}", [128, 128], F32) for i in range(2)]
                    A_ps = [ps1(f"A_ps{i}", [128, 512], F32) for i in range(2)]

                    grp_pool = {}
                    preb_k2 = {}
                    preb_pe = {}
                    preb_ss = {}
                    for gq in range(NTT):
                        pg = gq % 2
                        g0 = TT * gq
                        dcs0 = ct.Ds(lambda e, pg=pg, g0=g0: e.dma_start(out=cosg[pg][:], in_=cosT[:, g0:g0 + TT]),
                                     waits=[("sp", grp_pool.get(gq - 2, 0))], sem="dcs")
                        dcs = ct.Ds(lambda e, pg=pg, g0=g0: e.dma_start(out=sing[pg][:], in_=sinT[:, g0:g0 + TT]),
                                    sem="dcs")
                        d_qg = ct.Dg(lambda e, pg=pg, g0=g0: e.dma_start(
                            out=qg[pg][:].rearrange("p (h n) -> p h n", h=4),
                            in_=qraw_d[:].rearrange("p (h n) -> p h n", h=4)[:, :, g0:g0 + TT]),
                            waits=[("sp", grp_pool.get(gq - 2, 0)),
                                   ("dqr", grp_pool.get(("d", gq - 2), 0)),
                                   ("dsp", m_pa_dgp)], sem="dqg")
                        d_sw1 = ct.Dg(lambda e, pg=pg: e.dma_start(out=qsw[0:64, :], in_=qg[pg][64:128, :]),
                                      waits=[("dqg", d_qg), ("sp", grp_pool.get(gq - 1, 0))], sem="dsw")
                        d_sw2 = ct.Dg(lambda e, pg=pg: e.dma_start(out=qsw[64:128, :], in_=qg[pg][0:64, :]), sem="dsw")
                        d_sw3 = ct.Dg(lambda e, g0=g0: e.dma_start(out=ksw[0:64, :], in_=k_raw[64:128, g0:g0 + TT]), sem="dsw")
                        d_sw4 = ct.Dg(lambda e, g0=g0: e.dma_start(out=ksw[64:128, :], in_=k_raw[0:64, g0:g0 + TT]), sem="dsw")
                        wro = [("dcs", dcs), ("dsw", d_sw4), ("dqg", d_qg), ("sv", m_pa_v)]
                        for h in range(4):
                            c0 = TT * h
                            for lo, hi in ((0, 64), (64, 128)):
                                sgn = ALU.subtract if lo == 0 else ALU.add
                                ct.G(lambda g, c0=c0, pg=pg, lo=lo, hi=hi: g.tensor_tensor(
                                    out=tp[0][lo:hi, :], in0=qg[pg][lo:hi, c0:c0 + TT],
                                    in1=cosg[pg][lo:hi, :], op=ALU.mult), waits=(wro if (h == 0 and lo == 0) else []))
                                ct.G(lambda g, c0=c0, pg=pg, lo=lo, hi=hi: g.tensor_tensor(
                                    out=tp[1][lo:hi, :], in0=qsw[lo:hi, c0:c0 + TT],
                                    in1=sing[pg][lo:hi, :], op=ALU.mult))
                                ct.G(lambda g, c0=c0, pg=pg, lo=lo, hi=hi, sgn=sgn: g.tensor_tensor(
                                    out=qg[pg][lo:hi, c0:c0 + TT], in0=tp[0][lo:hi, :],
                                    in1=tp[1][lo:hi, :], op=sgn))
                        m_qrope = ct.C["sp"]
                        ct.Dg(lambda e, pg=pg, g0=g0: e.dma_start(
                            out=qrope_d[:].rearrange("p (h n) -> p h n", h=4)[:, :, g0:g0 + TT],
                            in_=qg[pg][:].rearrange("p (h n) -> p h n", h=4)),
                            waits=[("sp", m_qrope)], sem="dqr")
                        for lo, hi in ((0, 64), (64, 128)):
                            sgn = ALU.subtract if lo == 0 else ALU.add
                            ct.G(lambda g, g0=g0, pg=pg, lo=lo, hi=hi: g.tensor_tensor(
                                out=tp[0][lo:hi, :], in0=k_raw[lo:hi, g0:g0 + TT],
                                in1=cosg[pg][lo:hi, :], op=ALU.mult))
                            ct.G(lambda g, g0=g0, pg=pg, lo=lo, hi=hi: g.tensor_tensor(
                                out=tp[1][lo:hi, :], in0=ksw[lo:hi, :],
                                in1=sing[pg][lo:hi, :], op=ALU.mult))
                            ct.G(lambda g, g0=g0, pg=pg, lo=lo, hi=hi, sgn=sgn: g.tensor_tensor(
                                out=k_rope[lo:hi, g0:g0 + TT], in0=tp[0][lo:hi, :],
                                in1=tp[1][lo:hi, :], op=sgn))
                        m_kr = ct.C["sp"]
                        grp_pool[gq] = m_kr
                        grp_pool[("d", gq)] = ct.C["dqr"]

                        for cc4 in range(4):
                            c = 4 * gq + cc4
                            c0 = CH * c
                            pc = c % 2
                            m_s1 = ct.S(lambda s, c0=c0, pc=pc: s.activation(
                                out=tS1[pc][:], in_=k_raw[:, c0:c0 + CH], func=AF.Exp, scale=-1.0),
                                waits=[("sv", preb_k2.get(c - 2, 0)), ("st", preb_pe.get(c - 2, 0))])
                            m_l1 = ct.S(lambda s, pc=pc: s.activation(out=tS2[pc][:], in_=tS1[pc][:],
                                                                      func=AF.Ln, bias=oneb[:, 0:1]))
                            m_tr = ct.T(lambda t, pc=pc: t.transpose(out=l1T_ps[pc][:], in_=tS2[pc][:],
                                                                     identity=idf_sb[:]),
                                        waits=[("ss", m_l1)], inc=1)
                            m_l1T = ct.S(lambda s, pc=pc: s.copy(out=l1T_sb[pc][:], in_=l1T_ps[pc][:]),
                                         waits=[("st", m_tr)])
                            m_s4 = ct.S(lambda s, c0=c0, pc=pc: s.activation(out=tS4[pc][:],
                                                                             in_=k_raw[:, c0:c0 + CH], func=AF.Exp))
                            m_A = ct.T(lambda t, pc=pc: t.matmul(out=A_ps[pc][:, 0:128], lhsT=l1T_sb[pc][:],
                                                                 rhs=nu16_sb[:], start=True, stop=True),
                                       waits=[("ss", m_l1T), ("dli", d_nu)])
                            m_smk = ct.T(lambda t, pc=pc: t.matmul(out=A_ps[pc][:, 480:481], lhsT=tS4[pc][:],
                                                                   rhs=ones_f[:], start=False, stop=True),
                                         waits=[("ss", m_s4)], inc=1)
                            ct.S(lambda s, c0=c0, pc=pc: s.activation(out=eA[:, c0:c0 + CH],
                                                                      in_=A_ps[pc][:, 0:128], func=AF.Exp),
                                 waits=[("st", m_smk)])
                            ct.S(lambda s, pc=pc: s.activation(out=tS1[pc][:], in_=A_ps[pc][:, 0:128],
                                                               func=AF.Exp, scale=-1.0))
                            m_eac = ct.S(lambda s, c=c, pc=pc: s.activation(out=eacs[:, c:c + 1],
                                                                            in_=A_ps[pc][:, 127:128], func=AF.Exp))
                            ct.V(lambda v, c=c, pc=pc: v.reciprocal(out=rkc[:, c:c + 1],
                                                                    in_=A_ps[pc][:, 480:481]),
                                 waits=[("st", m_smk)])
                            ct.V(lambda v, pc=pc: v.tensor_tensor(out=tV1[pc][:], in0=tS4[pc][:],
                                                                  in1=tS1[pc][:], op=ALU.mult),
                                 waits=[("ss", m_eac)])
                            ct.V(lambda v, c0=c0, pc=pc: v.tensor_copy(out=ktl[:, c0:c0 + CH], in_=tV1[pc][:]))
                            m_k2c = ct.V(lambda v, c0=c0, c=c, pc=pc: v.tensor_scalar(
                                out=k2[:, c0:c0 + CH], in0=tV1[pc][:], scalar1=eacs[:, c:c + 1],
                                scalar2=None, op0=ALU.mult))
                            preb_k2[c] = m_k2c
                            preb_pe[c] = ct.C["st"]
                            preb_ss[c] = ct.C["ss"]
                        emit_dq_tile()
                        emit_dq_tile()
                    m_preb_v = ct.C["sv"]
                    m_preb_p = ct.C["sp"]
                    m_preb_s = ct.C["ss"]
                    if part == "attn" and dbg:
                        wdbg = [("sv", ct.C["sv"]), ("ss", ct.C["ss"]), ("sp", ct.C["sp"]),
                                ("st", ct.C["st"]), ("dgp", ct.C["dgp"])]
                        ct.Do(lambda e: e.dma_start(out=dbg_t["d_kraw"], in_=k_raw[:]), waits=wdbg)
                        ct.Do(lambda e: e.dma_start(out=dbg_t["d_krope"], in_=k_rope[:]))
                        ct.Do(lambda e: e.dma_start(out=dbg_t["d_ktl"], in_=ktl[:]))
                        ct.Do(lambda e: e.dma_start(out=dbg_t["d_k2"], in_=k2[:]))
                        ct.Do(lambda e: e.dma_start(out=dbg_t["d_eA"], in_=eA[:]))
                        ct.Do(lambda e: e.dma_start(out=dbg_t["d_vtok"], in_=v_tok[:]))
                        ct.Do(lambda e: e.dma_start(out=dbg_t["d_rkc"], in_=rkc[:]))
                        ct.Do(lambda e: e.dma_start(out=dbg_t["d_eacs"], in_=eacs[:]))
                        ct.Do(lambda e: e.dma_start(out=dbg_t["d_rcol"], in_=rcol_sb[:]))
                        ct.Do(lambda e: e.dma_start(out=dbg_t["d_qraw"], in_=qraw_d[:]))
                        ct.Do(lambda e: e.dma_start(out=dbg_t["d_qrope"], in_=qrope_d[:]))

                esB = ExitStack()
                with esB:
                    sbB = lambda n, s, d: esB.enter_context(nc.sbuf_tensor(n, s, d))
                    psB = lambda n, s, d: esB.enter_context(nc.psum_tensor(n, s, d))
                    P_sb = [sbB(f"P{i}", [128, 512], BF16) for i in range(2)]
                    qro = [sbB(f"qro{i}", [128, 512], BF16) for i in range(2)]
                    Pg_sb = sbB("Pg", [128, 512], BF16)
                    qrawc = sbB("qrawc", [128, 512], BF16)
                    qe_f = sbB("qe_f", [128, 512], F32)
                    qhat = sbB("qhat", [128, 512], BF16)
                    k2t_sb = sbB("k2t", [128, 128], BF16)
                    v2t = sbB("v2t", [128, 128], BF16)
                    rq_sb = sbB("rq_sb", [128, 4], F32)
                    rd_sb = sbB("rd_sb", [128, 4], F32)
                    gla_f = sbB("gla_f", [128, 512], F32)
                    o_tok = sbB("o_tok", [128, 512], BF16)
                    ofeat_q = sbB("ofeat_q", [128, 4 * TQ], BF16)
                    wo_sb = sbB("wo_sb", [128, 4 * DM], BF16)
                    wop_sb = [sbB(f"wop{i}", [128, DM], F32) for i in range(2)]

                    ya = [[psB(f"ya{c}_{b}", [128, 512], F32) for b in range(2)] for c in range(2)]
                    gl_ps = [psB(f"gl{c}", [128, 512], F32) for c in range(2)]
                    sc_ps = [psB(f"sc{i}", [128, 512], F32) for i in range(2)]

                    for oc in range(4):
                        ct.Dg(lambda e, oc=oc: e.dma_start(
                            out=wo_sb[:, DM * oc:DM * (oc + 1)], in_=wo[128 * oc:128 * (oc + 1), :]),
                            sem="dwo")
                    d_wo = ct.C["dwo"]

                    rot = sc_ps
                    YOF = [0, 132]
                    uidx = 0
                    exp_marks = {}
                    bank_wait = {0: [], 1: []}
                    m_fin_s = m_preb_s
                    m_finv = {0: m_preb_v, 1: m_preb_v}
                    m_sbf = 0
                    m_sq_prev = 0
                    m_qS_prev = 0
                    m_kv_prev = 0
                    m_ot_prev = 0
                    m_oftr_prev = 0
                    m_qh_prev = 0
                    m_rsd = 0
                    tile_q_last = {}
                    pend = {}

                    def diag_block(tau):
                        nonlocal_vars = None
                        return

                    def emit_diag(tau, dmarks):
                        ctx = tau % 2
                        nonlocal_ = dmarks
                        m_y = dmarks["m_y"]
                        d_q = ct.Dg(lambda e, tau=tau: e.dma_start(
                            out=qrawc[:].rearrange("p (h n) -> p h n", h=4),
                            in_=qraw_d[:].rearrange("p (h n) -> p h n", h=4)[:, :, CH * tau:CH * (tau + 1)]),
                            waits=[("ss", dmarks["fin_s_prev"])], sem="dqc")
                        m_qe = ct.S(lambda s: s.activation(out=qe_f[:], in_=qrawc[:], func=AF.Exp),
                                    waits=[("dqc", d_q), ("sv", dmarks["qh_prev"]), ("st", dmarks["sq_prev"])])
                        for h in range(4):
                            ct.V(lambda v, h=h, tau=tau: v.scalar_tensor_tensor(
                                out=qhat[:, 128 * h:128 * (h + 1)], in0=qe_f[:, 128 * h:128 * (h + 1)],
                                scalar=0.5, in1=eA[:, CH * tau:CH * (tau + 1)],
                                op0=ALU.mult, op1=ALU.mult),
                                waits=([("ss", m_qe), ("st", dmarks["qS_prev"]),
                                        ("ss", preb_ss[tau]), ("sv", preb_k2[tau])] if h == 0 else []))
                        m_qh = ct.C["sv"]
                        m_gs = ct.T(lambda t, tau=tau, ctx=ctx: t.matmul(
                            out=gl_ps[ctx][:], lhsT=ktl[:, CH * tau:CH * (tau + 1)], rhs=qhat[:],
                            start=True, stop=True),
                            waits=[("sv", m_qh), ("st", m_y), ("ss", dmarks["fin_s_prev"])], inc=1)
                        for h in range(4):
                            ct.V(lambda v, h=h, tau=tau, ctx=ctx: v.scalar_tensor_tensor(
                                out=Pg_sb[:, 128 * h:128 * (h + 1)], in0=gl_ps[ctx][:, 128 * h:128 * (h + 1)],
                                scalar=rkc[:, tau:tau + 1], in1=mask_sb[:],
                                op0=ALU.mult, op1=ALU.mult),
                                waits=([("st", m_gs)] if h == 0 else []))
                        m_pg = ct.C["sv"]
                        for h in range(4):
                            ct.T(lambda t, h=h, tau=tau, ctx=ctx: t.matmul(
                                out=gl_ps[ctx][:, 128 * h:128 * (h + 1)], lhsT=Pg_sb[:, 128 * h:128 * (h + 1)],
                                rhs=v_tok[:, 129 * tau:129 * tau + 128],
                                start=(h == 0), stop=False), waits=([("sv", m_pg)] if h == 0 else []))
                        for h in range(4):
                            ct.T(lambda t, h=h, ctx=ctx: t.matmul(
                                out=gl_ps[ctx][:, 128 * h:128 * (h + 1)], lhsT=qhat[:, 128 * h:128 * (h + 1)],
                                rhs=Sbf[:], start=False, stop=(h == 3)),
                                waits=([("ss", dmarks["sbf_prev"])] if h == 0 else []), inc=(1 if h == 3 else 0))
                        m_qS = ct.C["st"]
                        for h in range(4):
                            ct.T(lambda t, h=h, ctx=ctx: t.matmul(
                                out=ya[ctx][0][:, 400 + h:401 + h], lhsT=qe_f[:, 128 * h:128 * (h + 1)],
                                rhs=ones_f[:], start=False, stop=(h == 3)),
                                waits=([("st", m_y)] if h == 0 else []), inc=(1 if h == 3 else 0))
                        m_sq = ct.C["st"]
                        m_k2t = ct.T(lambda t, tau=tau, ctx=ctx: t.transpose(
                            out=ya[ctx][0][:, 300:364].bitcast(BF16), in_=k2[:, CH * tau:CH * (tau + 1)],
                            identity=idb_sb[:]),
                            waits=[("dgp", d_idb), ("sv", preb_k2[tau])], inc=1)
                        m_k2s = ct.S(lambda s, ctx=ctx: s.copy(out=k2t_sb[:], in_=ya[ctx][0][:, 300:364].bitcast(BF16)),
                                     waits=[("st", m_k2t)])
                        m_v2 = ct.V(lambda v, tau=tau: v.tensor_scalar(
                            out=v2t[:], in0=v_tok[:, 129 * tau:129 * tau + 128],
                            scalar1=rkc[:, tau:tau + 1], scalar2=None, op0=ALU.mult),
                            waits=[("st", dmarks["kv_prev"])])
                        m_kv = ct.T(lambda t, ctx=ctx: t.matmul(
                            out=ya[ctx][1][:, 300:428], lhsT=k2t_sb[:], rhs=v2t[:],
                            start=False, stop=True), waits=[("ss", m_k2s), ("sv", m_v2)], inc=1)
                        m_su = ct.V(lambda v, tau=tau, ctx=ctx: v.scalar_tensor_tensor(
                            out=S_sb[:], in0=S_sb[:], scalar=eacs[:, tau:tau + 1], in1=ya[ctx][1][:, 300:428],
                            op0=ALU.mult, op1=ALU.add), waits=[("st", max(m_kv, m_qS))])
                        m_sbf_n = ct.S(lambda s: s.copy(out=Sbf[:], in_=S_sb[:]), waits=[("sv", m_su)])
                        m_rq = ct.V(lambda v, ctx=ctx: v.reciprocal(out=rq_sb[:], in_=ya[ctx][0][:, 400:404]),
                                    waits=[("st", m_sq)])
                        for h in range(4):
                            bnk, off = h // 2, YOF[h % 2]
                            ct.V(lambda v, h=h, ctx=ctx, bnk=bnk, off=off: v.reciprocal(
                                out=rd_sb[:, h:h + 1], in_=ya[ctx][bnk][:, off + 128:off + 129]),
                                waits=([("st", m_qS)] if h == 0 else []))
                        ct.V(lambda v: v.tensor_scalar(out=rd_sb[:], in0=rd_sb[:], scalar1=0.5,
                                                       scalar2=None, op0=ALU.mult))
                        m_glaev = ct.S(lambda s, ctx=ctx: s.copy(out=gla_f[:], in_=gl_ps[ctx][:]),
                                       waits=[("st", m_qS), ("sv", dmarks["ot_prev"])])
                        for h in range(4):
                            bnk, off = h // 2, YOF[h % 2]
                            ct.V(lambda v, h=h: v.tensor_scalar(
                                out=gla_f[:, 128 * h:128 * (h + 1)], in0=gla_f[:, 128 * h:128 * (h + 1)],
                                scalar1=rq_sb[:, h:h + 1], scalar2=None, op0=ALU.mult),
                                waits=([("ss", m_glaev), ("sv", m_rq)] if h == 0 else []))
                            ct.V(lambda v, h=h, ctx=ctx, bnk=bnk, off=off: v.scalar_tensor_tensor(
                                out=o_tok[:, 128 * h:128 * (h + 1)], in0=ya[ctx][bnk][:, off:off + 128],
                                scalar=rd_sb[:, h:h + 1], in1=gla_f[:, 128 * h:128 * (h + 1)],
                                op0=ALU.mult, op1=ALU.add),
                                waits=([("st", dmarks["oftr_prev"])] if h == 0 else []))
                        m_ot = ct.C["sv"]
                        tl = 128 * (tau % 16)
                        trdst = [(0, 436), (1, 436), (0, 300), (1, 300)]
                        for h in range(4):
                            bnk, off = trdst[h]
                            mtr = ct.T(lambda t, h=h, ctx=ctx, bnk=bnk, off=off: t.transpose(
                                out=ya[ctx][bnk][:, off:off + 64].bitcast(BF16),
                                in_=o_tok[:, 128 * h:128 * (h + 1)], identity=idb_sb[:]),
                                waits=([("sv", m_ot)] if h == 0 else []), inc=1)
                            ct.S(lambda s, h=h, tl=tl, ctx=ctx, bnk=bnk, off=off: s.copy(
                                out=ofeat_q[:, TQ * h + tl:TQ * h + tl + 128],
                                in_=ya[ctx][bnk][:, off:off + 64].bitcast(BF16)), waits=[("st", mtr)])
                        return {"m_sq": m_sq, "m_qS": m_qS, "m_kv": m_kv, "m_qh": m_qh,
                                "m_sbf": m_sbf_n, "m_ot": m_ot,
                                "m_oftr": ct.C["st"], "fin_s": ct.C["ss"], "fin_v": ct.C["sv"]}

                    def emit_wo(qtr):
                        nonlocal m_rsd
                        ctx = (16 * qtr + 15) % 2
                        wbanks = [ya[ctx][0], ya[ctx][1], gl_ps[ctx]]
                        em = {}
                        rsm = {}
                        widx = 0
                        for ts in range(16):
                            for kq in range(4):
                                bk = wbanks[widx % 3]
                                for oc in range(4):
                                    w0 = []
                                    if oc == 0:
                                        w0 = [("sv", em.get(widx - 3, 0))]
                                        if widx < 3:
                                            w0 += [("sv", m_finv[ctx]), ("ss", ct.C["ss"]), ("dwo", d_wo)]
                                    ct.T(lambda t, ts=ts, kq=kq, oc=oc, bk=bk: t.matmul(
                                        out=bk[:, 0:512],
                                        lhsT=ofeat_q[:, TQ * oc + 128 * ts:TQ * oc + 128 * (ts + 1)],
                                        rhs=wo_sb[:, DM * oc + 512 * kq:DM * oc + 512 * (kq + 1)],
                                        start=(oc == 0), stop=(oc == 3)),
                                        waits=w0, inc=(1 if oc == 3 else 0))
                                m_wp = ct.C["st"]
                                pwo = widx % 2
                                ct.V(lambda v, bk=bk, kq=kq, pwo=pwo: v.tensor_copy(
                                    out=wop_sb[pwo][:, 512 * kq:512 * (kq + 1)], in_=bk[:, 0:512]),
                                    waits=[("st", m_wp), ("dh1", rsm.get(widx - 2, 0))])
                                em[widx] = ct.C["sv"]
                                m_rsd = ct.Dg(lambda e, qtr=qtr, ts=ts, kq=kq, pwo=pwo: e.dma_start(
                                    out=rsin[qtr][128 * ts:128 * (ts + 1), 512 * kq:512 * (kq + 1)],
                                    in_=wop_sb[pwo][:, 512 * kq:512 * (kq + 1)]),
                                    waits=[("sv", em[widx])], sem="dh1")
                                rsm[widx] = m_rsd
                                widx += 1
                        m_finv[ctx] = ct.C["sv"]
                        ct.CC(lambda e, qtr=qtr: e.collective_compute(
                            "ReduceScatter", ALU.add, replica_groups=[[0, 1, 2, 3], [4, 5, 6, 7]],
                            ins=[rsin[qtr][:]], outs=[rsout[qtr][:]]),
                            waits=[("dh1", m_rsd)])

                    pending_diag = None
                    pend_y = None
                    for tau in range(NCH):
                        pt = tau % 2
                        ctx = tau % 2
                        d_qt = ct.Dg(lambda e, tau=tau, pt=pt: e.dma_start(
                            out=qro[pt][:].rearrange("p (h n) -> p h n", h=4),
                            in_=qrope_d[:].rearrange("p (h n) -> p h n", h=4)[:, :, CH * tau:CH * (tau + 1)]),
                            waits=[("st", tile_q_last.get(tau - 2, 0)),
                                   ("dqr", grp_pool[("d", tau // 4)])], sem=f"dql{pt}")
                        q3 = qro[pt][:].rearrange("p (h n) -> p h n", h=4)
                        for c in range(tau + 1):
                            r = uidx % 2
                            diag = (c == tau)
                            wsc = [("ss", exp_marks.get(uidx - 1, 0)), (f"dql{pt}", d_qt),
                                   ("sp", grp_pool[c // 4])] + bank_wait[r]
                            bank_wait[r] = []
                            m_sc = ct.T(lambda t, c=c, r=r, q3=q3: t.matmul(
                                out=rot[r][:], lhsT=k_rope[:, CH * c:CH * (c + 1)], rhs=q3,
                                start=True, stop=True), waits=wsc, inc=1)
                            m_exp = ct.S(lambda s, r=r: s.activation(
                                out=P_sb[r][:], in_=rot[r][:], func=AF.Exp, scale=RSQ),
                                waits=[("st", m_sc)], serial=False)
                            exp_marks[uidx] = m_exp
                            ywait = [("ss", m_exp)]
                            if diag:
                                for h in range(4):
                                    ct.V(lambda v, h=h, r=r: v.tensor_tensor(
                                        out=P_sb[r][:, 128 * h:128 * (h + 1)],
                                        in0=P_sb[r][:, 128 * h:128 * (h + 1)],
                                        in1=mask_sb[:], op=ALU.mult),
                                        waits=([("ss", m_exp), ("dli", d_mk)] if h == 0 else []))
                                ywait = [("sv", ct.C["sv"])]
                            if c == 0:
                                ywait.append(("sv", m_finv[ctx]))
                                ywait.append(("ss", m_fin_s))
                            if pend_y is not None:
                                _r, _c, _diag, _ctx, _yw = pend_y
                                for h in range(4):
                                    bnk, off = h // 2, YOF[h % 2]
                                    ct.T(lambda t, h=h, _r=_r, _c=_c, _diag=_diag, _ctx=_ctx, bnk=bnk, off=off: t.matmul(
                                        out=ya[_ctx][bnk][:, off:off + 129], lhsT=P_sb[_r][:, 128 * h:128 * (h + 1)],
                                        rhs=v_tok[:, 129 * _c:129 * _c + 129],
                                        start=(_c == 0 and h % 2 == 0), stop=_diag),
                                        waits=(_yw if h == 0 else []), inc=(1 if h == 3 else 0))
                            pend_y = (r, c, diag, ctx, ywait)
                            m_y = ct.C["st"]
                            uidx += 1
                            if c == 2 and pending_diag is not None:
                                ptau, pm = pending_diag
                                res = emit_diag(ptau, pm)
                                m_sq_prev, m_qS_prev, m_kv_prev = res["m_sq"], res["m_qS"], res["m_kv"]
                                m_qh_prev, m_sbf, m_ot_prev = res["m_qh"], res["m_sbf"], res["m_ot"]
                                m_oftr_prev, m_fin_s = res["m_oftr"], res["fin_s"]
                                m_finv[ptau % 2] = res["fin_v"]
                                pending_diag = None
                                if ptau % 16 == 15:
                                    if ptau == 15 and need_ffn:
                                        dq_finish_and_ag()
                                    emit_wo(ptau // 16)
                        if pend_y is not None:
                            _r, _c, _diag, _ctx, _yw = pend_y
                            for h in range(4):
                                bnk, off = h // 2, YOF[h % 2]
                                ct.T(lambda t, h=h, _r=_r, _c=_c, _diag=_diag, _ctx=_ctx, bnk=bnk, off=off: t.matmul(
                                    out=ya[_ctx][bnk][:, off:off + 129], lhsT=P_sb[_r][:, 128 * h:128 * (h + 1)],
                                    rhs=v_tok[:, 129 * _c:129 * _c + 129],
                                    start=(_c == 0 and h % 2 == 0), stop=_diag),
                                    waits=(_yw if h == 0 else []), inc=(1 if h == 3 else 0))
                            pend_y = None
                        m_y = ct.C["st"]
                        tile_q_last[tau] = ct.C["st"]
                        if pending_diag is not None:
                            ptau, pm = pending_diag
                            res = emit_diag(ptau, pm)
                            m_sq_prev, m_qS_prev, m_kv_prev = res["m_sq"], res["m_qS"], res["m_kv"]
                            m_qh_prev, m_sbf, m_ot_prev = res["m_qh"], res["m_sbf"], res["m_ot"]
                            m_oftr_prev, m_fin_s = res["m_oftr"], res["fin_s"]
                            m_finv[ptau % 2] = res["fin_v"]
                            if ptau % 16 == 15:
                                if ptau == 15 and need_ffn:
                                    dq_finish_and_ag()
                                emit_wo(ptau // 16)
                        pending_diag = (tau, {"m_y": m_y, "fin_s_prev": m_fin_s,
                                              "qh_prev": m_qh_prev, "sq_prev": m_sq_prev,
                                              "qS_prev": m_qS_prev, "kv_prev": m_kv_prev,
                                              "sbf_prev": m_sbf, "ot_prev": m_ot_prev,
                                              "oftr_prev": m_oftr_prev})
                        if tau < 12 and need_ffn:
                            for _ in range(6):
                                emit_dq_tile()
                    ptau, pm = pending_diag
                    res = emit_diag(ptau, pm)
                    m_fin_s = res["fin_s"]
                    m_finv[ptau % 2] = res["fin_v"]
                    emit_wo(3)
                    cc_rs_done = ct.C["cc"]

            if part == "attn":
                for i in range(4):
                    ct.Do(lambda e, i=i: e.dma_start(out=h1src[i], in_=rsout[i][:]),
                          waits=[("cc", cc_rs_done)])
                if dbg:
                    ct.Do(lambda e: e.dma_start(out=dbg_t["d_rsin"], in_=rsin[0][:]))
                ct.W("gpsimd", [("so", ct.C["so"])])

        if need_ffn:
            esC = ExitStack()
            with esC:
                sbC = lambda n, s, d: esC.enter_context(nc.sbuf_tensor(n, s, d))
                psC = lambda n, s, d: esC.enter_context(nc.psum_tensor(n, s, d))
                xT_f = sbC("xT_f", [128, KCH * TQ], BF16)
                esC1 = ExitStack()
                with esC1:
                    sbC1 = lambda n, s, d: esC1.enter_context(nc.sbuf_tensor(n, s, d))
                    att_f = sbC1("att_f", [128, DM], F32)
                    atsq = sbC1("atsq", [128, DM], F32)
                    x0t = sbC1("x0t", [128, DM], F32)
                    h1t = sbC1("h1t", [128, DM], F32)
                    fi_t = sbC1("fi_t", [128, DM], BF16)
                    mu_c = sbC1("mu_c", [128, 1], F32)
                    m2_c = sbC1("m2_c", [128, 1], F32)
                    r_c = sbC1("r_c", [128, 1], F32)
                    psC1 = lambda n, s, d: esC1.enter_context(nc.psum_tensor(n, s, d))
                    tr2_ps = [psC1(f"tr2_{i}", [128, 128], BF16) for i in range(2)]

                    wcc = [("cc", cc_rs_done)] if need_attn else []
                    for ts in range(16):
                        qtr, j = ts // 4, ts % 4
                        d_rs = ct.Ds(lambda e, qtr=qtr, j=j: e.dma_start(
                            out=att_f[:], in_=rsout[qtr][128 * j:128 * (j + 1), :]),
                            waits=wcc + [("st", ct.C["st"]), ("ss", ct.C["ss"]), ("sv", ct.C["sv"])],
                            sem="dpc")
                        d_x0 = ct.Ds(lambda e, ts=ts: e.dma_start(
                            out=x0t[:], in_=x0q[128 * ts:128 * (ts + 1), :]), sem="dpc")
                        ct.V(lambda v: v.tensor_reduce(out=mu_c[:], in_=att_f[:], axis=mybir.AxisListType.X,
                                                       op=ALU.add), waits=[("dpc", d_x0)])
                        ct.V(lambda v: v.tensor_scalar(out=mu_c[:], in0=mu_c[:], scalar1=1.0 / DM,
                                                       scalar2=None, op0=ALU.mult))
                        m_sq1 = ct.S(lambda s: s.activation(out=atsq[:], in_=att_f[:], func=AF.Square),
                                     waits=[("dpc", d_x0)])
                        ct.V(lambda v: v.tensor_reduce(out=m2_c[:], in_=atsq[:], axis=mybir.AxisListType.X,
                                                       op=ALU.add), waits=[("ss", m_sq1)])
                        ct.V(lambda v: v.tensor_scalar(out=m2_c[:], in0=m2_c[:], scalar1=1.0 / DM,
                                                       scalar2=None, op0=ALU.mult))
                        ct.V(lambda v: v.tensor_tensor(out=r_c[:], in0=mu_c[:], in1=mu_c[:], op=ALU.mult))
                        m_v1 = ct.V(lambda v: v.tensor_tensor(out=r_c[:], in0=m2_c[:], in1=r_c[:], op=ALU.subtract))
                        ct.S(lambda s: s.activation(out=r_c[:], in_=r_c[:], func=AF.Ln, bias=epsb[:, 0:1]),
                             waits=[("sv", m_v1)])
                        m_r1 = ct.S(lambda s: s.activation(out=r_c[:], in_=r_c[:], func=AF.Exp, scale=-0.5))
                        ct.V(lambda v: v.tensor_scalar(out=att_f[:], in0=att_f[:], scalar1=mu_c[:, 0:1],
                                                       scalar2=r_c[:, 0:1], op0=ALU.subtract, op1=ALU.mult),
                             waits=[("ss", m_r1)])
                        m_h1 = ct.V(lambda v: v.tensor_tensor(out=h1t[:], in0=att_f[:], in1=x0t[:], op=ALU.add),
                                    waits=[("dh1", ct.C["dh1"])])
                        ct.Dg(lambda e, ts=ts: e.dma_start(out=h1_d[128 * ts:128 * (ts + 1), :], in_=h1t[:]),
                              waits=[("sv", m_h1)], sem="dh1")
                        ct.V(lambda v: v.tensor_reduce(out=mu_c[:], in_=h1t[:], axis=mybir.AxisListType.X,
                                                       op=ALU.add))
                        ct.V(lambda v: v.tensor_scalar(out=mu_c[:], in0=mu_c[:], scalar1=1.0 / DM,
                                                       scalar2=None, op0=ALU.mult))
                        m_sq2 = ct.S(lambda s: s.activation(out=atsq[:], in_=h1t[:], func=AF.Square),
                                     waits=[("sv", m_h1)])
                        ct.V(lambda v: v.tensor_reduce(out=m2_c[:], in_=atsq[:], axis=mybir.AxisListType.X,
                                                       op=ALU.add), waits=[("ss", m_sq2)])
                        ct.V(lambda v: v.tensor_scalar(out=m2_c[:], in0=m2_c[:], scalar1=1.0 / DM,
                                                       scalar2=None, op0=ALU.mult))
                        ct.V(lambda v: v.tensor_tensor(out=r_c[:], in0=mu_c[:], in1=mu_c[:], op=ALU.mult))
                        m_v2 = ct.V(lambda v: v.tensor_tensor(out=r_c[:], in0=m2_c[:], in1=r_c[:], op=ALU.subtract))
                        ct.S(lambda s: s.activation(out=r_c[:], in_=r_c[:], func=AF.Ln, bias=epsb[:, 0:1]),
                             waits=[("sv", m_v2)])
                        m_r2 = ct.S(lambda s: s.activation(out=r_c[:], in_=r_c[:], func=AF.Exp, scale=-0.5))
                        m_fi = ct.V(lambda v: v.tensor_scalar(out=fi_t[:], in0=h1t[:], scalar1=mu_c[:, 0:1],
                                                              scalar2=r_c[:, 0:1], op0=ALU.subtract, op1=ALU.mult),
                                    waits=[("ss", m_r2)])
                        for kc in range(KCH):
                            pk = kc % 2
                            mtr = ct.T(lambda t, kc=kc, pk=pk: t.transpose(
                                out=tr2_ps[pk][:], in_=fi_t[:, 128 * kc:128 * (kc + 1)], identity=idb_sb[:]),
                                waits=([("sv", m_fi), ("ss", ct.C["ss"])] if kc < 2 else [("ss", ct.C["ss"])]),
                                inc=1)
                            ct.S(lambda s, kc=kc, ts=ts, pk=pk: s.copy(
                                out=xT_f[:, TQ * kc + 128 * ts:TQ * kc + 128 * (ts + 1)], in_=tr2_ps[pk][:]),
                                waits=[("st", mtr)])
                    m_xtf = ct.C["ss"]

                if not need_attn:
                    dq_finish_and_ag()
                esF1 = ExitStack()
                with esF1:
                    sbF = lambda n, s, d: esF1.enter_context(nc.sbuf_tensor(n, s, d))
                    psF = lambda n, s, d: esF1.enter_context(nc.psum_tensor(n, s, d))
                    wgbuf = [sbF(f"wg{i}", [128, KCH * 256], BF16) for i in range(2)]
                    wubuf = [sbF(f"wu{i}", [128, KCH * 256], BF16) for i in range(2)]
                    gt1 = sbF("gt1", [128, 1024], F32)
                    gt2 = sbF("gt2", [128, 1024], F32)
                    inter_sb = [sbF(f"inter{i}", [128, 1024], BF16) for i in range(2)]
                    gps = [psF(f"gps{i}", [128, 1024], F32) for i in range(2)]
                    ups = [psF(f"ups{i}", [128, 1024], F32) for i in range(2)]

                    def _sd(slot):
                        cc_, m = slot // 16, slot % 16
                        return 8 * cc_ + m if m < 8 else 64 + 8 * cc_ + (m - 8)

                    wbuf_mark = {}
                    ss_idx = 0
                    inter_dma = {}
                    for w in range(64):
                        pw = w % 2
                        s0 = 2 * w
                        np_off = 128 * _sd(s0)
                        wld = [("st", wbuf_mark.get(w - 2, 0)),
                               ("cc", dqst["cc"] if w == 0 else 0)]
                        for piece in range(4):
                            ct.Dg(lambda e, pw=pw, np_off=np_off, piece=piece: e.dma_start(
                                out=wgbuf[pw][:, 1024 * piece:1024 * (piece + 1)],
                                in_=gate_wt.rearrange("(c p) n -> p c n", p=128)[:, 4 * piece:4 * (piece + 1), np_off:np_off + 256]),
                                waits=(wld if piece == 0 else []), sem=f"dwd{pw}")
                            ct.Dg(lambda e, pw=pw, np_off=np_off, piece=piece: e.dma_start(
                                out=wubuf[pw][:, 1024 * piece:1024 * (piece + 1)],
                                in_=up_wt.rearrange("(c p) n -> p c n", p=128)[:, 4 * piece:4 * (piece + 1), np_off:np_off + 256]),
                                sem=f"dwd{pw}")
                        d_wg = ct.C[f"dwd{pw}"]
                        for sl in range(2):
                            slot = s0 + sl
                            for th in range(2):
                                pss = ss_idx % 2
                                ss_idx += 1
                                wmm = [(f"dwd{pw}", d_wg), ("ss", m_xtf),
                                       ("sv", inter_dma.get(("cons", ss_idx - 2 - 1), 0))]
                                for kc in range(KCH):
                                    for tq_ in range(2):
                                        kp = (kc // 2) if kc % 2 == 0 else 8 + kc // 2
                                        ct.T(lambda t, kc=kc, kp=kp, tq_=tq_, th=th, sl=sl, pw=pw, pss=pss: t.matmul(
                                            out=gps[pss][:, 512 * tq_:512 * (tq_ + 1)],
                                            lhsT=wgbuf[pw][:, 256 * kc + 128 * sl:256 * kc + 128 * (sl + 1)],
                                            rhs=xT_f[:, TQ * kp + 1024 * th + 512 * tq_:TQ * kp + 1024 * th + 512 * (tq_ + 1)],
                                            start=(kc == 0), stop=(kc == KCH - 1)),
                                            waits=(wmm if (kc == 0 and tq_ == 0) else []))
                                for kc in range(KCH):
                                    for tq_ in range(2):
                                        kp = (kc // 2) if kc % 2 == 0 else 8 + kc // 2
                                        ct.T(lambda t, kc=kc, kp=kp, tq_=tq_, th=th, sl=sl, pw=pw, pss=pss: t.matmul(
                                            out=ups[pss][:, 512 * tq_:512 * (tq_ + 1)],
                                            lhsT=wubuf[pw][:, 256 * kc + 128 * sl:256 * kc + 128 * (sl + 1)],
                                            rhs=xT_f[:, TQ * kp + 1024 * th + 512 * tq_:TQ * kp + 1024 * th + 512 * (tq_ + 1)],
                                            start=(kc == 0), stop=(kc == KCH - 1)),
                                            inc=(1 if (kc == KCH - 1 and tq_ == 1) else 0))
                                m_gu = ct.C["st"]
                                wbuf_mark[w] = m_gu
                                m_g2 = ct.S(lambda s, pss=pss: s.activation(out=gt1[:], in_=gps[pss][:],
                                                                            func=AF.Square),
                                            waits=[("st", m_gu), ("sv", ct.C["sv"])])
                                ct.V(lambda v: v.tensor_scalar(out=gt1[:], in0=gt1[:], scalar1=C2, scalar2=C1,
                                                               op0=ALU.mult, op1=ALU.add), waits=[("ss", m_g2)])
                                m_t3 = ct.V(lambda v, pss=pss: v.tensor_tensor(out=gt2[:], in0=gt1[:],
                                                                               in1=gps[pss][:], op=ALU.mult))
                                m_t4 = ct.S(lambda s: s.activation(out=gt2[:], in_=gt2[:], func=AF.Tanh),
                                            waits=[("sv", m_t3)])
                                ct.V(lambda v, pss=pss: v.scalar_tensor_tensor(
                                    out=gt1[:], in0=gt2[:], scalar=1.0, in1=gps[pss][:],
                                    op0=ALU.add, op1=ALU.mult), waits=[("ss", m_t4)])
                                pi = ss_idx % 2
                                m_int = ct.V(lambda v, pss=pss, pi=pi: v.scalar_tensor_tensor(
                                    out=inter_sb[pi][:], in0=gt1[:], scalar=0.5, in1=ups[pss][:],
                                    op0=ALU.mult, op1=ALU.mult),
                                    waits=[("dnt", inter_dma.get(("dma", ss_idx - 3), 0))])
                                inter_dma[("cons", ss_idx - 1)] = m_int
                                md = ct.Dg(lambda e, slot=slot, th=th, pi=pi: e.dma_start(
                                    out=inter_d[128 * slot:128 * (slot + 1), 1024 * th:1024 * (th + 1)],
                                    in_=inter_sb[pi][:]), waits=[("sv", m_int)], sem="dnt")
                                inter_dma[("dma", ss_idx - 1)] = md
                    m_f1_dma = ct.C["dnt"]

                esF2 = ExitStack()
                with esF2:
                    sbF2 = lambda n, s, d: esF2.enter_context(nc.sbuf_tensor(n, s, d))
                    psF2 = lambda n, s, d: esF2.enter_context(nc.psum_tensor(n, s, d))
                    ints_sb = sbF2("ints", [128, 128 * 256], BF16)
                    wdbuf = [sbF2(f"wd{i}", [128, DM], BF16) for i in range(4)]
                    dn_f = sbF2("dn_f", [128, DM], F32)
                    dsq = sbF2("dsq", [128, DM], F32)
                    h1b = sbF2("h1b", [128, DM], F32)
                    mu3 = sbF2("mu3", [128, 1], F32)
                    m23 = sbF2("m23", [128, 1], F32)
                    r3 = sbF2("r3", [128, 1], F32)
                    d_ps = [psF2(f"dps{i}", [128, DM], F32) for i in range(2)]

                    wd_mark = {}
                    m_evac = {}
                    for pp in range(8):
                        tc0 = 256 * pp
                        wint = [("dnt", m_f1_dma), ("st", m_evac.get(pp - 1, 0)),
                                ("sv", m_evac.get(("v", pp - 1), 0))]
                        dints = 0
                        for grp in range(8):
                            dints = ct.Dg(lambda e, grp=grp, tc0=tc0: e.dma_start(
                                out=ints_sb[:, 256 * 16 * grp:256 * 16 * (grp + 1)],
                                in_=inter_d.rearrange("(sl p) t -> p sl t", p=128)[:, 16 * grp:16 * (grp + 1), tc0:tc0 + 256]),
                                waits=(wint if grp == 0 else []), sem="dnl")
                        for slot in range(128):
                            pws = slot % 4
                            d_wd = ct.Dg(lambda e, slot=slot, pws=pws: e.dma_start(
                                out=wdbuf[pws][:], in_=down_wt[128 * slot:128 * (slot + 1), :]),
                                waits=[("st", wd_mark.get((pp, slot - 4), wd_mark.get((pp - 1, 124 + slot), 0)))],
                                sem=f"dwd{pws}")
                            for tsb in range(2):
                                for hq in range(4):
                                    ct.T(lambda t, slot=slot, tsb=tsb, hq=hq, pws=pws: t.matmul(
                                        out=d_ps[tsb][:, 512 * hq:512 * (hq + 1)],
                                        lhsT=ints_sb[:, 256 * slot + 128 * tsb:256 * slot + 128 * (tsb + 1)],
                                        rhs=wdbuf[pws][:, 512 * hq:512 * (hq + 1)],
                                        start=(slot == 0), stop=(slot == 127)),
                                        waits=([(f"dwd{pws}", d_wd), ("dnl", dints)] if (tsb == 0 and hq == 0) else []),
                                        inc=(1 if (tsb == 1 and hq == 3) else 0))
                            wd_mark[(pp, slot)] = ct.C["st"]
                        m_dn = ct.C["st"]
                        for tsb in range(2):
                            ts_g = 2 * pp + tsb
                            d_h1 = ct.Ds(lambda e, ts_g=ts_g: e.dma_start(
                                out=h1b[:], in_=h1_d[128 * ts_g:128 * (ts_g + 1), :]),
                                waits=[("so", ct.C["so"]), ("dh1", ct.C["dh1"])], sem="dhl")
                            m_cp = ct.V(lambda v, tsb=tsb: v.tensor_copy(out=dn_f[:], in_=d_ps[tsb][:]),
                                        waits=[("st", m_dn), ("so", ct.C["so"])])
                            if part == "ffn" and dbg:
                                m_dbgd = ct.Dg(lambda e, ts_g=ts_g: e.dma_start(
                                    out=dbg_f["f_down"][128 * ts_g:128 * (ts_g + 1), :], in_=dn_f[:]),
                                    waits=[("sv", m_cp)])
                                ct.W("vector", [("dgp", m_dbgd)])
                            ct.V(lambda v: v.tensor_reduce(out=mu3[:], in_=dn_f[:], axis=mybir.AxisListType.X,
                                                           op=ALU.add))
                            ct.V(lambda v: v.tensor_scalar(out=mu3[:], in0=mu3[:], scalar1=1.0 / DM,
                                                           scalar2=None, op0=ALU.mult))
                            m_sq3 = ct.S(lambda s: s.activation(out=dsq[:], in_=dn_f[:], func=AF.Square),
                                         waits=[("sv", m_cp)])
                            ct.V(lambda v: v.tensor_reduce(out=m23[:], in_=dsq[:], axis=mybir.AxisListType.X,
                                                           op=ALU.add), waits=[("ss", m_sq3)])
                            ct.V(lambda v: v.tensor_scalar(out=m23[:], in0=m23[:], scalar1=1.0 / DM,
                                                           scalar2=None, op0=ALU.mult))
                            ct.V(lambda v: v.tensor_tensor(out=r3[:], in0=mu3[:], in1=mu3[:], op=ALU.mult))
                            m_v3 = ct.V(lambda v: v.tensor_tensor(out=r3[:], in0=m23[:], in1=r3[:],
                                                                  op=ALU.subtract))
                            ct.S(lambda s: s.activation(out=r3[:], in_=r3[:], func=AF.Ln, bias=epsb[:, 0:1]),
                                 waits=[("sv", m_v3)])
                            m_r3 = ct.S(lambda s: s.activation(out=r3[:], in_=r3[:], func=AF.Exp, scale=-0.5))
                            ct.V(lambda v: v.tensor_scalar(out=dn_f[:], in0=dn_f[:], scalar1=mu3[:, 0:1],
                                                           scalar2=r3[:, 0:1], op0=ALU.subtract, op1=ALU.mult),
                                 waits=[("ss", m_r3)])
                            m_out = ct.V(lambda v: v.tensor_tensor(out=dn_f[:], in0=dn_f[:], in1=h1b[:],
                                                                   op=ALU.add), waits=[("dhl", d_h1)])
                            ct.Do(lambda e, ts_g=ts_g: e.dma_start(
                                out=out_d[128 * ts_g:128 * (ts_g + 1), :], in_=dn_f[:]),
                                waits=[("sv", m_out)])
                        m_evac[pp] = ct.C["st"]
                        m_evac[("v", pp)] = ct.C["sv"]
                    if part == "ffn" and dbg:
                        ct.Do(lambda e: e.dma_start(out=dbg_f["f_h1"], in_=h1_d[:]))
                        ct.Do(lambda e: e.dma_start(out=dbg_f["f_inter"], in_=inter_d[:]))
                        ct.Do(lambda e: e.dma_start(out=dbg_f["f_gw"], in_=gate_wt[:]))
                        ct.Do(lambda e: e.dma_start(out=dbg_f["f_dw"], in_=down_wt[:]))
                    ct.W("gpsimd", [("so", ct.C["so"])])

        with nc.Block() as block:
            for eng in ("sync", "gpsimd", "vector", "scalar", "tensor"):
                ops = ct.ops[eng]
                if ops:
                    getattr(block, eng)(lambda e, ops=ops: [f(e) for f in ops])
    return nc


FEAT = np.concatenate([np.arange(0, 2048, 2), np.arange(1, 2048, 2)])
NORD = np.concatenate([np.arange(0, 16384, 2), np.arange(1, 16384, 2)])
INVFEAT = np.argsort(FEAT)


def _tok_slice(g):
    return np.concatenate([np.arange(2048 * q + 512 * g, 2048 * q + 512 * g + 512)
                           for q in range(4)])


def _host_inputs(inputs):
    x = inputs["L0_attn_hidden_states"]
    Wq = inputs["L0_attn_Wq"]; Wk = inputs["L0_attn_Wk"]
    Wv = inputs["L0_attn_Wv"]; Wo = inputs["L0_attn_Wo"]

    inv_freq = 1.0 / (10000.0 ** (np.arange(0, 128, 2, dtype=np.float64) / 128.0))
    freqs = np.arange(NSEQ, dtype=np.float64)[:, None] * inv_freq[None, :]
    cosT = np.ascontiguousarray(np.tile(np.cos(freqs).T.astype(np.float32), (2, 1)))
    sinT = np.ascontiguousarray(np.tile(np.sin(freqs).T.astype(np.float32), (2, 1)))
    idm = np.eye(128, dtype=np.float32)
    sgrid = np.arange(128)[:, None]
    tgrid = np.arange(128)[None, :]
    masku = (sgrid <= tgrid).astype(np.float32)
    negu16 = masku * np.float32(-1.0 / 16.0)

    def packed_T(W_packed):
        b = W_packed.view('<u1')[:, ::4]
        return np.ascontiguousarray(b.T)
    gb = packed_T(inputs["L0_ffn_gate_W_packed"])[:, NORD]
    ub = packed_T(inputs["L0_ffn_up_W_packed"])[:, NORD]
    db = packed_T(inputs["L0_ffn_down_W_packed"])[:, FEAT]
    gsT = np.repeat(inputs["L0_ffn_gate_scales"].T, 16, axis=0)[:, NORD]
    usT = np.repeat(inputs["L0_ffn_up_scales"].T, 16, axis=0)[:, NORD]
    dsT = np.repeat(inputs["L0_ffn_down_scales"].T, 16, axis=0)[:, FEAT]

    maps = []
    for c in range(8):
        beta, g = c // 4, c % 4
        toks = _tok_slice(g)
        xb = x[beta]
        m = {
            "xT": np.ascontiguousarray(xb.T[FEAT]),
            "x0q": np.ascontiguousarray(xb[toks][:, FEAT]),
            "wq": np.ascontiguousarray(Wq[FEAT, 512 * g:512 * (g + 1)]),
            "wk": np.ascontiguousarray(Wk[FEAT, 128 * g:128 * (g + 1)]),
            "wv": np.ascontiguousarray(Wv[FEAT, 128 * g:128 * (g + 1)]),
            "wo": np.ascontiguousarray(Wo[512 * g:512 * (g + 1)][:, FEAT]),
            "cosT": cosT, "sinT": sinT,
            "idm": idm, "masku": masku, "negu16": negu16,
            "gbytes": np.ascontiguousarray(gb[128 * c:128 * (c + 1)]),
            "ubytes": np.ascontiguousarray(ub[128 * c:128 * (c + 1)]),
            "dbytes": np.ascontiguousarray(db[1024 * c:1024 * (c + 1)]),
            "gscale": np.ascontiguousarray(gsT[128 * c:128 * (c + 1)]),
            "uscale": np.ascontiguousarray(usT[128 * c:128 * (c + 1)]),
            "dscale": np.ascontiguousarray(dsT[1024 * c:1024 * (c + 1)]),
        }
        maps.append(m)
    return maps


def kernel(**inputs):
    maps = _host_inputs(inputs)
    nc = build_program("all")
    res = run_bass_kernel_spmd(nc, maps, core_ids=list(range(8))).results
    out = np.zeros((2, NSEQ, DM), dtype=np.float32)
    for c in range(8):
        beta, g = c // 4, c % 4
        toks = _tok_slice(g)
        out[beta, toks] = res[c]["out"][:, INVFEAT]
    return out


# version 07 (final)
# speedup vs baseline: 1.1144x; 1.0856x over previous
import numpy as np
import ml_dtypes
from contextlib import ExitStack
import concourse.bass as bass
import concourse.mybir as mybir
from concourse.bass_utils import run_bass_kernel_spmd

F32 = mybir.dt.float32
BF16 = mybir.dt.bfloat16
U8 = mybir.dt.uint8
I32 = mybir.dt.int32
FP8 = mybir.dt.float8e4
AF = mybir.ActivationFunctionType
ALU = mybir.AluOpType

NSEQ = 8192
DM = 2048
CH = 128
NCH = NSEQ // CH
TT = 512
NTT = NSEQ // TT
MLP = 16384
KCH = DM // 128
TQ = 2048
RSQ = float(128) ** -0.5
C1 = 0.7978845608028654
C2 = 0.7978845608028654 * 0.044715
DQW = 256
N_DQ = (2 * MLP // DQW) + (8 * DM // DQW)

TWO_LAUNCH = False


class Ctx:
    def __init__(self, nc):
        self.nc = nc
        self.ops = {k: [] for k in ("sync", "gpsimd", "vector", "scalar", "tensor")}
        self.C = {k: 0 for k in ('dli', 'dgp', 'sv', 'ss', 'st', 'sp', 'cc', 'so', 'dwq', 'dsp', 'dqi', 'dsc', 'dqo', 'dqg', 'dsw', 'dqr', 'dql0', 'dql1', 'dqc', 'drc', 'dxa0', 'dxa1', 'dcs', 'dnt', 'dnl', 'dwd0', 'dwd1', 'dwd2', 'dwd3', 'dh1', 'dpc', 'dhl', 'dwo')}
        self.SEM = {}

    def add(self, eng, fn):
        self.ops[eng].append(fn)

    def _mk(self, eng, build, waits, sem, inc):
        waits = [(s, v) for (s, v) in waits if v > 0]
        self.C[sem] += inc
        SEM = self.SEM

        def f(e, _b=build, _w=tuple(waits), _s=sem, _i=inc):
            for s, v in _w:
                e.wait_ge(SEM[s], v)
            ins = _b(e)
            if _i:
                ins.then_inc(SEM[_s], _i)
        self.add(eng, f)
        return self.C[sem]

    def V(self, build, waits=(), serial=True):
        w = list(waits)
        if serial:
            w.append(("sv", self.C["sv"]))
        return self._mk("vector", build, w, "sv", 1)

    def S(self, build, waits=(), serial=True):
        w = list(waits)
        if serial:
            w.append(("ss", self.C["ss"]))
        return self._mk("scalar", build, w, "ss", 1)

    def G(self, build, waits=(), serial=True):
        w = list(waits)
        if serial:
            w.append(("sp", self.C["sp"]))
        return self._mk("gpsimd", build, w, "sp", 1)

    def T(self, build, waits=(), inc=0):
        return self._mk("tensor", build, waits, "st", inc)

    def Dg(self, build, waits=(), sem="dgp"):
        return self._mk("gpsimd", build, waits, sem, 16)

    def Ds(self, build, waits=(), sem="dli"):
        return self._mk("sync", build, waits, sem, 16)

    def Do(self, build, waits=()):
        return self._mk("gpsimd", build, waits, "so", 16)

    def CC(self, build, waits=()):
        return self._mk("gpsimd", build, waits, "cc", 1)

    def W(self, eng, waits):
        waits = [(s, v) for (s, v) in waits if v > 0]
        SEM = self.SEM

        def f(e, _w=tuple(waits)):
            for s, v in _w:
                e.wait_ge(SEM[s], v)
        self.add(eng, f)


def build_program(part="all", dbg=False):
    nc = bass.Bass("TRN2", target_bir_lowering=False)
    ct = Ctx(nc)
    need_attn = part in ("all", "attn")
    need_ffn = part in ("all", "ffn")

    din = lambda n, s, d: nc.dram_tensor(n, s, d, kind="ExternalInput").ap()
    dint = lambda n, s, d, sp="Local": nc.dram_tensor(n, s, d, addr_space=sp).ap()

    if need_attn:
        xT = din("xT", [DM, NSEQ], F32)
        wq = din("wq", [DM, 512], F32)
        wk = din("wk", [DM, 128], F32)
        wv = din("wv", [DM, 128], F32)
        wo = din("wo", [512, DM], F32)
        cosT = din("cosT", [128, NSEQ], F32)
        sinT = din("sinT", [128, NSEQ], F32)
    if need_ffn:
        x0q = din("x0q", [TQ, DM], F32)
        gbytes = din("gbytes", [128, MLP], U8)
        ubytes = din("ubytes", [128, MLP], U8)
        dbytes = din("dbytes", [1024, DM], U8)
        gscale = din("gscale", [128, MLP], F32)
        uscale = din("uscale", [128, MLP], F32)
        dscale = din("dscale", [1024, DM], F32)
    idm = din("idm", [128, 128], F32)
    masku = din("masku", [128, 128], F32)
    negu16 = din("negu16", [128, 128], F32)

    if need_attn:
        qraw_d = dint("qraw_d", [128, 4 * NSEQ], BF16)
        qrope_d = dint("qrope_d", [128, 4 * NSEQ], BF16)
        rstd_d = dint("rstd_d", [1, TT], F32)
        rsin = [dint(f"rsin{i}", [TQ, DM], F32) for i in range(4)]
        rsout = [dint(f"rsout{i}", [512, DM], F32) for i in range(4)]
    if part == "attn":
        h1src = [nc.dram_tensor(f"ho{i}", [512, DM], F32, kind="ExternalOutput").ap()
                 for i in range(4)]
        if dbg:
            dbg_t = {}
            for nm, shp, dt_ in (("d_qraw", [128, 4 * NSEQ], BF16), ("d_qrope", [128, 4 * NSEQ], BF16),
                                 ("d_kraw", [128, NSEQ], BF16), ("d_krope", [128, NSEQ], BF16),
                                 ("d_ktl", [128, NSEQ], BF16), ("d_k2", [128, NSEQ], BF16),
                                 ("d_eA", [128, NSEQ], BF16), ("d_vtok", [128, NCH * 129], BF16),
                                 ("d_rkc", [128, NCH], F32), ("d_eacs", [128, NCH], F32),
                                 ("d_rsin", [TQ, DM], F32), ("d_ofeat", [128, 4 * TQ], BF16),
                                 ("d_rcol", [128, 4], F32)):
                dbg_t[nm] = nc.dram_tensor(nm, shp, dt_, kind="ExternalOutput").ap()
    if need_ffn:
        out_d = nc.dram_tensor("out", [TQ, DM], F32, kind="ExternalOutput").ap()
        agg_in = dint("agg_in", [256, MLP], BF16)
        agu_in = dint("agu_in", [256, MLP], BF16)
        agd_in = dint("agd_in", [2048, DM], BF16)
        gate_wt = dint("gate_wt", [DM, MLP], BF16, "Shared")
        up_wt = dint("up_wt", [DM, MLP], BF16, "Shared")
        down_wt = dint("down_wt", [MLP, DM], BF16, "Shared")
        inter_d = dint("inter_d", [MLP, TQ], BF16)
        h1_d = dint("h1_d", [TQ, DM], F32)
    if part == "ffn":
        rsout = [din(f"rsi{i}", [512, DM], F32) for i in range(4)]
        if dbg:
            dbg_f = {}
            for nm, shp, dt_ in (("f_h1", [TQ, DM], F32), ("f_inter", [MLP, TQ], BF16),
                                 ("f_gw", [DM, MLP], BF16), ("f_dw", [MLP, DM], BF16),
                                 ("f_down", [TQ, DM], F32)):
                dbg_f[nm] = nc.dram_tensor(nm, shp, dt_, kind="ExternalOutput").ap()

    es = ExitStack()
    with es:
        sb = lambda n, s, d: es.enter_context(nc.sbuf_tensor(n, s, d))
        for s in ('dli', 'dgp', 'sv', 'ss', 'st', 'sp', 'cc', 'so', 'dwq', 'dsp', 'dqi', 'dsc', 'dqo', 'dqg', 'dsw', 'dqr', 'dql0', 'dql1', 'dqc', 'drc', 'dxa0', 'dxa1', 'dcs', 'dnt', 'dnl', 'dwd0', 'dwd1', 'dwd2', 'dwd3', 'dh1', 'dpc', 'dhl', 'dwo'):
            ct.SEM[s] = es.enter_context(nc.semaphore(s))

        idf_sb = sb("idf", [128, 128], F32)
        idb_sb = sb("idb", [128, 128], BF16)
        mask_sb = sb("mask", [128, 128], F32)
        nu16_sb = sb("nu16", [128, 128], F32)
        ones_bf = sb("ones_bf", [128, 1], BF16)
        ones_f = sb("ones_f", [128, 1], F32)
        onesk1 = sb("onesk1", [1, 128], F32)
        epsb = sb("epsb", [128, 1], F32)
        oneb = sb("oneb", [128, 1], F32)

        ct.Ds(lambda e: e.dma_start(out=idf_sb[:], in_=idm))
        d_mk = ct.Ds(lambda e: e.dma_start(out=mask_sb[:], in_=masku))
        d_nu = ct.Ds(lambda e: e.dma_start(out=nu16_sb[:], in_=negu16))
        d_idb = ct.Dg(lambda e: e.dma_start(out=idb_sb[:], in_=idm))
        ct.V(lambda v: v.memset(ones_bf[:], 1.0))
        ct.V(lambda v: v.memset(ones_f[:], 1.0))
        ct.V(lambda v: v.memset(onesk1[:], 1.0))
        ct.V(lambda v: v.memset(epsb[:], 1e-5))
        m_const = ct.V(lambda v: v.memset(oneb[:], 1.0))

        if need_ffn:
            dqv = sb("dqv", [128, DQW], I32)
            dqs = sb("dqs", [128, DQW], F32)
            dqt1 = sb("dqt1", [128, DQW], I32)
            dqt2 = sb("dqt2", [128, DQW], I32)
            dq8 = sb("dq8", [128, DQW], U8)
            dqbf = sb("dqbf", [128, DQW], BF16)
            dqlo = sb("dqlo", [128, DQW], BF16)
            dqhi = sb("dqhi", [128, DQW], BF16)
            dq_jobs = []
            nw = MLP // DQW
            for w in range(nw):
                dq_jobs.append((gbytes[:, DQW * w:DQW * (w + 1)], gscale[:, DQW * w:DQW * (w + 1)],
                                agg_in[0:128, DQW * w:DQW * (w + 1)], agg_in[128:256, DQW * w:DQW * (w + 1)]))
            for w in range(nw):
                dq_jobs.append((ubytes[:, DQW * w:DQW * (w + 1)], uscale[:, DQW * w:DQW * (w + 1)],
                                agu_in[0:128, DQW * w:DQW * (w + 1)], agu_in[128:256, DQW * w:DQW * (w + 1)]))
            nwd = DM // DQW
            for jc in range(8):
                for w in range(nwd):
                    dq_jobs.append((dbytes[128 * jc:128 * (jc + 1), DQW * w:DQW * (w + 1)],
                                    dscale[128 * jc:128 * (jc + 1), DQW * w:DQW * (w + 1)],
                                    agd_in[128 * jc:128 * (jc + 1), DQW * w:DQW * (w + 1)],
                                    agd_in[1024 + 128 * jc:1152 + 128 * jc, DQW * w:DQW * (w + 1)]))
            dqst = {"i": 0, "last_out": 0, "cc": 0}

            def emit_dq_tile():
                i = dqst["i"]
                if i >= len(dq_jobs):
                    return
                dqst["i"] += 1
                src, scl, dlo, dhi = dq_jobs[i]
                d_in = ct.Dg(lambda e, s=src: e.dma_start(out=dqv[:], in_=s),
                             waits=[("sv", dqst.get("mcons", 0)), ("dqo", dqst.get("mout", 0))],
                             sem="dqi")
                d_sc = ct.Ds(lambda e, s=scl: e.dma_start(out=dqs[:], in_=s),
                             waits=[("sv", dqst.get("mcons", 0))], sem="dsc")
                wd = [("dqi", d_in), ("dsc", d_sc)]
                ct.V(lambda v: v.tensor_scalar(out=dqt1[:], in0=dqv[:], scalar1=8, scalar2=4,
                                               op0=ALU.bitwise_and, op1=ALU.logical_shift_left), waits=wd)
                ct.V(lambda v: v.tensor_scalar(out=dqt2[:], in0=dqv[:], scalar1=7, scalar2=2,
                                               op0=ALU.bitwise_and, op1=ALU.logical_shift_left))
                ct.V(lambda v: v.tensor_tensor(out=dq8[:], in0=dqt1[:], in1=dqt2[:], op=ALU.add))
                ct.V(lambda v: v.tensor_copy(out=dqbf[:], in_=dq8[:].bitcast(FP8)))
                ct.V(lambda v: v.scalar_tensor_tensor(out=dqlo[:], in0=dqbf[:], scalar=64.0,
                                                      in1=dqs[:], op0=ALU.mult, op1=ALU.mult))
                ct.V(lambda v: v.tensor_scalar(out=dqt1[:], in0=dqv[:], scalar1=0x70, scalar2=2,
                                               op0=ALU.bitwise_and, op1=ALU.logical_shift_right))
                ct.V(lambda v: v.tensor_single_scalar(out=dqt2[:], in_=dqv[:], scalar=0x80, op=ALU.bitwise_and))
                ct.V(lambda v: v.tensor_tensor(out=dq8[:], in0=dqt1[:], in1=dqt2[:], op=ALU.add))
                ct.V(lambda v: v.tensor_copy(out=dqbf[:], in_=dq8[:].bitcast(FP8)))
                mcons = ct.V(lambda v: v.scalar_tensor_tensor(out=dqhi[:], in0=dqbf[:], scalar=64.0,
                                                              in1=dqs[:], op0=ALU.mult, op1=ALU.mult))
                dqst["mcons"] = mcons
                ct.Dg(lambda e, d=dlo: e.dma_start(out=d, in_=dqlo[:]), waits=[("sv", mcons)], sem="dqo")
                m2 = ct.Dg(lambda e, d=dhi: e.dma_start(out=d, in_=dqhi[:]), sem="dqo")
                dqst["mout"] = m2
                dqst["last_out"] = m2

            def dq_finish_and_ag():
                while dqst["i"] < len(dq_jobs):
                    emit_dq_tile()
                ct.CC(lambda e: e.collective_compute("AllGather", ALU.bypass,
                                                     replica_groups=[list(range(8))],
                                                     ins=[agg_in[:]], outs=[gate_wt[:]]),
                      waits=[("dqo", dqst["last_out"])])
                ct.CC(lambda e: e.collective_compute("AllGather", ALU.bypass,
                                                     replica_groups=[list(range(8))],
                                                     ins=[agu_in[:]], outs=[up_wt[:]]))
                dqst["cc"] = ct.CC(lambda e: e.collective_compute("AllGather", ALU.bypass,
                                                                  replica_groups=[list(range(8))],
                                                                  ins=[agd_in[:]], outs=[down_wt[:]]))
        else:
            def emit_dq_tile():
                pass

            def dq_finish_and_ag():
                pass

        cc_rs_done = 0
        if need_attn:
            esA = ExitStack()
            with esA:
                sbA = lambda n, s, d: esA.enter_context(nc.sbuf_tensor(n, s, d))
                v_tok = sbA("v_tok", [128, NCH * 129], BF16)
                k_rope = sbA("k_rope", [128, NSEQ], BF16)
                ktl = sbA("ktl", [128, NSEQ], BF16)
                k2 = sbA("k2", [128, NSEQ], BF16)
                eA = sbA("eA", [128, NSEQ], BF16)
                rkc = sbA("rkc", [128, NCH], F32)
                eacs = sbA("eacs", [128, NCH], F32)
                S_sb = sbA("S_sb", [128, 128], F32)
                Sbf = sbA("Sbf", [128, 128], BF16)
                wsum_sb = sbA("wsum_sb", [1, 640], F32)
                wsumv_sb = sbA("wsumv_sb", [1, 128], F32)
                k_raw = sbA("k_raw", [128, NSEQ], BF16)

                ct.V(lambda v: v.memset(S_sb[:], 0.0))
                ct.V(lambda v: v.memset(Sbf[:], 0.0))
                m_vones = ct.V(lambda v: v.memset(
                    v_tok[:].rearrange("p (c w) -> p c w", w=129)[:, :, 128:129], 1.0))

                esA1 = ExitStack()
                with esA1:
                    sb1 = lambda n, s, d: esA1.enter_context(nc.sbuf_tensor(n, s, d))
                    ps1 = lambda n, s, d: esA1.enter_context(nc.psum_tensor(n, s, d))
                    wqkv_sb = sb1("wqkv", [128, KCH * 640], BF16)
                    wv_sb = sb1("wv_sb", [128, KCH * 128], BF16)
                    xa = [sb1(f"xa{i}", [128, TT], BF16) for i in range(2)]
                    xsq = [sb1(f"xsq{i}", [128, TT], BF16) for i in range(2)]
                    qstage = sb1("qstage", [128, 4 * TT], BF16)
                    murow = sb1("murow", [1, TT], F32)
                    nmurow = sb1("nmurow", [1, TT], F32)
                    m2row = sb1("m2row", [1, TT], F32)
                    varrow = sb1("varrow", [1, TT], F32)
                    rstdrow = sb1("rstdrow", [1, TT], F32)
                    rbc_sb = sb1("rbc_sb", [128, TT], F32)
                    rcol_sb = sb1("rcol_sb", [128, 4], F32)
                    esPA = ExitStack()
                    psPA = lambda n, s, d: esPA.enter_context(nc.psum_tensor(n, s, d))
                    proj_ps = [psPA(f"proj{i}", [128, TT], F32) for i in range(5)]
                    v_ps = psPA("v_ps", [128, TT], F32)
                    misc_ps = psPA("misc_ps", [128, TT], F32)
                    stats2_ps = psPA("stats2_ps", [128, TT], F32)

                    for kc in range(KCH):
                        ct.Dg(lambda e, kc=kc: e.dma_start(
                            out=wqkv_sb[:, 640 * kc:640 * kc + 512], in_=wq[128 * kc:128 * (kc + 1), :]), sem="dwq")
                        ct.Dg(lambda e, kc=kc: e.dma_start(
                            out=wqkv_sb[:, 640 * kc + 512:640 * (kc + 1)], in_=wk[128 * kc:128 * (kc + 1), :]), sem="dwq")
                        ct.Dg(lambda e, kc=kc: e.dma_start(
                            out=wv_sb[:, 128 * kc:128 * (kc + 1)], in_=wv[128 * kc:128 * (kc + 1), :]), sem="dwq")
                    d_w = ct.C["dwq"]
                    for kc in range(KCH):
                        ct.T(lambda t, kc=kc: t.matmul(out=misc_ps[0:1, 0:512], lhsT=ones_bf[:],
                                                       rhs=wqkv_sb[:, 640 * kc:640 * kc + 512],
                                                       start=(kc == 0), stop=(kc == KCH - 1)),
                             waits=([("dwq", d_w), ("sv", m_const)] if kc == 0 else []))
                        ct.T(lambda t, kc=kc: t.matmul(out=stats2_ps[0:1, 0:128], lhsT=ones_bf[:],
                                                       rhs=wqkv_sb[:, 640 * kc + 512:640 * (kc + 1)],
                                                       start=(kc == 0), stop=(kc == KCH - 1)))
                        ct.T(lambda t, kc=kc: t.matmul(out=v_ps[0:1, 0:128], lhsT=ones_bf[:],
                                                       rhs=wv_sb[:, 128 * kc:128 * (kc + 1)],
                                                       start=(kc == 0), stop=(kc == KCH - 1)),
                             inc=(1 if kc == KCH - 1 else 0))
                    m_ws = ct.C["st"]
                    ct.S(lambda s: s.copy(out=wsum_sb[0:1, 0:512], in_=misc_ps[0:1, 0:512]), waits=[("st", m_ws)])
                    ct.S(lambda s: s.copy(out=wsum_sb[0:1, 512:640], in_=stats2_ps[0:1, 0:128]))
                    m_wsv = ct.S(lambda s: s.copy(out=wsumv_sb[:], in_=v_ps[0:1, 0:128]))

                    ch_end_st = {}
                    ch_end_ss = {}
                    prev_ev = {"sv": 0, "ss": 0}
                    prev_spill = 0
                    for i in range(NTT):
                        t0 = TT * i
                        for kc in range(KCH):
                            p = kc % 2
                            wxa = [("st", ch_end_st.get((i, kc - 2), ch_end_st.get((i - 1, KCH - 2 + kc), 0))),
                                   ("ss", ch_end_ss.get((i, kc - 2), ch_end_ss.get((i - 1, KCH - 2 + kc), 0)))]
                            d_xa = ct.Dg(lambda e, p=p, kc=kc, t0=t0: e.dma_start(
                                out=xa[p][:], in_=xT[128 * kc:128 * (kc + 1), t0:t0 + TT]), waits=wxa,
                                sem=f"dxa{p}")
                            wfirst = [(f"dxa{p}", d_xa)] + ([("ss", m_wsv)] if (i == 0 and kc == 0) else [])
                            if kc == 0:
                                wfirst += [("sv", prev_ev["sv"]), ("ss", prev_ev["ss"])]
                            ct.T(lambda t, p=p, kc=kc: t.matmul(
                                out=misc_ps[0:1, :], lhsT=ones_bf[:], rhs=xa[p][:],
                                start=(kc == 0), stop=(kc == KCH - 1)), waits=wfirst)
                            msq = ct.S(lambda s, p=p: s.activation(out=xsq[p][:], in_=xa[p][:], func=AF.Square),
                                       waits=[(f"dxa{p}", d_xa)])
                            ct.T(lambda t, p=p, kc=kc: t.matmul(
                                out=stats2_ps[0:1, :], lhsT=ones_bf[:], rhs=xsq[p][:],
                                start=(kc == 0), stop=(kc == KCH - 1)), waits=[("ss", msq)])
                            for mc in range(5):
                                ct.T(lambda t, mc=mc, kc=kc, p=p: t.matmul(
                                    out=proj_ps[mc][:],
                                    lhsT=wqkv_sb[:, 640 * kc + 128 * mc:640 * kc + 128 * (mc + 1)],
                                    rhs=xa[p][:], start=(kc == 0), stop=False),
                                    waits=([("sv", prev_ev["sv"]), ("dwq", d_w)] if kc == 0 else []))
                            for ts4 in range(4):
                                ct.T(lambda t, ts4=ts4, kc=kc, p=p: t.matmul(
                                    out=v_ps[:, 128 * ts4:128 * (ts4 + 1)],
                                    lhsT=xa[p][:, 128 * ts4:128 * (ts4 + 1)],
                                    rhs=wv_sb[:, 128 * kc:128 * (kc + 1)],
                                    start=(kc == 0 and ts4 == 0), stop=False),
                                    waits=([("sv", prev_ev["sv"])] if (kc == 0 and ts4 == 0) else []),
                                    inc=(1 if ts4 == 3 else 0))
                            ch_end_st[(i, kc)] = ct.C["st"]
                            ch_end_ss[(i, kc)] = ct.C["ss"]
                        m_stats = ch_end_st[(i, KCH - 1)]
                        ct.V(lambda v: v.tensor_scalar(out=murow[:], in0=misc_ps[0:1, :],
                                                       scalar1=1.0 / DM, scalar2=None, op0=ALU.mult),
                             waits=[("st", m_stats)])
                        m_negmu = ct.V(lambda v: v.tensor_scalar(out=nmurow[:], in0=murow[:],
                                                                 scalar1=-1.0, scalar2=None, op0=ALU.mult))
                        ct.V(lambda v: v.tensor_scalar(out=m2row[:], in0=stats2_ps[0:1, :],
                                                       scalar1=1.0 / DM, scalar2=None, op0=ALU.mult))
                        ct.V(lambda v: v.tensor_tensor(out=varrow[:], in0=murow[:], in1=murow[:], op=ALU.mult))
                        m_var = ct.V(lambda v: v.tensor_tensor(out=varrow[:], in0=m2row[:], in1=varrow[:],
                                                               op=ALU.subtract))
                        ct.S(lambda s: s.activation(out=varrow[:], in_=varrow[:], func=AF.Ln,
                                                    bias=epsb[0:1, 0:1]), waits=[("sv", m_var)])
                        m_rstd = ct.S(lambda s: s.activation(out=rstdrow[:], in_=varrow[:], func=AF.Exp,
                                                             scale=-0.5))
                        for mc in range(5):
                            ct.T(lambda t, mc=mc: t.matmul(
                                out=proj_ps[mc][:], lhsT=wsum_sb[0:1, 128 * mc:128 * (mc + 1)],
                                rhs=nmurow[:], start=False, stop=True),
                                waits=([("sv", m_negmu)] if mc == 0 else []), inc=(1 if mc == 4 else 0))
                        m_proj = ct.C["st"]
                        for ts4 in range(4):
                            ct.T(lambda t, ts4=ts4: t.matmul(
                                out=v_ps[:, 128 * ts4:128 * (ts4 + 1)],
                                lhsT=nmurow[0:1, 128 * ts4:128 * (ts4 + 1)], rhs=wsumv_sb[:],
                                start=False, stop=True), inc=(1 if ts4 == 3 else 0))
                        m_vproj = ct.C["st"]
                        m_rbc = ct.T(lambda t: t.matmul(out=misc_ps[:], lhsT=onesk1[:], rhs=rstdrow[:],
                                                        start=True, stop=True),
                                     waits=[("ss", m_rstd), ("sv", m_var)], inc=1)
                        m_rd1 = ct.Dg(lambda e: e.dma_start(out=rstd_d, in_=rstdrow[:]),
                              waits=[("ss", m_rstd), ("sv", prev_ev["sv"])], sem="drc")
                        m_rcold = ct.Dg(lambda e: e.dma_start(
                            out=rcol_sb[:], in_=rstd_d.rearrange("o (j p) -> p (o j)", p=128),
                            allow_slow_non_contiguous=True), waits=[("drc", m_rd1)], sem="drc")
                        m_rbcsb = ct.S(lambda s: s.copy(out=rbc_sb[:], in_=misc_ps[:]), waits=[("st", m_rbc)])
                        for mc in range(5):
                            dst = (qstage[:, TT * mc:TT * (mc + 1)] if mc < 4
                                   else k_raw[:, t0:t0 + TT])
                            ct.V(lambda v, dst=dst, mc=mc: v.tensor_tensor(
                                out=dst, in0=proj_ps[mc][:], in1=rbc_sb[:], op=ALU.mult),
                                waits=[("st", m_proj), ("ss", m_rbcsb)] +
                                      ([("dsp", prev_spill)] if mc == 0 else []))
                        m_qk = ct.C["sv"]
                        for ts4 in range(4):
                            c0 = 129 * (4 * i + ts4)
                            ct.V(lambda v, ts4=ts4, c0=c0: v.tensor_scalar(
                                out=v_tok[:, c0:c0 + 128], in0=v_ps[:, 128 * ts4:128 * (ts4 + 1)],
                                scalar1=rcol_sb[:, ts4:ts4 + 1], scalar2=None, op0=ALU.mult),
                                waits=[("st", m_vproj), ("drc", m_rcold), ("sv", m_vones)])
                        m_spill = ct.Dg(lambda e, t0=t0: e.dma_start(
                            out=qraw_d[:].rearrange("p (h n) -> p h n", h=4)[:, :, t0:t0 + TT],
                            in_=qstage[:].rearrange("p (h n) -> p h n", h=4)),
                            waits=[("sv", m_qk)], sem="dsp")
                        ch_end_st[(i, KCH - 1)] = ct.C["st"]
                        ch_end_ss[(i, KCH - 1)] = ct.C["ss"]
                        prev_ev = {"sv": ct.C["sv"], "ss": ct.C["ss"]}
                        prev_spill = m_spill
                        for _ in range(4):
                            emit_dq_tile()
                    m_pa_v = ct.C["sv"]
                    m_pa_dgp = ct.C["dsp"]

                    esPA.close()
                    cosg = [sb1(f"cosg{i}", [128, TT], F32) for i in range(2)]
                    sing = [sb1(f"sing{i}", [128, TT], F32) for i in range(2)]
                    qg = [sb1(f"qg{i}", [128, 4 * TT], BF16) for i in range(2)]
                    qsw = sb1("qsw", [128, 4 * TT], BF16)
                    ksw = sb1("ksw", [128, TT], BF16)
                    tS1 = [sb1(f"tS1_{i}", [128, 128], F32) for i in range(2)]
                    tS2 = [sb1(f"tS2_{i}", [128, 128], F32) for i in range(2)]
                    tS4 = [sb1(f"tS4_{i}", [128, 128], F32) for i in range(2)]
                    tV1 = [sb1(f"tV1_{i}", [128, 128], F32) for i in range(2)]
                    l1T_sb = [sb1(f"l1T_{i}", [128, 128], F32) for i in range(2)]
                    tp = [sb1(f"tp{i}", [128, TT], F32) for i in range(2)]
                    l1T_ps = [ps1(f"l1T_ps{i}", [128, 128], F32) for i in range(2)]
                    A_ps = [ps1(f"A_ps{i}", [128, 512], F32) for i in range(2)]

                    grp_pool = {}
                    preb_k2 = {}
                    preb_pe = {}
                    preb_ss = {}
                    for gq in range(NTT):
                        pg = gq % 2
                        g0 = TT * gq
                        dcs0 = ct.Ds(lambda e, pg=pg, g0=g0: e.dma_start(out=cosg[pg][:], in_=cosT[:, g0:g0 + TT]),
                                     waits=[("sp", grp_pool.get(gq - 2, 0))], sem="dcs")
                        dcs = ct.Ds(lambda e, pg=pg, g0=g0: e.dma_start(out=sing[pg][:], in_=sinT[:, g0:g0 + TT]),
                                    sem="dcs")
                        d_qg = ct.Dg(lambda e, pg=pg, g0=g0: e.dma_start(
                            out=qg[pg][:].rearrange("p (h n) -> p h n", h=4),
                            in_=qraw_d[:].rearrange("p (h n) -> p h n", h=4)[:, :, g0:g0 + TT]),
                            waits=[("sp", grp_pool.get(gq - 2, 0)),
                                   ("dqr", grp_pool.get(("d", gq - 2), 0)),
                                   ("dsp", m_pa_dgp)], sem="dqg")
                        d_sw1 = ct.Dg(lambda e, pg=pg: e.dma_start(out=qsw[0:64, :], in_=qg[pg][64:128, :]),
                                      waits=[("dqg", d_qg), ("sp", grp_pool.get(gq - 1, 0))], sem="dsw")
                        d_sw2 = ct.Dg(lambda e, pg=pg: e.dma_start(out=qsw[64:128, :], in_=qg[pg][0:64, :]), sem="dsw")
                        d_sw3 = ct.Dg(lambda e, g0=g0: e.dma_start(out=ksw[0:64, :], in_=k_raw[64:128, g0:g0 + TT]), sem="dsw")
                        d_sw4 = ct.Dg(lambda e, g0=g0: e.dma_start(out=ksw[64:128, :], in_=k_raw[0:64, g0:g0 + TT]), sem="dsw")
                        wro = [("dcs", dcs), ("dsw", d_sw4), ("dqg", d_qg), ("sv", m_pa_v)]
                        for h in range(4):
                            c0 = TT * h
                            for lo, hi in ((0, 64), (64, 128)):
                                sgn = ALU.subtract if lo == 0 else ALU.add
                                ct.G(lambda g, c0=c0, pg=pg, lo=lo, hi=hi: g.tensor_tensor(
                                    out=tp[0][lo:hi, :], in0=qg[pg][lo:hi, c0:c0 + TT],
                                    in1=cosg[pg][lo:hi, :], op=ALU.mult), waits=(wro if (h == 0 and lo == 0) else []))
                                ct.G(lambda g, c0=c0, pg=pg, lo=lo, hi=hi: g.tensor_tensor(
                                    out=tp[1][lo:hi, :], in0=qsw[lo:hi, c0:c0 + TT],
                                    in1=sing[pg][lo:hi, :], op=ALU.mult))
                                ct.G(lambda g, c0=c0, pg=pg, lo=lo, hi=hi, sgn=sgn: g.tensor_tensor(
                                    out=qg[pg][lo:hi, c0:c0 + TT], in0=tp[0][lo:hi, :],
                                    in1=tp[1][lo:hi, :], op=sgn))
                        m_qrope = ct.C["sp"]
                        ct.Dg(lambda e, pg=pg, g0=g0: e.dma_start(
                            out=qrope_d[:].rearrange("p (h n) -> p h n", h=4)[:, :, g0:g0 + TT],
                            in_=qg[pg][:].rearrange("p (h n) -> p h n", h=4)),
                            waits=[("sp", m_qrope)], sem="dqr")
                        for lo, hi in ((0, 64), (64, 128)):
                            sgn = ALU.subtract if lo == 0 else ALU.add
                            ct.G(lambda g, g0=g0, pg=pg, lo=lo, hi=hi: g.tensor_tensor(
                                out=tp[0][lo:hi, :], in0=k_raw[lo:hi, g0:g0 + TT],
                                in1=cosg[pg][lo:hi, :], op=ALU.mult))
                            ct.G(lambda g, g0=g0, pg=pg, lo=lo, hi=hi: g.tensor_tensor(
                                out=tp[1][lo:hi, :], in0=ksw[lo:hi, :],
                                in1=sing[pg][lo:hi, :], op=ALU.mult))
                            ct.G(lambda g, g0=g0, pg=pg, lo=lo, hi=hi, sgn=sgn: g.tensor_tensor(
                                out=k_rope[lo:hi, g0:g0 + TT], in0=tp[0][lo:hi, :],
                                in1=tp[1][lo:hi, :], op=sgn))
                        m_kr = ct.C["sp"]
                        grp_pool[gq] = m_kr
                        grp_pool[("d", gq)] = ct.C["dqr"]

                        for cc4 in range(4):
                            c = 4 * gq + cc4
                            c0 = CH * c
                            pc = c % 2
                            m_s1 = ct.S(lambda s, c0=c0, pc=pc: s.activation(
                                out=tS1[pc][:], in_=k_raw[:, c0:c0 + CH], func=AF.Exp, scale=-1.0),
                                waits=[("sv", preb_k2.get(c - 2, 0)), ("st", preb_pe.get(c - 2, 0))])
                            m_l1 = ct.S(lambda s, pc=pc: s.activation(out=tS2[pc][:], in_=tS1[pc][:],
                                                                      func=AF.Ln, bias=oneb[:, 0:1]))
                            m_tr = ct.T(lambda t, pc=pc: t.transpose(out=l1T_ps[pc][:], in_=tS2[pc][:],
                                                                     identity=idf_sb[:]),
                                        waits=[("ss", m_l1)], inc=1)
                            m_l1T = ct.S(lambda s, pc=pc: s.copy(out=l1T_sb[pc][:], in_=l1T_ps[pc][:]),
                                         waits=[("st", m_tr)])
                            m_s4 = ct.S(lambda s, c0=c0, pc=pc: s.activation(out=tS4[pc][:],
                                                                             in_=k_raw[:, c0:c0 + CH], func=AF.Exp))
                            m_A = ct.T(lambda t, pc=pc: t.matmul(out=A_ps[pc][:, 0:128], lhsT=l1T_sb[pc][:],
                                                                 rhs=nu16_sb[:], start=True, stop=True),
                                       waits=[("ss", m_l1T), ("dli", d_nu)])
                            m_smk = ct.T(lambda t, pc=pc: t.matmul(out=A_ps[pc][:, 480:481], lhsT=tS4[pc][:],
                                                                   rhs=ones_f[:], start=False, stop=True),
                                         waits=[("ss", m_s4)], inc=1)
                            ct.S(lambda s, c0=c0, pc=pc: s.activation(out=eA[:, c0:c0 + CH],
                                                                      in_=A_ps[pc][:, 0:128], func=AF.Exp),
                                 waits=[("st", m_smk)])
                            ct.S(lambda s, pc=pc: s.activation(out=tS1[pc][:], in_=A_ps[pc][:, 0:128],
                                                               func=AF.Exp, scale=-1.0))
                            m_eac = ct.S(lambda s, c=c, pc=pc: s.activation(out=eacs[:, c:c + 1],
                                                                            in_=A_ps[pc][:, 127:128], func=AF.Exp))
                            ct.V(lambda v, c=c, pc=pc: v.reciprocal(out=rkc[:, c:c + 1],
                                                                    in_=A_ps[pc][:, 480:481]),
                                 waits=[("st", m_smk)])
                            ct.V(lambda v, pc=pc: v.tensor_tensor(out=tV1[pc][:], in0=tS4[pc][:],
                                                                  in1=tS1[pc][:], op=ALU.mult),
                                 waits=[("ss", m_eac)])
                            ct.V(lambda v, c0=c0, pc=pc: v.tensor_copy(out=ktl[:, c0:c0 + CH], in_=tV1[pc][:]))
                            m_k2c = ct.V(lambda v, c0=c0, c=c, pc=pc: v.tensor_scalar(
                                out=k2[:, c0:c0 + CH], in0=tV1[pc][:], scalar1=eacs[:, c:c + 1],
                                scalar2=None, op0=ALU.mult))
                            preb_k2[c] = m_k2c
                            preb_pe[c] = ct.C["st"]
                            preb_ss[c] = ct.C["ss"]
                        emit_dq_tile()
                        emit_dq_tile()
                    m_preb_v = ct.C["sv"]
                    m_preb_p = ct.C["sp"]
                    m_preb_s = ct.C["ss"]
                    if part == "attn" and dbg:
                        wdbg = [("sv", ct.C["sv"]), ("ss", ct.C["ss"]), ("sp", ct.C["sp"]),
                                ("st", ct.C["st"]), ("dgp", ct.C["dgp"])]
                        ct.Do(lambda e: e.dma_start(out=dbg_t["d_kraw"], in_=k_raw[:]), waits=wdbg)
                        ct.Do(lambda e: e.dma_start(out=dbg_t["d_krope"], in_=k_rope[:]))
                        ct.Do(lambda e: e.dma_start(out=dbg_t["d_ktl"], in_=ktl[:]))
                        ct.Do(lambda e: e.dma_start(out=dbg_t["d_k2"], in_=k2[:]))
                        ct.Do(lambda e: e.dma_start(out=dbg_t["d_eA"], in_=eA[:]))
                        ct.Do(lambda e: e.dma_start(out=dbg_t["d_vtok"], in_=v_tok[:]))
                        ct.Do(lambda e: e.dma_start(out=dbg_t["d_rkc"], in_=rkc[:]))
                        ct.Do(lambda e: e.dma_start(out=dbg_t["d_eacs"], in_=eacs[:]))
                        ct.Do(lambda e: e.dma_start(out=dbg_t["d_rcol"], in_=rcol_sb[:]))
                        ct.Do(lambda e: e.dma_start(out=dbg_t["d_qraw"], in_=qraw_d[:]))
                        ct.Do(lambda e: e.dma_start(out=dbg_t["d_qrope"], in_=qrope_d[:]))

                esB = ExitStack()
                with esB:
                    sbB = lambda n, s, d: esB.enter_context(nc.sbuf_tensor(n, s, d))
                    psB = lambda n, s, d: esB.enter_context(nc.psum_tensor(n, s, d))
                    P_sb = [sbB(f"P{i}", [128, 512], BF16) for i in range(2)]
                    qro = [sbB(f"qro{i}", [128, 512], BF16) for i in range(2)]
                    Pg_sb = sbB("Pg", [128, 512], BF16)
                    qrawc = sbB("qrawc", [128, 512], BF16)
                    qe_f = sbB("qe_f", [128, 512], F32)
                    qhat = sbB("qhat", [128, 512], BF16)
                    k2t_sb = sbB("k2t", [128, 128], BF16)
                    v2t = sbB("v2t", [128, 128], BF16)
                    rq_sb = sbB("rq_sb", [128, 4], F32)
                    rd_sb = sbB("rd_sb", [128, 4], F32)
                    gla_f = sbB("gla_f", [128, 512], F32)
                    o_tok = sbB("o_tok", [128, 512], BF16)
                    ofeat_q = sbB("ofeat_q", [128, 4 * TQ], BF16)
                    wo_sb = sbB("wo_sb", [128, 4 * DM], BF16)
                    wop_sb = [sbB(f"wop{i}", [128, DM], F32) for i in range(2)]

                    ya = [[psB(f"ya{c}_{b}", [128, 512], F32) for b in range(2)] for c in range(2)]
                    gl_ps = [psB(f"gl{c}", [128, 512], F32) for c in range(2)]
                    sc_ps = [psB(f"sc{i}", [128, 512], F32) for i in range(2)]

                    for oc in range(4):
                        ct.Dg(lambda e, oc=oc: e.dma_start(
                            out=wo_sb[:, DM * oc:DM * (oc + 1)], in_=wo[128 * oc:128 * (oc + 1), :]),
                            sem="dwo")
                    d_wo = ct.C["dwo"]

                    rot = sc_ps
                    YOF = [0, 132]
                    uidx = 0
                    exp_marks = {}
                    bank_wait = {0: [], 1: []}
                    m_fin_s = m_preb_s
                    m_finv = {0: m_preb_v, 1: m_preb_v}
                    m_sbf = 0
                    m_sq_prev = 0
                    m_qS_prev = 0
                    m_kv_prev = 0
                    m_ot_prev = 0
                    m_oftr_prev = 0
                    m_qh_prev = 0
                    m_rsd = 0
                    tile_q_last = {}
                    pend = {}

                    def diag_block(tau):
                        nonlocal_vars = None
                        return

                    def emit_diag(tau, dmarks):
                        ctx = tau % 2
                        nonlocal_ = dmarks
                        m_y = dmarks["m_y"]
                        d_q = ct.Dg(lambda e, tau=tau: e.dma_start(
                            out=qrawc[:].rearrange("p (h n) -> p h n", h=4),
                            in_=qraw_d[:].rearrange("p (h n) -> p h n", h=4)[:, :, CH * tau:CH * (tau + 1)]),
                            waits=[("ss", dmarks["fin_s_prev"])], sem="dqc")
                        m_qe = ct.S(lambda s: s.activation(out=qe_f[:], in_=qrawc[:], func=AF.Exp),
                                    waits=[("dqc", d_q), ("sv", dmarks["qh_prev"]), ("st", dmarks["sq_prev"])])
                        for h in range(4):
                            ct.V(lambda v, h=h, tau=tau: v.scalar_tensor_tensor(
                                out=qhat[:, 128 * h:128 * (h + 1)], in0=qe_f[:, 128 * h:128 * (h + 1)],
                                scalar=0.5, in1=eA[:, CH * tau:CH * (tau + 1)],
                                op0=ALU.mult, op1=ALU.mult),
                                waits=([("ss", m_qe), ("st", dmarks["qS_prev"]),
                                        ("ss", preb_ss[tau]), ("sv", preb_k2[tau])] if h == 0 else []))
                        m_qh = ct.C["sv"]
                        m_gs = ct.T(lambda t, tau=tau, ctx=ctx: t.matmul(
                            out=gl_ps[ctx][:], lhsT=ktl[:, CH * tau:CH * (tau + 1)], rhs=qhat[:],
                            start=True, stop=True),
                            waits=[("sv", m_qh), ("st", m_y), ("ss", dmarks["fin_s_prev"])], inc=1)
                        for h in range(4):
                            ct.V(lambda v, h=h, tau=tau, ctx=ctx: v.scalar_tensor_tensor(
                                out=Pg_sb[:, 128 * h:128 * (h + 1)], in0=gl_ps[ctx][:, 128 * h:128 * (h + 1)],
                                scalar=rkc[:, tau:tau + 1], in1=mask_sb[:],
                                op0=ALU.mult, op1=ALU.mult),
                                waits=([("st", m_gs)] if h == 0 else []))
                        m_pg = ct.C["sv"]
                        for h in range(4):
                            ct.T(lambda t, h=h, tau=tau, ctx=ctx: t.matmul(
                                out=gl_ps[ctx][:, 128 * h:128 * (h + 1)], lhsT=Pg_sb[:, 128 * h:128 * (h + 1)],
                                rhs=v_tok[:, 129 * tau:129 * tau + 128],
                                start=(h == 0), stop=False), waits=([("sv", m_pg)] if h == 0 else []))
                        for h in range(4):
                            ct.T(lambda t, h=h, ctx=ctx: t.matmul(
                                out=gl_ps[ctx][:, 128 * h:128 * (h + 1)], lhsT=qhat[:, 128 * h:128 * (h + 1)],
                                rhs=Sbf[:], start=False, stop=(h == 3)),
                                waits=([("ss", dmarks["sbf_prev"])] if h == 0 else []), inc=(1 if h == 3 else 0))
                        m_qS = ct.C["st"]
                        for h in range(4):
                            ct.T(lambda t, h=h, ctx=ctx: t.matmul(
                                out=ya[ctx][0][:, 400 + h:401 + h], lhsT=qe_f[:, 128 * h:128 * (h + 1)],
                                rhs=ones_f[:], start=False, stop=(h == 3)),
                                waits=([("st", m_y)] if h == 0 else []), inc=(1 if h == 3 else 0))
                        m_sq = ct.C["st"]
                        m_k2t = ct.T(lambda t, tau=tau, ctx=ctx: t.transpose(
                            out=ya[ctx][0][:, 300:364].bitcast(BF16), in_=k2[:, CH * tau:CH * (tau + 1)],
                            identity=idb_sb[:]),
                            waits=[("dgp", d_idb), ("sv", preb_k2[tau])], inc=1)
                        m_k2s = ct.S(lambda s, ctx=ctx: s.copy(out=k2t_sb[:], in_=ya[ctx][0][:, 300:364].bitcast(BF16)),
                                     waits=[("st", m_k2t)])
                        m_v2 = ct.V(lambda v, tau=tau: v.tensor_scalar(
                            out=v2t[:], in0=v_tok[:, 129 * tau:129 * tau + 128],
                            scalar1=rkc[:, tau:tau + 1], scalar2=None, op0=ALU.mult),
                            waits=[("st", dmarks["kv_prev"])])
                        m_kv = ct.T(lambda t, ctx=ctx: t.matmul(
                            out=ya[ctx][1][:, 300:428], lhsT=k2t_sb[:], rhs=v2t[:],
                            start=False, stop=True), waits=[("ss", m_k2s), ("sv", m_v2)], inc=1)
                        m_su = ct.V(lambda v, tau=tau, ctx=ctx: v.scalar_tensor_tensor(
                            out=S_sb[:], in0=S_sb[:], scalar=eacs[:, tau:tau + 1], in1=ya[ctx][1][:, 300:428],
                            op0=ALU.mult, op1=ALU.add), waits=[("st", max(m_kv, m_qS))])
                        m_sbf_n = ct.S(lambda s: s.copy(out=Sbf[:], in_=S_sb[:]), waits=[("sv", m_su)])
                        m_rq = ct.V(lambda v, ctx=ctx: v.reciprocal(out=rq_sb[:], in_=ya[ctx][0][:, 400:404]),
                                    waits=[("st", m_sq)])
                        for h in range(4):
                            bnk, off = h // 2, YOF[h % 2]
                            ct.V(lambda v, h=h, ctx=ctx, bnk=bnk, off=off: v.reciprocal(
                                out=rd_sb[:, h:h + 1], in_=ya[ctx][bnk][:, off + 128:off + 129]),
                                waits=([("st", m_qS)] if h == 0 else []))
                        ct.V(lambda v: v.tensor_scalar(out=rd_sb[:], in0=rd_sb[:], scalar1=0.5,
                                                       scalar2=None, op0=ALU.mult))
                        m_glaev = ct.S(lambda s, ctx=ctx: s.copy(out=gla_f[:], in_=gl_ps[ctx][:]),
                                       waits=[("st", m_qS), ("sv", dmarks["ot_prev"])])
                        for h in range(4):
                            bnk, off = h // 2, YOF[h % 2]
                            ct.V(lambda v, h=h: v.tensor_scalar(
                                out=gla_f[:, 128 * h:128 * (h + 1)], in0=gla_f[:, 128 * h:128 * (h + 1)],
                                scalar1=rq_sb[:, h:h + 1], scalar2=None, op0=ALU.mult),
                                waits=([("ss", m_glaev), ("sv", m_rq)] if h == 0 else []))
                            ct.V(lambda v, h=h, ctx=ctx, bnk=bnk, off=off: v.scalar_tensor_tensor(
                                out=o_tok[:, 128 * h:128 * (h + 1)], in0=ya[ctx][bnk][:, off:off + 128],
                                scalar=rd_sb[:, h:h + 1], in1=gla_f[:, 128 * h:128 * (h + 1)],
                                op0=ALU.mult, op1=ALU.add),
                                waits=([("st", dmarks["oftr_prev"])] if h == 0 else []))
                        m_ot = ct.C["sv"]
                        tl = 128 * (tau % 16)
                        trdst = [(0, 436), (1, 436), (0, 300), (1, 300)]
                        for h in range(4):
                            bnk, off = trdst[h]
                            mtr = ct.T(lambda t, h=h, ctx=ctx, bnk=bnk, off=off: t.transpose(
                                out=ya[ctx][bnk][:, off:off + 64].bitcast(BF16),
                                in_=o_tok[:, 128 * h:128 * (h + 1)], identity=idb_sb[:]),
                                waits=([("sv", m_ot)] if h == 0 else []), inc=1)
                            ct.S(lambda s, h=h, tl=tl, ctx=ctx, bnk=bnk, off=off: s.copy(
                                out=ofeat_q[:, TQ * h + tl:TQ * h + tl + 128],
                                in_=ya[ctx][bnk][:, off:off + 64].bitcast(BF16)), waits=[("st", mtr)])
                        return {"m_sq": m_sq, "m_qS": m_qS, "m_kv": m_kv, "m_qh": m_qh,
                                "m_sbf": m_sbf_n, "m_ot": m_ot,
                                "m_oftr": ct.C["st"], "fin_s": ct.C["ss"], "fin_v": ct.C["sv"]}

                    def emit_wo(qtr):
                        nonlocal m_rsd
                        ctx = (16 * qtr + 15) % 2
                        wbanks = [ya[ctx][0], ya[ctx][1], gl_ps[ctx]]
                        em = {}
                        rsm = {}
                        widx = 0
                        for ts in range(16):
                            for kq in range(4):
                                bk = wbanks[widx % 3]
                                for oc in range(4):
                                    w0 = []
                                    if oc == 0:
                                        w0 = [("sv", em.get(widx - 3, 0))]
                                        if widx < 3:
                                            w0 += [("sv", m_finv[ctx]), ("ss", ct.C["ss"]), ("dwo", d_wo)]
                                    ct.T(lambda t, ts=ts, kq=kq, oc=oc, bk=bk: t.matmul(
                                        out=bk[:, 0:512],
                                        lhsT=ofeat_q[:, TQ * oc + 128 * ts:TQ * oc + 128 * (ts + 1)],
                                        rhs=wo_sb[:, DM * oc + 512 * kq:DM * oc + 512 * (kq + 1)],
                                        start=(oc == 0), stop=(oc == 3)),
                                        waits=w0, inc=(1 if oc == 3 else 0))
                                m_wp = ct.C["st"]
                                pwo = widx % 2
                                ct.V(lambda v, bk=bk, kq=kq, pwo=pwo: v.tensor_copy(
                                    out=wop_sb[pwo][:, 512 * kq:512 * (kq + 1)], in_=bk[:, 0:512]),
                                    waits=[("st", m_wp), ("dh1", rsm.get(widx - 2, 0))])
                                em[widx] = ct.C["sv"]
                                m_rsd = ct.Dg(lambda e, qtr=qtr, ts=ts, kq=kq, pwo=pwo: e.dma_start(
                                    out=rsin[qtr][128 * ts:128 * (ts + 1), 512 * kq:512 * (kq + 1)],
                                    in_=wop_sb[pwo][:, 512 * kq:512 * (kq + 1)]),
                                    waits=[("sv", em[widx])], sem="dh1")
                                rsm[widx] = m_rsd
                                widx += 1
                        m_finv[ctx] = ct.C["sv"]
                        ct.CC(lambda e, qtr=qtr: e.collective_compute(
                            "ReduceScatter", ALU.add, replica_groups=[[0, 1, 2, 3], [4, 5, 6, 7]],
                            ins=[rsin[qtr][:]], outs=[rsout[qtr][:]]),
                            waits=[("dh1", m_rsd)])

                    pending_diag = None
                    pend_y = None
                    for tau in range(NCH):
                        pt = tau % 2
                        ctx = tau % 2
                        d_qt = ct.Dg(lambda e, tau=tau, pt=pt: e.dma_start(
                            out=qro[pt][:].rearrange("p (h n) -> p h n", h=4),
                            in_=qrope_d[:].rearrange("p (h n) -> p h n", h=4)[:, :, CH * tau:CH * (tau + 1)]),
                            waits=[("st", tile_q_last.get(tau - 2, 0)),
                                   ("dqr", grp_pool[("d", tau // 4)])], sem=f"dql{pt}")
                        q3 = qro[pt][:].rearrange("p (h n) -> p h n", h=4)
                        for c in range(tau + 1):
                            r = uidx % 2
                            diag = (c == tau)
                            wsc = [("ss", exp_marks.get(uidx - 2, 0)), (f"dql{pt}", d_qt),
                                   ("sp", grp_pool[c // 4])] + bank_wait[r]
                            bank_wait[r] = []
                            m_sc = ct.T(lambda t, c=c, r=r, q3=q3: t.matmul(
                                out=rot[r][:], lhsT=k_rope[:, CH * c:CH * (c + 1)], rhs=q3,
                                start=True, stop=True), waits=wsc, inc=1)
                            m_exp = ct.S(lambda s, r=r: s.activation(
                                out=P_sb[r][:], in_=rot[r][:], func=AF.Exp, scale=RSQ),
                                waits=[("st", m_sc)], serial=False)
                            exp_marks[uidx] = m_exp
                            ywait = [("ss", m_exp)]
                            if diag:
                                for h in range(4):
                                    ct.V(lambda v, h=h, r=r: v.tensor_tensor(
                                        out=P_sb[r][:, 128 * h:128 * (h + 1)],
                                        in0=P_sb[r][:, 128 * h:128 * (h + 1)],
                                        in1=mask_sb[:], op=ALU.mult),
                                        waits=([("ss", m_exp), ("dli", d_mk)] if h == 0 else []))
                                ywait = [("sv", ct.C["sv"])]
                            if c == 0:
                                ywait.append(("sv", m_finv[ctx]))
                                ywait.append(("ss", m_fin_s))
                            if pend_y is not None:
                                _r, _c, _diag, _ctx, _yw = pend_y
                                for h in range(4):
                                    bnk, off = h // 2, YOF[h % 2]
                                    ct.T(lambda t, h=h, _r=_r, _c=_c, _diag=_diag, _ctx=_ctx, bnk=bnk, off=off: t.matmul(
                                        out=ya[_ctx][bnk][:, off:off + 129], lhsT=P_sb[_r][:, 128 * h:128 * (h + 1)],
                                        rhs=v_tok[:, 129 * _c:129 * _c + 129],
                                        start=(_c == 0 and h % 2 == 0), stop=_diag),
                                        waits=(_yw if h == 0 else []), inc=(1 if h == 3 else 0))
                            pend_y = (r, c, diag, ctx, ywait)
                            m_y = ct.C["st"]
                            uidx += 1
                            if c == 2 and pending_diag is not None:
                                ptau, pm = pending_diag
                                res = emit_diag(ptau, pm)
                                m_sq_prev, m_qS_prev, m_kv_prev = res["m_sq"], res["m_qS"], res["m_kv"]
                                m_qh_prev, m_sbf, m_ot_prev = res["m_qh"], res["m_sbf"], res["m_ot"]
                                m_oftr_prev, m_fin_s = res["m_oftr"], res["fin_s"]
                                m_finv[ptau % 2] = res["fin_v"]
                                pending_diag = None
                                if ptau % 16 == 15:
                                    if ptau == 15 and need_ffn:
                                        dq_finish_and_ag()
                                    emit_wo(ptau // 16)
                        if pend_y is not None:
                            _r, _c, _diag, _ctx, _yw = pend_y
                            for h in range(4):
                                bnk, off = h // 2, YOF[h % 2]
                                ct.T(lambda t, h=h, _r=_r, _c=_c, _diag=_diag, _ctx=_ctx, bnk=bnk, off=off: t.matmul(
                                    out=ya[_ctx][bnk][:, off:off + 129], lhsT=P_sb[_r][:, 128 * h:128 * (h + 1)],
                                    rhs=v_tok[:, 129 * _c:129 * _c + 129],
                                    start=(_c == 0 and h % 2 == 0), stop=_diag),
                                    waits=(_yw if h == 0 else []), inc=(1 if h == 3 else 0))
                            pend_y = None
                        m_y = ct.C["st"]
                        tile_q_last[tau] = ct.C["st"]
                        if pending_diag is not None:
                            ptau, pm = pending_diag
                            res = emit_diag(ptau, pm)
                            m_sq_prev, m_qS_prev, m_kv_prev = res["m_sq"], res["m_qS"], res["m_kv"]
                            m_qh_prev, m_sbf, m_ot_prev = res["m_qh"], res["m_sbf"], res["m_ot"]
                            m_oftr_prev, m_fin_s = res["m_oftr"], res["fin_s"]
                            m_finv[ptau % 2] = res["fin_v"]
                            if ptau % 16 == 15:
                                if ptau == 15 and need_ffn:
                                    dq_finish_and_ag()
                                emit_wo(ptau // 16)
                        pending_diag = (tau, {"m_y": m_y, "fin_s_prev": m_fin_s,
                                              "qh_prev": m_qh_prev, "sq_prev": m_sq_prev,
                                              "qS_prev": m_qS_prev, "kv_prev": m_kv_prev,
                                              "sbf_prev": m_sbf, "ot_prev": m_ot_prev,
                                              "oftr_prev": m_oftr_prev})
                        if tau < 12 and need_ffn:
                            for _ in range(6):
                                emit_dq_tile()
                    ptau, pm = pending_diag
                    res = emit_diag(ptau, pm)
                    m_fin_s = res["fin_s"]
                    m_finv[ptau % 2] = res["fin_v"]
                    emit_wo(3)
                    cc_rs_done = ct.C["cc"]

            if part == "attn":
                for i in range(4):
                    ct.Do(lambda e, i=i: e.dma_start(out=h1src[i], in_=rsout[i][:]),
                          waits=[("cc", cc_rs_done)])
                if dbg:
                    ct.Do(lambda e: e.dma_start(out=dbg_t["d_rsin"], in_=rsin[0][:]))
                ct.W("gpsimd", [("so", ct.C["so"])])

        if need_ffn:
            esC = ExitStack()
            with esC:
                sbC = lambda n, s, d: esC.enter_context(nc.sbuf_tensor(n, s, d))
                psC = lambda n, s, d: esC.enter_context(nc.psum_tensor(n, s, d))
                xT_f = sbC("xT_f", [128, KCH * TQ], BF16)
                esC1 = ExitStack()
                with esC1:
                    sbC1 = lambda n, s, d: esC1.enter_context(nc.sbuf_tensor(n, s, d))
                    att_f = sbC1("att_f", [128, DM], F32)
                    atsq = sbC1("atsq", [128, DM], F32)
                    x0t = sbC1("x0t", [128, DM], F32)
                    h1t = sbC1("h1t", [128, DM], F32)
                    fi_t = sbC1("fi_t", [128, DM], BF16)
                    mu_c = sbC1("mu_c", [128, 1], F32)
                    m2_c = sbC1("m2_c", [128, 1], F32)
                    r_c = sbC1("r_c", [128, 1], F32)
                    psC1 = lambda n, s, d: esC1.enter_context(nc.psum_tensor(n, s, d))
                    tr2_ps = [psC1(f"tr2_{i}", [128, 128], BF16) for i in range(2)]

                    wcc = [("cc", cc_rs_done)] if need_attn else []
                    for ts in range(16):
                        qtr, j = ts // 4, ts % 4
                        d_rs = ct.Ds(lambda e, qtr=qtr, j=j: e.dma_start(
                            out=att_f[:], in_=rsout[qtr][128 * j:128 * (j + 1), :]),
                            waits=wcc + [("st", ct.C["st"]), ("ss", ct.C["ss"]), ("sv", ct.C["sv"])],
                            sem="dpc")
                        d_x0 = ct.Ds(lambda e, ts=ts: e.dma_start(
                            out=x0t[:], in_=x0q[128 * ts:128 * (ts + 1), :]), sem="dpc")
                        ct.V(lambda v: v.tensor_reduce(out=mu_c[:], in_=att_f[:], axis=mybir.AxisListType.X,
                                                       op=ALU.add), waits=[("dpc", d_x0)])
                        ct.V(lambda v: v.tensor_scalar(out=mu_c[:], in0=mu_c[:], scalar1=1.0 / DM,
                                                       scalar2=None, op0=ALU.mult))
                        m_sq1 = ct.S(lambda s: s.activation(out=atsq[:], in_=att_f[:], func=AF.Square),
                                     waits=[("dpc", d_x0)])
                        ct.V(lambda v: v.tensor_reduce(out=m2_c[:], in_=atsq[:], axis=mybir.AxisListType.X,
                                                       op=ALU.add), waits=[("ss", m_sq1)])
                        ct.V(lambda v: v.tensor_scalar(out=m2_c[:], in0=m2_c[:], scalar1=1.0 / DM,
                                                       scalar2=None, op0=ALU.mult))
                        ct.V(lambda v: v.tensor_tensor(out=r_c[:], in0=mu_c[:], in1=mu_c[:], op=ALU.mult))
                        m_v1 = ct.V(lambda v: v.tensor_tensor(out=r_c[:], in0=m2_c[:], in1=r_c[:], op=ALU.subtract))
                        ct.S(lambda s: s.activation(out=r_c[:], in_=r_c[:], func=AF.Ln, bias=epsb[:, 0:1]),
                             waits=[("sv", m_v1)])
                        m_r1 = ct.S(lambda s: s.activation(out=r_c[:], in_=r_c[:], func=AF.Exp, scale=-0.5))
                        ct.V(lambda v: v.tensor_scalar(out=att_f[:], in0=att_f[:], scalar1=mu_c[:, 0:1],
                                                       scalar2=r_c[:, 0:1], op0=ALU.subtract, op1=ALU.mult),
                             waits=[("ss", m_r1)])
                        m_h1 = ct.V(lambda v: v.tensor_tensor(out=h1t[:], in0=att_f[:], in1=x0t[:], op=ALU.add),
                                    waits=[("dh1", ct.C["dh1"])])
                        ct.Dg(lambda e, ts=ts: e.dma_start(out=h1_d[128 * ts:128 * (ts + 1), :], in_=h1t[:]),
                              waits=[("sv", m_h1)], sem="dh1")
                        ct.V(lambda v: v.tensor_reduce(out=mu_c[:], in_=h1t[:], axis=mybir.AxisListType.X,
                                                       op=ALU.add))
                        ct.V(lambda v: v.tensor_scalar(out=mu_c[:], in0=mu_c[:], scalar1=1.0 / DM,
                                                       scalar2=None, op0=ALU.mult))
                        m_sq2 = ct.S(lambda s: s.activation(out=atsq[:], in_=h1t[:], func=AF.Square),
                                     waits=[("sv", m_h1)])
                        ct.V(lambda v: v.tensor_reduce(out=m2_c[:], in_=atsq[:], axis=mybir.AxisListType.X,
                                                       op=ALU.add), waits=[("ss", m_sq2)])
                        ct.V(lambda v: v.tensor_scalar(out=m2_c[:], in0=m2_c[:], scalar1=1.0 / DM,
                                                       scalar2=None, op0=ALU.mult))
                        ct.V(lambda v: v.tensor_tensor(out=r_c[:], in0=mu_c[:], in1=mu_c[:], op=ALU.mult))
                        m_v2 = ct.V(lambda v: v.tensor_tensor(out=r_c[:], in0=m2_c[:], in1=r_c[:], op=ALU.subtract))
                        ct.S(lambda s: s.activation(out=r_c[:], in_=r_c[:], func=AF.Ln, bias=epsb[:, 0:1]),
                             waits=[("sv", m_v2)])
                        m_r2 = ct.S(lambda s: s.activation(out=r_c[:], in_=r_c[:], func=AF.Exp, scale=-0.5))
                        m_fi = ct.V(lambda v: v.tensor_scalar(out=fi_t[:], in0=h1t[:], scalar1=mu_c[:, 0:1],
                                                              scalar2=r_c[:, 0:1], op0=ALU.subtract, op1=ALU.mult),
                                    waits=[("ss", m_r2)])
                        for kc in range(KCH):
                            pk = kc % 2
                            mtr = ct.T(lambda t, kc=kc, pk=pk: t.transpose(
                                out=tr2_ps[pk][:], in_=fi_t[:, 128 * kc:128 * (kc + 1)], identity=idb_sb[:]),
                                waits=([("sv", m_fi), ("ss", ct.C["ss"])] if kc < 2 else [("ss", ct.C["ss"])]),
                                inc=1)
                            ct.S(lambda s, kc=kc, ts=ts, pk=pk: s.copy(
                                out=xT_f[:, TQ * kc + 128 * ts:TQ * kc + 128 * (ts + 1)], in_=tr2_ps[pk][:]),
                                waits=[("st", mtr)])
                    m_xtf = ct.C["ss"]

                if not need_attn:
                    dq_finish_and_ag()
                esF1 = ExitStack()
                with esF1:
                    sbF = lambda n, s, d: esF1.enter_context(nc.sbuf_tensor(n, s, d))
                    psF = lambda n, s, d: esF1.enter_context(nc.psum_tensor(n, s, d))
                    wgbuf = [sbF(f"wg{i}", [128, KCH * 256], BF16) for i in range(2)]
                    wubuf = [sbF(f"wu{i}", [128, KCH * 256], BF16) for i in range(2)]
                    gt1 = sbF("gt1", [128, 1024], F32)
                    gt2 = sbF("gt2", [128, 1024], F32)
                    inter_sb = [sbF(f"inter{i}", [128, 1024], BF16) for i in range(2)]
                    gps = [psF(f"gps{i}", [128, 1024], F32) for i in range(2)]
                    ups = [psF(f"ups{i}", [128, 1024], F32) for i in range(2)]

                    def _sd(slot):
                        cc_, m = slot // 16, slot % 16
                        return 8 * cc_ + m if m < 8 else 64 + 8 * cc_ + (m - 8)

                    wbuf_mark = {}
                    ss_idx = 0
                    inter_dma = {}
                    for w in range(64):
                        pw = w % 2
                        s0 = 2 * w
                        np_off = 128 * _sd(s0)
                        wld = [("st", wbuf_mark.get(w - 2, 0)),
                               ("cc", dqst["cc"] if w == 0 else 0)]
                        for piece in range(4):
                            ct.Dg(lambda e, pw=pw, np_off=np_off, piece=piece: e.dma_start(
                                out=wgbuf[pw][:, 1024 * piece:1024 * (piece + 1)],
                                in_=gate_wt.rearrange("(c p) n -> p c n", p=128)[:, 4 * piece:4 * (piece + 1), np_off:np_off + 256]),
                                waits=(wld if piece == 0 else []), sem=f"dwd{pw}")
                            ct.Dg(lambda e, pw=pw, np_off=np_off, piece=piece: e.dma_start(
                                out=wubuf[pw][:, 1024 * piece:1024 * (piece + 1)],
                                in_=up_wt.rearrange("(c p) n -> p c n", p=128)[:, 4 * piece:4 * (piece + 1), np_off:np_off + 256]),
                                sem=f"dwd{pw}")
                        d_wg = ct.C[f"dwd{pw}"]
                        for sl in range(2):
                            slot = s0 + sl
                            for th in range(2):
                                pss = ss_idx % 2
                                ss_idx += 1
                                wmm = [(f"dwd{pw}", d_wg), ("ss", m_xtf),
                                       ("sv", inter_dma.get(("cons", ss_idx - 2 - 1), 0))]
                                for kc in range(KCH):
                                    for tq_ in range(2):
                                        kp = (kc // 2) if kc % 2 == 0 else 8 + kc // 2
                                        ct.T(lambda t, kc=kc, kp=kp, tq_=tq_, th=th, sl=sl, pw=pw, pss=pss: t.matmul(
                                            out=gps[pss][:, 512 * tq_:512 * (tq_ + 1)],
                                            lhsT=wgbuf[pw][:, 256 * kc + 128 * sl:256 * kc + 128 * (sl + 1)],
                                            rhs=xT_f[:, TQ * kp + 1024 * th + 512 * tq_:TQ * kp + 1024 * th + 512 * (tq_ + 1)],
                                            start=(kc == 0), stop=(kc == KCH - 1)),
                                            waits=(wmm if (kc == 0 and tq_ == 0) else []))
                                for kc in range(KCH):
                                    for tq_ in range(2):
                                        kp = (kc // 2) if kc % 2 == 0 else 8 + kc // 2
                                        ct.T(lambda t, kc=kc, kp=kp, tq_=tq_, th=th, sl=sl, pw=pw, pss=pss: t.matmul(
                                            out=ups[pss][:, 512 * tq_:512 * (tq_ + 1)],
                                            lhsT=wubuf[pw][:, 256 * kc + 128 * sl:256 * kc + 128 * (sl + 1)],
                                            rhs=xT_f[:, TQ * kp + 1024 * th + 512 * tq_:TQ * kp + 1024 * th + 512 * (tq_ + 1)],
                                            start=(kc == 0), stop=(kc == KCH - 1)),
                                            inc=(1 if (kc == KCH - 1 and tq_ == 1) else 0))
                                m_gu = ct.C["st"]
                                wbuf_mark[w] = m_gu
                                m_g2 = ct.S(lambda s, pss=pss: s.activation(out=gt1[:], in_=gps[pss][:],
                                                                            func=AF.Square),
                                            waits=[("st", m_gu), ("sv", ct.C["sv"])])
                                ct.V(lambda v: v.tensor_scalar(out=gt1[:], in0=gt1[:], scalar1=C2, scalar2=C1,
                                                               op0=ALU.mult, op1=ALU.add), waits=[("ss", m_g2)])
                                m_t3 = ct.V(lambda v, pss=pss: v.tensor_tensor(out=gt2[:], in0=gt1[:],
                                                                               in1=gps[pss][:], op=ALU.mult))
                                m_t4 = ct.S(lambda s: s.activation(out=gt2[:], in_=gt2[:], func=AF.Tanh),
                                            waits=[("sv", m_t3)])
                                ct.V(lambda v, pss=pss: v.scalar_tensor_tensor(
                                    out=gt1[:], in0=gt2[:], scalar=1.0, in1=gps[pss][:],
                                    op0=ALU.add, op1=ALU.mult), waits=[("ss", m_t4)])
                                pi = ss_idx % 2
                                m_int = ct.V(lambda v, pss=pss, pi=pi: v.scalar_tensor_tensor(
                                    out=inter_sb[pi][:], in0=gt1[:], scalar=0.5, in1=ups[pss][:],
                                    op0=ALU.mult, op1=ALU.mult),
                                    waits=[("dnt", inter_dma.get(("dma", ss_idx - 3), 0))])
                                inter_dma[("cons", ss_idx - 1)] = m_int
                                md = ct.Dg(lambda e, slot=slot, th=th, pi=pi: e.dma_start(
                                    out=inter_d[128 * slot:128 * (slot + 1), 1024 * th:1024 * (th + 1)],
                                    in_=inter_sb[pi][:]), waits=[("sv", m_int)], sem="dnt")
                                inter_dma[("dma", ss_idx - 1)] = md
                    m_f1_dma = ct.C["dnt"]

                esF2 = ExitStack()
                with esF2:
                    sbF2 = lambda n, s, d: esF2.enter_context(nc.sbuf_tensor(n, s, d))
                    psF2 = lambda n, s, d: esF2.enter_context(nc.psum_tensor(n, s, d))
                    ints_sb = sbF2("ints", [128, 128 * 256], BF16)
                    wdbuf = [sbF2(f"wd{i}", [128, DM], BF16) for i in range(4)]
                    dn_f = sbF2("dn_f", [128, DM], F32)
                    dsq = sbF2("dsq", [128, DM], F32)
                    h1b = sbF2("h1b", [128, DM], F32)
                    mu3 = sbF2("mu3", [128, 1], F32)
                    m23 = sbF2("m23", [128, 1], F32)
                    r3 = sbF2("r3", [128, 1], F32)
                    d_ps = [psF2(f"dps{i}", [128, DM], F32) for i in range(2)]

                    wd_mark = {}
                    m_evac = {}
                    for pp in range(8):
                        tc0 = 256 * pp
                        wint = [("dnt", m_f1_dma), ("st", m_evac.get(pp - 1, 0)),
                                ("sv", m_evac.get(("v", pp - 1), 0))]
                        dints = 0
                        for grp in range(8):
                            dints = ct.Dg(lambda e, grp=grp, tc0=tc0: e.dma_start(
                                out=ints_sb[:, 256 * 16 * grp:256 * 16 * (grp + 1)],
                                in_=inter_d.rearrange("(sl p) t -> p sl t", p=128)[:, 16 * grp:16 * (grp + 1), tc0:tc0 + 256]),
                                waits=(wint if grp == 0 else []), sem="dnl")
                        for slot in range(128):
                            pws = slot % 4
                            d_wd = ct.Dg(lambda e, slot=slot, pws=pws: e.dma_start(
                                out=wdbuf[pws][:], in_=down_wt[128 * slot:128 * (slot + 1), :]),
                                waits=[("st", wd_mark.get((pp, slot - 4), wd_mark.get((pp - 1, 124 + slot), 0)))],
                                sem=f"dwd{pws}")
                            for tsb in range(2):
                                for hq in range(4):
                                    ct.T(lambda t, slot=slot, tsb=tsb, hq=hq, pws=pws: t.matmul(
                                        out=d_ps[tsb][:, 512 * hq:512 * (hq + 1)],
                                        lhsT=ints_sb[:, 256 * slot + 128 * tsb:256 * slot + 128 * (tsb + 1)],
                                        rhs=wdbuf[pws][:, 512 * hq:512 * (hq + 1)],
                                        start=(slot == 0), stop=(slot == 127)),
                                        waits=([(f"dwd{pws}", d_wd), ("dnl", dints)] if (tsb == 0 and hq == 0) else []),
                                        inc=(1 if (tsb == 1 and hq == 3) else 0))
                            wd_mark[(pp, slot)] = ct.C["st"]
                        m_dn = ct.C["st"]
                        for tsb in range(2):
                            ts_g = 2 * pp + tsb
                            d_h1 = ct.Ds(lambda e, ts_g=ts_g: e.dma_start(
                                out=h1b[:], in_=h1_d[128 * ts_g:128 * (ts_g + 1), :]),
                                waits=[("so", ct.C["so"]), ("dh1", ct.C["dh1"])], sem="dhl")
                            m_cp = ct.V(lambda v, tsb=tsb: v.tensor_copy(out=dn_f[:], in_=d_ps[tsb][:]),
                                        waits=[("st", m_dn), ("so", ct.C["so"])])
                            if part == "ffn" and dbg:
                                m_dbgd = ct.Dg(lambda e, ts_g=ts_g: e.dma_start(
                                    out=dbg_f["f_down"][128 * ts_g:128 * (ts_g + 1), :], in_=dn_f[:]),
                                    waits=[("sv", m_cp)])
                                ct.W("vector", [("dgp", m_dbgd)])
                            ct.V(lambda v: v.tensor_reduce(out=mu3[:], in_=dn_f[:], axis=mybir.AxisListType.X,
                                                           op=ALU.add))
                            ct.V(lambda v: v.tensor_scalar(out=mu3[:], in0=mu3[:], scalar1=1.0 / DM,
                                                           scalar2=None, op0=ALU.mult))
                            m_sq3 = ct.S(lambda s: s.activation(out=dsq[:], in_=dn_f[:], func=AF.Square),
                                         waits=[("sv", m_cp)])
                            ct.V(lambda v: v.tensor_reduce(out=m23[:], in_=dsq[:], axis=mybir.AxisListType.X,
                                                           op=ALU.add), waits=[("ss", m_sq3)])
                            ct.V(lambda v: v.tensor_scalar(out=m23[:], in0=m23[:], scalar1=1.0 / DM,
                                                           scalar2=None, op0=ALU.mult))
                            ct.V(lambda v: v.tensor_tensor(out=r3[:], in0=mu3[:], in1=mu3[:], op=ALU.mult))
                            m_v3 = ct.V(lambda v: v.tensor_tensor(out=r3[:], in0=m23[:], in1=r3[:],
                                                                  op=ALU.subtract))
                            ct.S(lambda s: s.activation(out=r3[:], in_=r3[:], func=AF.Ln, bias=epsb[:, 0:1]),
                                 waits=[("sv", m_v3)])
                            m_r3 = ct.S(lambda s: s.activation(out=r3[:], in_=r3[:], func=AF.Exp, scale=-0.5))
                            ct.V(lambda v: v.tensor_scalar(out=dn_f[:], in0=dn_f[:], scalar1=mu3[:, 0:1],
                                                           scalar2=r3[:, 0:1], op0=ALU.subtract, op1=ALU.mult),
                                 waits=[("ss", m_r3)])
                            m_out = ct.V(lambda v: v.tensor_tensor(out=dn_f[:], in0=dn_f[:], in1=h1b[:],
                                                                   op=ALU.add), waits=[("dhl", d_h1)])
                            ct.Do(lambda e, ts_g=ts_g: e.dma_start(
                                out=out_d[128 * ts_g:128 * (ts_g + 1), :], in_=dn_f[:]),
                                waits=[("sv", m_out)])
                        m_evac[pp] = ct.C["st"]
                        m_evac[("v", pp)] = ct.C["sv"]
                    if part == "ffn" and dbg:
                        ct.Do(lambda e: e.dma_start(out=dbg_f["f_h1"], in_=h1_d[:]))
                        ct.Do(lambda e: e.dma_start(out=dbg_f["f_inter"], in_=inter_d[:]))
                        ct.Do(lambda e: e.dma_start(out=dbg_f["f_gw"], in_=gate_wt[:]))
                        ct.Do(lambda e: e.dma_start(out=dbg_f["f_dw"], in_=down_wt[:]))
                    ct.W("gpsimd", [("so", ct.C["so"])])

        with nc.Block() as block:
            for eng in ("sync", "gpsimd", "vector", "scalar", "tensor"):
                ops = ct.ops[eng]
                if ops:
                    getattr(block, eng)(lambda e, ops=ops: [f(e) for f in ops])
    return nc


FEAT = np.concatenate([np.arange(0, 2048, 2), np.arange(1, 2048, 2)])
NORD = np.concatenate([np.arange(0, 16384, 2), np.arange(1, 16384, 2)])
INVFEAT = np.argsort(FEAT)


def _tok_slice(g):
    return np.concatenate([np.arange(2048 * q + 512 * g, 2048 * q + 512 * g + 512)
                           for q in range(4)])


def _host_inputs(inputs):
    x = inputs["L0_attn_hidden_states"]
    Wq = inputs["L0_attn_Wq"]; Wk = inputs["L0_attn_Wk"]
    Wv = inputs["L0_attn_Wv"]; Wo = inputs["L0_attn_Wo"]

    inv_freq = 1.0 / (10000.0 ** (np.arange(0, 128, 2, dtype=np.float64) / 128.0))
    freqs = np.arange(NSEQ, dtype=np.float64)[:, None] * inv_freq[None, :]
    cosT = np.ascontiguousarray(np.tile(np.cos(freqs).T.astype(np.float32), (2, 1)))
    sinT = np.ascontiguousarray(np.tile(np.sin(freqs).T.astype(np.float32), (2, 1)))
    idm = np.eye(128, dtype=np.float32)
    sgrid = np.arange(128)[:, None]
    tgrid = np.arange(128)[None, :]
    masku = (sgrid <= tgrid).astype(np.float32)
    negu16 = masku * np.float32(-1.0 / 16.0)

    def packed_T(W_packed):
        b = W_packed.view('<u1')[:, ::4]
        return np.ascontiguousarray(b.T)
    gb = packed_T(inputs["L0_ffn_gate_W_packed"])[:, NORD]
    ub = packed_T(inputs["L0_ffn_up_W_packed"])[:, NORD]
    db = packed_T(inputs["L0_ffn_down_W_packed"])[:, FEAT]
    gsT = np.repeat(inputs["L0_ffn_gate_scales"].T, 16, axis=0)[:, NORD]
    usT = np.repeat(inputs["L0_ffn_up_scales"].T, 16, axis=0)[:, NORD]
    dsT = np.repeat(inputs["L0_ffn_down_scales"].T, 16, axis=0)[:, FEAT]

    maps = []
    for c in range(8):
        beta, g = c // 4, c % 4
        toks = _tok_slice(g)
        xb = x[beta]
        m = {
            "xT": np.ascontiguousarray(xb.T[FEAT]),
            "x0q": np.ascontiguousarray(xb[toks][:, FEAT]),
            "wq": np.ascontiguousarray(Wq[FEAT, 512 * g:512 * (g + 1)]),
            "wk": np.ascontiguousarray(Wk[FEAT, 128 * g:128 * (g + 1)]),
            "wv": np.ascontiguousarray(Wv[FEAT, 128 * g:128 * (g + 1)]),
            "wo": np.ascontiguousarray(Wo[512 * g:512 * (g + 1)][:, FEAT]),
            "cosT": cosT, "sinT": sinT,
            "idm": idm, "masku": masku, "negu16": negu16,
            "gbytes": np.ascontiguousarray(gb[128 * c:128 * (c + 1)]),
            "ubytes": np.ascontiguousarray(ub[128 * c:128 * (c + 1)]),
            "dbytes": np.ascontiguousarray(db[1024 * c:1024 * (c + 1)]),
            "gscale": np.ascontiguousarray(gsT[128 * c:128 * (c + 1)]),
            "uscale": np.ascontiguousarray(usT[128 * c:128 * (c + 1)]),
            "dscale": np.ascontiguousarray(dsT[1024 * c:1024 * (c + 1)]),
        }
        maps.append(m)
    return maps


def kernel(**inputs):
    maps = _host_inputs(inputs)
    nc = build_program("all")
    res = run_bass_kernel_spmd(nc, maps, core_ids=list(range(8))).results
    out = np.zeros((2, NSEQ, DM), dtype=np.float32)
    for c in range(8):
        beta, g = c // 4, c % 4
        toks = _tok_slice(g)
        out[beta, toks] = res[c]["out"][:, INVFEAT]
    return out
